# version 02
# baseline (speedup 1.0000x reference)
import sys
sys.path.insert(0, "/opt/trn_rl_repo")
import numpy as np
from contextlib import ExitStack
import concourse.bass as bass
import concourse.mybir as mybir

F32 = mybir.dt.float32
BF16 = mybir.dt.bfloat16
AF = mybir.ActivationFunctionType
ALU = mybir.AluOpType
AX = mybir.AxisListType

C = 1024
NDC = 8
H = 16
HK = 64
CH = 128


class Sem:
    __slots__ = ("h", "n")
    def __init__(self, h):
        self.h = h
        self.n = 0
    def inc(self, instr, k=1):
        instr.then_inc(self.h, k)
        self.n += k
        return self.n


class B:
    def __init__(self, nc=None):
        self.nc = nc or bass.Bass("TRN2", target_bir_lowering=False)
        self.ctx = ExitStack()
        self.ctx.__enter__()
        self.block = self.ctx.enter_context(self.nc.Block())
        self._sems = {}
        self.ve = self.sem("Ove")
        self.sc = self.sem("Osc")
        self.gc = self.sem("Ogc")
        self.te = self.sem("Ote")
        self.wstage()

    def sem(self, name):
        if name not in self._sems:
            self._sems[name] = Sem(self.ctx.enter_context(self.nc.semaphore(name)))
        return self._sems[name]

    def sb(self, name, shape, dt):
        stk = self._scopes[-1] if getattr(self, "_scopes", None) else self.ctx
        return stk.enter_context(self.nc.sbuf_tensor(name, shape, dt))

    def scope(self):
        from contextlib import contextmanager
        if not hasattr(self, "_scopes"):
            self._scopes = []
        @contextmanager
        def _cm():
            stk = ExitStack()
            self._scopes.append(stk)
            try:
                yield
            finally:
                self._scopes.pop()
                stk.close()
        return _cm()

    def barrier(self):
        marks = {s: s.n for s in self._sems.values() if s.n > 0}
        def _w(eng):
            gate(eng, marks)
        for engname in ["sync", "vector", "scalar", "tensor", "gpsimd"]:
            getattr(self.block, engname)(_w)
        return marks

    def wstage(self):
        if not hasattr(self, "_wstage"):
            bufs = [self.ctx.enter_context(self.nc.sbuf_tensor(f"WST{i}", [128, 1024], F32))
                    for i in range(2)]
            ring = DmaRing(self, "WSTr", 2)
            self._wstage = (bufs, ring, [0, 0])
        return self._wstage

    def ps(self, name, shape, dt=F32):
        stk = self._scopes[-1] if getattr(self, "_scopes", None) else self.ctx
        return stk.enter_context(self.nc.psum_tensor(name, shape, dt))

    def _sect(self, engname, osem, f):
        mark = osem.n
        def wrapped(eng):
            if mark > 0:
                eng.wait_ge(osem.h, mark)
            f(eng)
        getattr(self.block, engname)(wrapped)

    def vec(self, f):
        self._sect("vector", self.ve, lambda e: f(e, lambda i, k=1: self.ve.inc(i, k)))
    def scl(self, f):
        self._sect("scalar", self.sc, lambda e: f(e, lambda i, k=1: self.sc.inc(i, k)))
    def gpc(self, f):
        self._sect("gpsimd", self.gc, lambda e: f(e, lambda i, k=1: self.gc.inc(i, k)))
    def ten(self, f):
        def wrapped(eng):
            last = f(eng)
            assert last is not None, "ten section must return last matmul"
            self.te.inc(last, 1)
        self.block.tensor(wrapped)
    def syn(self, f):
        self.block.sync(f)
    def gpd(self, f):
        self.block.gpsimd(f)

    def close(self):
        self.ctx.__exit__(None, None, None)


def gate(eng, marks):
    for s, v in marks.items():
        if v > 0:
            eng.wait_ge(s.h, v)


def snap(*sems):
    return {s: s.n for s in sems}


class DmaRing:
    def __init__(self, b, tag, depth=2):
        self.sems = [b.sem(f"{tag}r{i}") for i in range(depth)]
        self.depth = depth
    def issue(self, instr, slot):
        return self.sems[slot % self.depth].inc(instr, 16)
    def wait_slot(self, eng, slot):
        s = self.sems[slot % self.depth]
        if s.n > 0:
            eng.wait_ge(s.h, s.n)
    def marks(self):
        return {s: s.n for s in self.sems}



def load_cast_fm(b, tag, dram, nrc, ncols):
    stage, ring, cvm = b.wstage()
    assert ncols <= 1024
    tiles = [b.sb(f"{tag}b{i}", [128, ncols], BF16) for i in range(nrc)]
    for i in range(nrc):
        slot = ring.sems[0].n + ring.sems[1].n
        sl = (slot // 16) % 2
        def _ld(sync, i=i, sl=sl, pm=cvm[sl]):
            if pm > 0:
                sync.wait_ge(b.ve.h, pm)
            ring.issue(sync.dma_start(out=stage[sl][:, 0:ncols],
                                      in_=dram[i * 128:(i + 1) * 128, 0:ncols]), sl)
        b.syn(_ld)
        def _cv(v, I, i=i, sl=sl):
            ring.wait_slot(v, sl)
            I(v.tensor_copy(tiles[i][:, :], stage[sl][:, 0:ncols]))
        b.vec(_cv)
        cvm[sl] = b.ve.n
    return tiles, snap(b.ve)


def load_fm_raw(b, tag, dram, nrc, ncols, dt, wait=None):
    ld = b.sem(f"{tag}ld")
    nrows_tot = dram.shape[0]
    tiles = [b.sb(f"{tag}t{i}", [128, ncols], dt) for i in range(nrc)]
    def _ld(sync):
        if wait:
            gate(sync, wait)
        for i in range(nrc):
            r = min(128, nrows_tot - i * 128)
            ld.inc(sync.dma_start(out=tiles[i][0:r, :],
                                  in_=dram[i * 128:i * 128 + r, 0:ncols]), 16)
    b.syn(_ld)
    return tiles, snap(ld)


def load_col(b, tag, dram_vec, nrc):
    ld = b.sem(f"{tag}ld")
    tile = b.sb(f"{tag}c", [128, nrc], F32)
    def _ld(sync):
        for i in range(nrc):
            ld.inc(sync.dma_start(out=tile[:, i:i + 1],
                                  in_=dram_vec[i * 128:(i + 1) * 128, 0:1]), 16)
    b.syn(_ld)
    return tile, snap(ld)



def proj_fm(b, tag, wtiles, wmarks, xtiles, xmarks, T, writer,
            cout=C, tb=512, nin=NDC, xoff=0):
    noc = cout // 128
    ntb = T // tb
    pp = [b.ps(f"{tag}p{i}", [128, tb], F32) for i in range(2)]
    wr_marks = [{}, {}]
    it = 0
    for oc in range(noc):
        for t in range(ntb):
            pbuf = it % 2
            pm = dict(wr_marks[pbuf])
            def _mm(te, oc=oc, t=t, pbuf=pbuf, it=it, pm=pm):
                if it == 0:
                    gate(te, wmarks)
                    gate(te, xmarks)
                gate(te, pm)
                last = None
                for dc in range(nin):
                    last = te.matmul(pp[pbuf][:, :],
                                     wtiles[dc][:, oc * 128:(oc + 1) * 128],
                                     xtiles[dc][:, xoff + t * tb:xoff + (t + 1) * tb],
                                     start=(dc == 0), stop=(dc == nin - 1))
                return last
            b.ten(_mm)
            wr_marks[pbuf] = writer(it, oc, t * tb, tb, pp[pbuf], snap(b.te))
            it += 1


def fm_store_writer(b, tag, out_dram, act=None, scale=None, out_dt=BF16,
                    gate_tiles=None, gmarks=None):
    sring = DmaRing(b, f"{tag}st", 2)
    ev = [b.sb(f"{tag}e{i}", [128, 512], out_dt) for i in range(2)]
    evg = [b.sb(f"{tag}g{i}", [128, 512], F32) for i in range(2)] if gate_tiles is not None else None

    def writer(it, oc, t0, tw, psum, mmarks):
        ebuf = it % 2
        def _ev(s, I, it=it, ebuf=ebuf, psum=psum, mmarks=mmarks):
            gate(s, mmarks)
            if it >= 2:
                sring.wait_slot(s, ebuf)
            dst = evg[ebuf] if gate_tiles is not None else ev[ebuf]
            I(s.activation(dst[:, 0:tw], psum[:, 0:tw],
                           act if act is not None else AF.Copy,
                           scale=(scale if scale is not None else 1.0)))
        b.scl(_ev)
        em = snap(b.sc)
        if gate_tiles is not None:
            def _ev2(v, I, it=it, oc=oc, t0=t0, tw=tw, ebuf=ebuf, em=em):
                gate(v, em)
                if it == 0 and gmarks:
                    gate(v, gmarks)
                if it >= 2:
                    sring.wait_slot(v, ebuf)
                I(v.tensor_mul(ev[ebuf][:, 0:tw], evg[ebuf][:, 0:tw],
                               gate_tiles[oc][:, t0:t0 + tw]))
            b.vec(_ev2)
            evm = snap(b.ve)
        else:
            evm = em
        def _st(g, oc=oc, t0=t0, tw=tw, ebuf=ebuf, evm=evm):
            gate(g, evm)
            sring.issue(g.dma_start(out=out_dram[oc * 128:(oc + 1) * 128, t0:t0 + tw],
                                    in_=ev[ebuf][:, 0:tw]), ebuf)
        b.gpd(_st)
        return dict(em)
    return writer, sring


def simple_proj(b, tag, w_dram, x_dram_or_tiles, out_dram, T, act=None,
                scale=None, out_dt=BF16, xmarks=None, gate_tiles=None,
                gmarks=None, cin=C, cout=C):
    wtiles, wmarks = load_cast_fm(b, f"{tag}w", w_dram, cin // 128, cout)
    if isinstance(x_dram_or_tiles, tuple):
        xtiles, xm = x_dram_or_tiles
        if xmarks:
            xm = dict(xm); xm.update(xmarks)
    else:
        xtiles, xm = load_fm_raw(b, f"{tag}x", x_dram_or_tiles, cin // 128, T, BF16,
                                 wait=xmarks)
    writer, sring = fm_store_writer(b, tag, out_dram, act=act, scale=scale,
                                     out_dt=out_dt, gate_tiles=gate_tiles, gmarks=gmarks)
    proj_fm(b, tag, wtiles, wmarks, xtiles, xm, T, writer, cout=cout, nin=cin // 128)
    return sring.marks()



def stage_ln(b, tag, x_dram, out_dram, T, ones_f32, omarks, eps=1e-5):
    tb = 512
    ntb = T // tb
    ld = b.sem(f"{tag}ld")
    sring = DmaRing(b, f"{tag}st", 2)
    epst = b.sb(f"{tag}ep", [128, 1], F32)
    def _eps(g, I):
        I(g.memset(epst[:, :], eps))
    b.gpc(_eps)
    epsm = snap(b.gc)
    xb = [[b.sb(f"{tag}x{i}_{dc}", [128, tb], F32) for dc in range(NDC)] for i in range(2)]
    xq = [b.sb(f"{tag}q{i}", [128, tb], F32) for i in range(2)]
    p1 = [b.ps(f"{tag}p1{i}", [128, tb], F32) for i in range(2)]
    p2 = [b.ps(f"{tag}p2{i}", [128, tb], F32) for i in range(2)]
    t_mu = [b.sb(f"{tag}mu{i}", [128, tb], F32) for i in range(2)]
    t_m2 = b.sb(f"{tag}m2", [128, tb], F32)
    t_va = b.sb(f"{tag}va", [128, tb], F32)
    t_rs = [b.sb(f"{tag}rs{i}", [128, tb], F32) for i in range(2)]
    t_bm = [b.sb(f"{tag}bm{i}", [128, tb], F32) for i in range(2)]
    t_t1 = b.sb(f"{tag}t1", [128, tb], F32)
    xn = [b.sb(f"{tag}o{i}", [128, tb], BF16) for i in range(2)]
    prev = [{}, {}]

    for t in range(ntb):
        buf = t % 2
        pm = dict(prev[buf])
        def _ld(sync, t=t, buf=buf, pm=pm):
            if t == 0:
                gate(sync, omarks)
            gate(sync, pm)
            for dc in range(NDC):
                ld.inc(sync.dma_start(out=xb[buf][dc][:, :],
                                      in_=x_dram[dc * 128:(dc + 1) * 128, t * tb:(t + 1) * tb]), 16)
        b.syn(_ld)
        ldm = ld.n

        temarks = []
        for dc in range(NDC):
            qb = dc % 2
            def _sq(s, I, t=t, buf=buf, dc=dc, qb=qb, ldm=ldm, pm=pm):
                s.wait_ge(ld.h, ldm)
                gate(s, pm)
                if dc >= 2:
                    s.wait_ge(b.te.h, temarks[dc - 2])
                I(s.activation(xq[qb][:, :], xb[buf][dc][:, :], AF.Square))
            b.scl(_sq)
            sqm = snap(b.sc)
            def _mm(te, t=t, buf=buf, dc=dc, qb=qb, sqm=sqm, pm=pm):
                gate(te, sqm)
                if dc == 0:
                    gate(te, pm)
                te.matmul(p1[buf][:, :], ones_f32[:, :], xb[buf][dc][:, :],
                          start=(dc == 0), stop=(dc == NDC - 1))
                return te.matmul(p2[buf][:, :], ones_f32[:, :], xq[qb][:, :],
                                 start=(dc == 0), stop=(dc == NDC - 1))
            b.ten(_mm)
            temarks.append(b.te.n)
        mmark = snap(b.te)

        def _v1(v, I, buf=buf, mmark=mmark, pm=pm):
            gate(v, mmark)
            gate(v, pm)
            I(v.tensor_scalar_mul(t_mu[buf][:, :], p1[buf][:, :], 1.0 / C))
        b.vec(_v1)
        vm1 = snap(b.ve)
        def _s1(s, I, buf=buf, vm1=vm1):
            gate(s, vm1)
            I(s.activation(t_m2[:, :], t_mu[buf][:, :], AF.Square))
        b.scl(_s1)
        sm1 = snap(b.sc)
        def _v2(v, I, buf=buf, sm1=sm1):
            gate(v, sm1)
            I(v.scalar_tensor_tensor(t_va[:, :], p2[buf][:, :], 1.0 / C, t_m2[:, :],
                                     ALU.mult, ALU.subtract))
        b.vec(_v2)
        vm2 = snap(b.ve)
        def _s2(s, I, buf=buf, vm2=vm2):
            gate(s, vm2)
            gate(s, epsm)
            I(s.activation(t_va[:, :], t_va[:, :], AF.Ln, bias=epst[:, 0:1]))
            s.wait_ge(b.sc.h, b.sc.n)
            I(s.activation(t_rs[buf][:, :], t_va[:, :], AF.Exp, scale=-0.5))
        b.scl(_s2)
        sm2 = snap(b.sc)
        def _v3(v, I, buf=buf, sm2=sm2):
            gate(v, sm2)
            I(v.tensor_mul(t_bm[buf][:, :], t_mu[buf][:, :], t_rs[buf][:, :]))
        b.vec(_v3)
        vm3 = snap(b.ve)

        for dc in range(NDC):
            ob = dc % 2
            def _vx(v, I, t=t, buf=buf, dc=dc, ob=ob, vm3=vm3):
                gate(v, vm3)
                if t > 0 or dc >= 2:
                    sring.wait_slot(v, ob)
                I(v.tensor_mul(t_t1[:, :], xb[buf][dc][:, :], t_rs[buf][:, :]))
                v.wait_ge(b.ve.h, b.ve.n)
                I(v.tensor_sub(xn[ob][:, :], t_t1[:, :], t_bm[buf][:, :]))
            b.vec(_vx)
            vxm = snap(b.ve)
            def _st(g, t=t, dc=dc, ob=ob, vxm=vxm):
                gate(g, vxm)
                sring.issue(g.dma_start(out=out_dram[dc * 128:(dc + 1) * 128, t * tb:(t + 1) * tb],
                                        in_=xn[ob][:, :]), ob)
            b.gpd(_st)
        pm2 = {b.ve: b.ve.n, b.sc: b.sc.n, b.te: b.te.n}
        pm2.update(sring.marks())
        prev[buf] = pm2
    return sring.marks()



def proj_tok(b, tag, stat_tiles, statmarks, w_tiles, wmarks, T, writer,
             cout=C, nin=NDC, statoff=0, kdim=128):
    ntc = T // 128
    nob = cout // 512
    pp = [b.ps(f"{tag}p{i}", [128, 512], F32) for i in range(2)]
    wr_marks = [{}, {}]
    it = 0
    for tc in range(ntc):
        for ob in range(nob):
            pbuf = it % 2
            pm = dict(wr_marks[pbuf])
            def _mm(te, tc=tc, ob=ob, pbuf=pbuf, it=it, pm=pm):
                if it == 0:
                    gate(te, statmarks)
                    gate(te, wmarks)
                gate(te, pm)
                last = None
                for dc in range(nin):
                    last = te.matmul(pp[pbuf][:, :],
                                     stat_tiles[dc][:, statoff + tc * 128:statoff + (tc + 1) * 128],
                                     w_tiles[dc][:, ob * 512:(ob + 1) * 512],
                                     start=(dc == 0), stop=(dc == nin - 1))
                return last
            b.ten(_mm)
            wr_marks[pbuf] = writer(it, tc, ob * 512, 512, pp[pbuf], snap(b.te))
            it += 1


def tok_store_writer(b, tag, out_dram, act=None, scale=None, out_dt=BF16):
    sring = DmaRing(b, f"{tag}st", 2)
    ev = [b.sb(f"{tag}e{i}", [128, 512], out_dt) for i in range(2)]

    def writer(it, tc, o0, ow, psum, mmarks):
        ebuf = it % 2
        def _ev(s, I, it=it, ebuf=ebuf, psum=psum, mmarks=mmarks):
            gate(s, mmarks)
            if it >= 2:
                sring.wait_slot(s, ebuf)
            I(s.activation(ev[ebuf][:, 0:ow], psum[:, 0:ow],
                           act if act is not None else AF.Copy,
                           scale=(scale if scale is not None else 1.0)))
        b.scl(_ev)
        em = snap(b.sc)
        def _st(g, tc=tc, o0=o0, ow=ow, ebuf=ebuf, em=em):
            gate(g, em)
            sring.issue(g.dma_start(out=out_dram[tc * 128:(tc + 1) * 128, o0:o0 + ow],
                                    in_=ev[ebuf][:, 0:ow]), ebuf)
        b.gpd(_st)
        return dict(em)
    return writer, sring



def load_cast_rows(b, tag, dram, row_ranges, ncols):
    stage, ring, cvm = b.wstage()
    assert ncols <= 1024
    tiles = []
    for i, (r0, nr) in enumerate(row_ranges):
        tiles.append(b.sb(f"{tag}b{i}", [nr, ncols], BF16))
        slot = ring.sems[0].n + ring.sems[1].n
        sl = (slot // 16) % 2
        def _ld(sync, i=i, r0=r0, nr=nr, sl=sl, pm=cvm[sl]):
            if pm > 0:
                sync.wait_ge(b.ve.h, pm)
            ring.issue(sync.dma_start(out=stage[sl][0:nr, 0:ncols],
                                      in_=dram[r0:r0 + nr, 0:ncols]), sl)
        b.syn(_ld)
        def _cv(v, I, i=i, nr=nr, sl=sl):
            ring.wait_slot(v, sl)
            I(v.tensor_copy(tiles[i][:, :], stage[sl][0:nr, 0:ncols]))
        b.vec(_cv)
        cvm[sl] = b.ve.n
    return tiles, snap(b.ve)



def stage_l0mix(b, tag, xn_dram, P, outs, T, xnmarks):
    tb = 512
    ntb = T // tb
    ntc = tb // 128
    names = ["w", "k", "v", "r", "g"]

    maa_x, mx_m = load_col(b, f"{tag}mx", P["maa_x"], NDC)
    maa = {}
    maam = {}
    for nm in names:
        maa[nm], maam[nm] = load_col(b, f"{tag}m{nm}", P[f"maa_{nm}"], NDC)
    w1t, w1m = load_cast_fm(b, f"{tag}w1", P["w1"], NDC, 160)
    w2t, w2m = load_cast_rows(b, f"{tag}w2", P["w2f"],
                              [(f * 32, 32) for f in range(5)], C)
    td1t, td1m = load_cast_fm(b, f"{tag}td1", P["td_w1"], NDC, 64)
    td2t, td2m = load_cast_rows(b, f"{tag}td2", P["td_w2"], [(0, 64)], C)
    Wt, Wm = {}, {}
    for wn in ["r", "k", "v", "g"]:
        Wt[wn], Wm[wn] = load_cast_fm(b, f"{tag}W{wn}", P[f"W_{wn}"], NDC, C)

    xr = DmaRing(b, f"{tag}xl", 2)
    xnb_t = [[b.sb(f"{tag}xn{i}_{dc}", [128, tb + 1], BF16) for dc in range(NDC)]
             for i in range(2)]
    xnm = dict(xnmarks)

    onesr = b.sb(f"{tag}o1", [1, 128], BF16)
    tdrow, tdrm = load_fm_raw(b, f"{tag}tdr", P["td"], 1, C, F32)
    tdrow_bf = b.sb(f"{tag}tdbf", [1, C], BF16)
    td_bc = b.sb(f"{tag}tdb", [128, C], F32)
    p_bc = b.ps(f"{tag}pbc", [128, 512], F32)
    def _o2(g, I):
        I(g.memset(onesr[:, :], 1.0))
    b.gpc(_o2)
    o2m = snap(b.gc)
    def _cvtd(v, I):
        gate(v, tdrm)
        gate(v, o2m)
        I(v.tensor_copy(tdrow_bf[:, :], tdrow[0][0:1, :]))
    b.vec(_cvtd)
    for half in range(2):
        cvtdm = snap(b.ve)
        def _bc(te, half=half, cvtdm=cvtdm):
            gate(te, cvtdm)
            return te.matmul(p_bc[:, :], onesr[:, :],
                             tdrow_bf[:, half * 512:(half + 1) * 512],
                             start=True, stop=True)
        b.ten(_bc)
        tem = snap(b.te)
        def _ev(v, I, half=half, tem=tem):
            gate(v, tem)
            I(v.tensor_copy(td_bc[:, half * 512:(half + 1) * 512], p_bc[:, :]))
        b.vec(_ev)
    tdbcm = snap(b.ve)

    xxb = [b.sb(f"{tag}xx{dc}", [128, tb], BF16) for dc in range(NDC)]
    xxx = b.sb(f"{tag}xxx", [128, tb], BF16)
    mixed = {nm: [b.sb(f"{tag}mi{nm}{dc}", [128, tb], BF16) for dc in range(NDC)]
             for nm in names}
    p_t1 = b.ps(f"{tag}pt1", [96, tb], F32)
    p_t2 = b.ps(f"{tag}pt2", [64, tb], F32)
    Tf = [b.sb(f"{tag}Tf{f}", [32, tb], BF16) for f in range(5)]
    p_m = [b.ps(f"{tag}pm{i}", [128, tb], F32) for i in range(2)]
    t_mf = b.sb(f"{tag}tmf", [128, tb], F32)
    t_mf2 = b.sb(f"{tag}tm2", [128, tb], F32)
    h_td = b.sb(f"{tag}htd", [64, T], BF16)
    p_h = b.ps(f"{tag}ph", [64, tb], F32)
    p_o = [b.ps(f"{tag}po{i}", [128, tb], F32) for i in range(2)]
    evo = [b.sb(f"{tag}ev{i}", [128, 512], BF16) for i in range(2)]
    ewt = b.sb(f"{tag}ewt", [128, 512], F32)
    oring = DmaRing(b, f"{tag}os", 2)
    prev_tb = {}

    for t in range(ntb):
        pm = dict(prev_tb)
        buf = t % 2
        xn = xnb_t[buf]
        def _ldx(sync, t=t, buf=buf, pm=pm):
            gate(sync, xnm)
            gate(sync, pm)
            for dc in range(NDC):
                if t == 0:
                    xr.issue(sync.dma_start(out=xnb_t[buf][dc][:, 1:tb + 1],
                                            in_=xn_dram[dc * 128:(dc + 1) * 128, 0:tb]), buf)
                else:
                    xr.issue(sync.dma_start(out=xnb_t[buf][dc][:, 0:tb + 1],
                                            in_=xn_dram[dc * 128:(dc + 1) * 128,
                                                        t * tb - 1:(t + 1) * tb]), buf)
        b.syn(_ldx)
        xlm = {xr.sems[buf]: xr.sems[buf].n}
        if t == 0:
            def _z0(v, I, buf=buf):
                for dc in range(NDC):
                    I(v.memset(xnb_t[buf][dc][:, 0:1], 0.0))
            b.vec(_z0)
        for dc in range(NDC):
            def _vx(v, I, t=t, dc=dc, pm=pm, xlm=xlm, xn=xn):
                if t == 0 and dc == 0:
                    gate(v, mx_m); gate(v, tdbcm)
                    for nm in names:
                        gate(v, maam[nm])
                gate(v, pm)
                gate(v, xlm)
                gate(v, {b.te: b.te.n})
                I(v.tensor_sub(xxb[dc][:, :], xn[dc][:, 0:tb],
                               xn[dc][:, 1:tb + 1]))
                v.wait_ge(b.ve.h, b.ve.n)
                I(v.scalar_tensor_tensor(xxx[:, :], xxb[dc][:, :], maa_x[:, dc:dc + 1],
                                         xn[dc][:, 1:tb + 1],
                                         ALU.mult, ALU.add))
            b.vec(_vx)
            vxm = snap(b.ve)
            def _m1(te, t=t, dc=dc, vxm=vxm, pm=pm):
                gate(te, vxm)
                if dc == 0:
                    gate(te, w1m); gate(te, pm)
                te.matmul(p_t1[:, :], w1t[dc][:, 0:96], xxx[:, :],
                          start=(dc == 0), stop=(dc == NDC - 1))
                return te.matmul(p_t2[:, :], w1t[dc][:, 96:160], xxx[:, :],
                                 start=(dc == 0), stop=(dc == NDC - 1))
            b.ten(_m1)
        m1m = snap(b.te)

        def _th(s, I, m1m=m1m):
            gate(s, m1m)
            for f in range(3):
                I(s.activation(Tf[f][:, :], p_t1[f * 32:(f + 1) * 32, :], AF.Tanh))
            for f in range(2):
                I(s.activation(Tf[3 + f][:, :], p_t2[f * 32:(f + 1) * 32, :], AF.Tanh))
        b.scl(_th)
        thm = snap(b.sc)

        it = 0
        evm_hist = [{}, {}]
        for fi, nm in enumerate(names):
            rhs = Tf[fi][:, :]
            for dc in range(NDC):
                pb = it % 2
                pmm = dict(evm_hist[pb])
                def _mf(te, fi=fi, dc=dc, pb=pb, thm=thm, rhs=rhs, pmm=pmm):
                    gate(te, thm)
                    if it < 2:
                        gate(te, w2m)
                    gate(te, pmm)
                    return te.matmul(p_m[pb][:, :], w2t[fi][:, dc * 128:(dc + 1) * 128],
                                     rhs, start=True, stop=True)
                b.ten(_mf)
                mfm = snap(b.te)
                def _mix(v, I, t=t, nm=nm, dc=dc, pb=pb, mfm=mfm, xn=xn):
                    gate(v, mfm)
                    I(v.tensor_scalar_add(t_mf[:, :], p_m[pb][:, :], maa[nm][:, dc:dc + 1]))
                    v.wait_ge(b.ve.h, b.ve.n)
                    I(v.tensor_mul(t_mf2[:, :], xxb[dc][:, :], t_mf[:, :]))
                    v.wait_ge(b.ve.h, b.ve.n)
                    I(v.tensor_add(mixed[nm][dc][:, :], t_mf2[:, :],
                                   xn[dc][:, 1:tb + 1]))
                b.vec(_mix)
                evm_hist[pb] = snap(b.ve)
                it += 1
        mixm = snap(b.ve)

        fm_jobs = [("r", "r", outs["r_fm"], None), ("k", "k", outs["k_fm"], None),
                   ("g", "g", outs["g_fm"], AF.Silu)]
        oit = 0
        for src_nm, wname, odram, act in fm_jobs:
            for oc in range(NDC):
                pb = oit % 2
                def _pmm(te, src_nm=src_nm, wname=wname, oc=oc, pb=pb, mixm=mixm, oit=oit):
                    gate(te, mixm)
                    if oit < 2:
                        gate(te, Wm[wname])
                    if oit >= 2:
                        gate(te, {b.sc: b.sc.n})
                    last = None
                    for dc in range(NDC):
                        last = te.matmul(p_o[pb][:, :], Wt[wname][dc][:, oc * 128:(oc + 1) * 128],
                                         mixed[src_nm][dc][:, :],
                                         start=(dc == 0), stop=(dc == NDC - 1))
                    return last
                b.ten(_pmm)
                tem = snap(b.te)
                def _ev(s, I, oc=oc, pb=pb, tem=tem, act=act, oit=oit):
                    gate(s, tem)
                    if oit >= 2:
                        oring.wait_slot(s, oit % 2)
                    I(s.activation(evo[oit % 2][:, :], p_o[pb][:, :],
                                   act if act is not None else AF.Copy))
                b.scl(_ev)
                evm = snap(b.sc)
                def _st(g, t=t, oc=oc, odram=odram, evm=evm, oit=oit):
                    gate(g, evm)
                    oring.issue(g.dma_start(
                        out=odram[oc * 128:(oc + 1) * 128, t * tb:(t + 1) * tb],
                        in_=evo[oit % 2][:, :]), oit % 2)
                b.gpd(_st)
                oit += 1

        for src_nm, wname, odram in [("v", "v", outs["v_tok"]), ("k", "k", outs["k_tok"])]:
            for tc in range(ntc):
                for ob in range(2):
                    pb = oit % 2
                    def _pmm(te, src_nm=src_nm, wname=wname, tc=tc, ob=ob, pb=pb, oit=oit):
                        gate(te, {b.sc: b.sc.n})
                        last = None
                        for dc in range(NDC):
                            last = te.matmul(p_o[pb][:, :],
                                             mixed[src_nm][dc][:, tc * 128:(tc + 1) * 128],
                                             Wt[wname][dc][:, ob * 512:(ob + 1) * 512],
                                             start=(dc == 0), stop=(dc == NDC - 1))
                        return last
                    b.ten(_pmm)
                    tem = snap(b.te)
                    def _ev(s, I, pb=pb, tem=tem, oit=oit):
                        gate(s, tem)
                        oring.wait_slot(s, oit % 2)
                        I(s.activation(evo[oit % 2][:, :], p_o[pb][:, :], AF.Copy))
                    b.scl(_ev)
                    evm = snap(b.sc)
                    def _st(g, t=t, tc=tc, ob=ob, odram=odram, evm=evm, oit=oit):
                        gate(g, evm)
                        oring.issue(g.dma_start(
                            out=odram[t * tb + tc * 128:t * tb + (tc + 1) * 128,
                                      ob * 512:(ob + 1) * 512],
                            in_=evo[oit % 2][:, :]), oit % 2)
                    b.gpd(_st)
                    oit += 1

        def _ph(te, t=t, mixm=mixm):
            gate(te, mixm)
            gate(te, td1m)
            last = None
            for dc in range(NDC):
                last = te.matmul(p_h[:, :], td1t[dc][:, 0:64], mixed["w"][dc][:, :],
                                 start=(dc == 0), stop=(dc == NDC - 1))
            return last
        b.ten(_ph)
        phm = snap(b.te)
        def _tanh2(s, I, t=t, phm=phm):
            gate(s, phm)
            I(s.activation(h_td[:, t * tb:(t + 1) * tb], p_h[:, :], AF.Tanh))
        b.scl(_tanh2)
        htm = snap(b.sc)

        for tc in range(ntc):
            for ob in range(2):
                pb = oit % 2
                def _pw(te, t=t, tc=tc, ob=ob, pb=pb, htm=htm, oit=oit):
                    gate(te, htm)
                    if oit < 2:
                        gate(te, td2m)
                    gate(te, {b.sc: b.sc.n})
                    return te.matmul(p_o[pb][:, :],
                                     h_td[:, t * tb + tc * 128:t * tb + (tc + 1) * 128],
                                     td2t[0][:, ob * 512:(ob + 1) * 512],
                                     start=True, stop=True)
                b.ten(_pw)
                tem = snap(b.te)
                def _vew(v, I, pb=pb, ob=ob, tem=tem):
                    gate(v, tem)
                    I(v.tensor_add(ewt[:, :], p_o[pb][:, :],
                                   td_bc[:, ob * 512:(ob + 1) * 512]))
                b.vec(_vew)
                vem = snap(b.ve)
                def _sew(s, I, pb=pb, vem=vem, oit=oit):
                    gate(s, vem)
                    oring.wait_slot(s, oit % 2)
                    I(s.activation(evo[oit % 2][:, :], ewt[:, :], AF.Exp))
                b.scl(_sew)
                sem_ = snap(b.sc)
                def _st(g, t=t, tc=tc, ob=ob, sem_=sem_, oit=oit):
                    gate(g, sem_)
                    oring.issue(g.dma_start(
                        out=outs["ew_tok"][t * tb + tc * 128:t * tb + (tc + 1) * 128,
                                           ob * 512:(ob + 1) * 512],
                        in_=evo[oit % 2][:, :]), oit % 2)
                b.gpd(_st)
                oit += 1

        prev_tb = {b.ve: b.ve.n, b.sc: b.sc.n, b.te: b.te.n, **oring.marks()}
    return oring.marks()


def stage_wkv(b, tag, ins, consts, outs, T, inmarks):
    import os
    WCUT = int(os.environ.get("WCUT", "99"))
    NCHK = T // CH
    NHP = 8
    eps = 1e-5 * 64
    lr = DmaRing(b, f"{tag}ld", 2)
    oring = DmaRing(b, f"{tag}os", 2)

    ucol, ucm = load_col(b, f"{tag}u", ins["u"], NDC)
    gcol, gcm = load_col(b, f"{tag}ga", ins["gamma"], NDC)
    bcol, bcm = load_col(b, f"{tag}be", ins["beta"], NDC)
    epst = b.sb(f"{tag}ep", [128, 1], F32)
    u_bd = [b.sb(f"{tag}ub{hp}", [128, 2], BF16) for hp in range(NHP)]
    S_bd = [b.sb(f"{tag}S{hp}", [128, 128], F32) for hp in range(NHP)]
    S_bf = [b.sb(f"{tag}Sb{hp}", [128, 128], BF16) for hp in range(NHP)]

    def _init(v, I):
        gate(v, ucm); gate(v, gcm); gate(v, bcm)
        I(v.memset(epst[:, :], eps))
        for hp in range(NHP):
            I(v.memset(S_bd[hp][:, :], 0.0))
            I(v.memset(S_bf[hp][:, :], 0.0))
            I(v.memset(u_bd[hp][:, :], 0.0))
        for hp in range(NHP):
            v.wait_ge(b.ve.h, b.ve.n)
            I(v.tensor_copy(u_bd[hp][0:64, 0:1], ucol[0:64, hp:hp + 1]))
            v.wait_ge(b.ve.h, b.ve.n)
            I(v.tensor_copy(u_bd[hp][64:128, 1:2], ucol[64:128, hp:hp + 1]))
    b.vec(_init)
    initm = snap(b.ve)

    ew_c = b.sb(f"{tag}ew", [128, C], BF16)
    kt_c = b.sb(f"{tag}ktk", [128, C], BF16)
    vt_c = b.sb(f"{tag}vtk", [128, C], BF16)
    E2 = b.sb(f"{tag}E2", [128, C], BF16)
    khat = b.sb(f"{tag}kh", [128, C], BF16)
    r_c = [b.sb(f"{tag}rc{i}", [128, 128], BF16) for i in range(NHP)]
    k_c = [b.sb(f"{tag}kc{i}", [128, 128], BF16) for i in range(NHP)]
    g_c = [b.sb(f"{tag}gc{i}", [128, 128], BF16) for i in range(NHP)]
    Efeat = b.sb(f"{tag}Ef", [128, 259], BF16)
    rt = b.sb(f"{tag}rt", [128, 128], BF16)
    ktl = b.sb(f"{tag}kt", [128, 128], BF16)
    Am = [b.sb(f"{tag}Am{i}", [128, 128], BF16) for i in range(2)]
    prk = b.sb(f"{tag}prk", [128, 128], BF16)
    vmod = b.sb(f"{tag}vm", [128, 128], BF16)
    y_sb = b.sb(f"{tag}ysb", [128, 128], BF16)
    ysq = b.sb(f"{tag}ysq", [128, 128], BF16)
    s_mu = b.sb(f"{tag}smu", [2, 128], F32)
    s_m2 = b.sb(f"{tag}sm2", [2, 128], F32)
    s_va = b.sb(f"{tag}sva", [2, 128], F32)
    s_rs = b.sb(f"{tag}srs", [2, 128], F32)
    s_bm = b.sb(f"{tag}sbm", [2, 128], F32)
    s_rsb = b.sb(f"{tag}srb", [2, 128], BF16)
    s_bmb = b.sb(f"{tag}sbb", [2, 128], BF16)
    plq = b.sb(f"{tag}plq", [128, 3], F32)
    t_y1 = b.sb(f"{tag}ty1", [128, 128], F32)
    t_y2 = b.sb(f"{tag}ty2", [128, 128], F32)
    t_s2 = b.sb(f"{tag}ts2", [128, 128], F32)
    ev_o = [b.sb(f"{tag}evo{i}", [128, 128], BF16) for i in range(2)]
    p_karg = b.ps(f"{tag}pka", [128, C], F32)
    _pf = b.ps(f"{tag}pf", [128, 512], F32)
    _pa = b.ps(f"{tag}pa", [128, 512], F32)
    _pb = b.ps(f"{tag}pb", [128, 512], F32)
    _pc = b.ps(f"{tag}pc", [128, 512], F32)
    p_feat = _pf[:, 0:259]
    p_bc = [_pf[:, 260:388], _pb[:, 258:386]]
    p_A = [_pa[:, 0:128], _pc[:, 0:128]]
    p_Y = _pa[:, 256:384]
    p_dS = _pc[:, 256:384]
    p_d = _pb[:, 0:2]
    p_st = [_pb[0:2, 2:130], _pb[0:2, 130:258]]

    prevc = dict(inmarks)

    for c in range(NCHK):
        t0 = c * CH
        pm = dict(prevc)
        def _ld(sync, c=c, t0=t0, pm=pm):
            gate(sync, pm)
            if c == 0:
                gate(sync, initm)
            lr.issue(sync.dma_start(out=ew_c[:, :], in_=ins["ew_tok"][t0:t0 + CH, :]), 0)
            lr.issue(sync.dma_start(out=kt_c[:, :], in_=ins["k_tok"][t0:t0 + CH, :]), 0)
            lr.issue(sync.dma_start(out=vt_c[:, :], in_=ins["v_tok"][t0:t0 + CH, :]), 0)
            for hp in range(NHP):
                lr.issue(sync.dma_start(out=r_c[hp][:, :],
                                        in_=ins["r_fm"][hp * 128:(hp + 1) * 128, t0:t0 + CH]), 0)
                lr.issue(sync.dma_start(out=k_c[hp][:, :],
                                        in_=ins["k_fm"][hp * 128:(hp + 1) * 128, t0:t0 + CH]), 0)
                lr.issue(sync.dma_start(out=g_c[hp][:, :],
                                        in_=ins["g_fm"][hp * 128:(hp + 1) * 128, t0:t0 + CH]), 0)
        b.syn(_ld)
        ldm = lr.marks()

        def _ctok(te, ldm=ldm, pm=pm):
            gate(te, ldm)
            gate(te, pm)
            te.matmul(p_karg[:, 0:512], consts["rtok"][:, :], ew_c[:, 0:512],
                      start=True, stop=True)
            return te.matmul(p_karg[:, 512:1024], consts["rtok"][:, :], ew_c[:, 512:1024],
                             start=True, stop=True)
        b.ten(_ctok)
        ctm = snap(b.te)
        def _e2(s, I, ctm=ctm):
            gate(s, ctm)
            I(s.activation(E2[:, :], p_karg[:, :], AF.Exp))
        b.scl(_e2)
        e2m = snap(b.sc)
        def _kh(v, I, e2m=e2m):
            gate(v, e2m)
            I(v.tensor_mul(khat[:, :], kt_c[:, :], E2[:, :]))
        b.vec(_kh)
        khm = snap(b.ve)

        for hp in range(NHP):
            def _cf(te, hp=hp, ldm=ldm):
                gate(te, ldm)
                gate(te, {b.sc: b.sc.n})
                return te.matmul(p_feat, ew_c[:, hp * 128:(hp + 1) * 128],
                                 consts["rfeat"][:, :], start=True, stop=True)
            b.ten(_cf)
            cfm = snap(b.te)
            def _ef(s, I, cfm=cfm):
                gate(s, cfm)
                gate(s, {b.ve: b.ve.n})
                I(s.activation(Efeat[:, :], p_feat, AF.Exp))
                s.wait_ge(b.sc.h, b.sc.n)
                I(s.activation(plq[:, :], _pf[:, 256:259], AF.Exp))
            b.scl(_ef)
            efm = snap(b.sc)
            def _rk(v, I, hp=hp, efm=efm, khm=khm):
                gate(v, efm)
                gate(v, khm)
                gate(v, {b.te: b.te.n})
                I(v.tensor_scalar_mul(S_bf[hp][:, :], S_bd[hp][:, :], plq[:, 2:3]))
                v.wait_ge(b.ve.h, b.ve.n)
                I(v.tensor_mul(rt[:, :], r_c[hp][:, :], Efeat[:, 0:128]))
                v.wait_ge(b.ve.h, b.ve.n)
                I(v.tensor_mul(ktl[:, :], k_c[hp][:, :], Efeat[:, 128:256]))
                v.wait_ge(b.ve.h, b.ve.n)
                I(v.tensor_mul(prk[:, :], r_c[hp][:, :], k_c[hp][:, :]))
            b.vec(_rk)
            rkm = snap(b.ve)
            def _am(te, hp=hp, rkm=rkm):
                gate(te, rkm)
                te.matmul(p_A[0], ktl[0:64, :], rt[0:64, :],
                          start=True, stop=True, tile_position=(0, 0))
                te.matmul(p_A[1], ktl[64:128, :], rt[64:128, :],
                          start=True, stop=True, tile_position=(64, 0))
                return te.matmul(p_d, prk[:, :], u_bd[hp][:, :],
                                 start=True, stop=True)
            if WCUT >= 3:
                b.ten(_am)
            amm = snap(b.te)
            def _mk(v, I, hp=hp, amm=amm):
                gate(v, amm)
                I(v.tensor_mul(Am[0][:, :], p_A[0], consts["swkv"][:, :]))
                v.wait_ge(b.ve.h, b.ve.n)
                I(v.tensor_mul(Am[1][:, :], p_A[1], consts["swkv"][:, :]))
                v.wait_ge(b.ve.h, b.ve.n)
                vv = vt_c[:, hp * 128:(hp + 1) * 128].rearrange("p (h j) -> p h j", h=2)
                dd = p_d.unsqueeze(2).broadcast_to([128, 2, 64])
                I(v.tensor_tensor(vmod[:, :].rearrange("p (h j) -> p h j", h=2),
                                  vv, dd, ALU.mult))
            if WCUT >= 4:
                b.vec(_mk)
            mkm = snap(b.ve)
            def _ym(te, hp=hp, mkm=mkm):
                gate(te, mkm)
                gate(te, {b.sc: b.sc.n})
                te.matmul(p_Y[0:64, :], vt_c[:, hp * 128:hp * 128 + 64], Am[0][:, :],
                          start=True, stop=False, tile_position=(0, 0))
                te.matmul(p_Y[0:64, :], vmod[:, 0:64], consts["ident"][:, :],
                          start=False, stop=False, tile_position=(0, 0))
                te.matmul(p_Y[0:64, :], S_bf[hp][:, 0:64], rt[:, :],
                          start=False, stop=True, tile_position=(0, 0))
                te.matmul(p_Y[64:128, :], vt_c[:, hp * 128 + 64:hp * 128 + 128], Am[1][:, :],
                          start=True, stop=False, tile_position=(0, 64))
                te.matmul(p_Y[64:128, :], vmod[:, 64:128], consts["ident"][:, :],
                          start=False, stop=False, tile_position=(0, 64))
                return te.matmul(p_Y[64:128, :], S_bf[hp][:, 64:128], rt[:, :],
                                 start=False, stop=True, tile_position=(0, 64))
            if WCUT >= 5:
                b.ten(_ym)
            ymm = snap(b.te)
            def _ye(s, I, ymm=ymm):
                gate(s, ymm)
                gate(s, {b.ve: b.ve.n})
                I(s.activation(y_sb[:, :], p_Y, AF.Copy))
                s.wait_ge(b.sc.h, b.sc.n)
                I(s.activation(ysq[:, :], p_Y, AF.Square))
            if WCUT >= 6:
                b.scl(_ye)
            yem = snap(b.sc)
            def _st(te, yem=yem):
                gate(te, yem)
                gate(te, {b.ve: b.ve.n})
                te.matmul(p_st[0], consts["bd2"][:, :], y_sb[:, :],
                          start=True, stop=True)
                return te.matmul(p_st[1], consts["bd2"][:, :], ysq[:, :],
                                 start=True, stop=True)
            if WCUT >= 6:
                b.ten(_st)
            stm = snap(b.te)
            def _sv(v, I, stm=stm):
                gate(v, stm)
                I(v.tensor_scalar_mul(s_mu[:, :], p_st[0], 1.0 / 64))
            if WCUT >= 6:
                b.vec(_sv)
            svm = snap(b.ve)
            def _ss(s, I, svm=svm):
                gate(s, svm)
                I(s.activation(s_m2[:, :], s_mu[:, :], AF.Square))
            if WCUT >= 6:
                b.scl(_ss)
            ssm = snap(b.sc)
            def _sv2(v, I, ssm=ssm):
                gate(v, ssm)
                I(v.scalar_tensor_tensor(s_va[:, :], p_st[1], 1.0 / 64, s_m2[:, :],
                                         ALU.mult, ALU.subtract))
            if WCUT >= 6:
                b.vec(_sv2)
            sv2m = snap(b.ve)
            def _ss2(s, I, sv2m=sv2m):
                gate(s, sv2m)
                I(s.activation(s_va[:, :], s_va[:, :], AF.Ln, bias=epst[0:2, 0:1]))
                s.wait_ge(b.sc.h, b.sc.n)
                I(s.activation(s_rs[:, :], s_va[:, :], AF.Exp, scale=-0.5))
            if WCUT >= 6:
                b.scl(_ss2)
            ss2m = snap(b.sc)
            def _sv3(v, I, ss2m=ss2m):
                gate(v, ss2m)
                I(v.tensor_mul(s_bm[:, :], s_mu[:, :], s_rs[:, :]))
                v.wait_ge(b.ve.h, b.ve.n)
                I(v.tensor_copy(s_rsb[:, :], s_rs[:, :]))
                v.wait_ge(b.ve.h, b.ve.n)
                I(v.tensor_copy(s_bmb[:, :], s_bm[:, :]))
            if WCUT >= 6:
                b.vec(_sv3)
            sv3m = snap(b.ve)
            def _bc(te, sv3m=sv3m):
                gate(te, sv3m)
                te.matmul(p_bc[0], consts["selh"][:, :], s_rsb[:, :],
                          start=True, stop=True)
                return te.matmul(p_bc[1], consts["selh"][:, :], s_bmb[:, :],
                                 start=True, stop=True)
            if WCUT >= 6:
                b.ten(_bc)
            bcm2 = snap(b.te)
            def _yn(v, I, hp=hp, bcm2=bcm2, oit=c * NHP + hp):
                gate(v, bcm2)
                I(v.tensor_mul(t_y1[:, :], y_sb[:, :], p_bc[0]))
                v.wait_ge(b.ve.h, b.ve.n)
                I(v.tensor_sub(t_y2[:, :], t_y1[:, :], p_bc[1]))
                v.wait_ge(b.ve.h, b.ve.n)
                I(v.tensor_scalar(t_y1[:, :], t_y2[:, :], gcol[:, hp:hp + 1],
                                  bcol[:, hp:hp + 1], ALU.mult, ALU.add))
                v.wait_ge(b.ve.h, b.ve.n)
                if oit >= 2:
                    oring.wait_slot(v, oit % 2)
                I(v.tensor_mul(ev_o[oit % 2][:, :], t_y1[:, :], g_c[hp][:, :]))
            if WCUT >= 6:
                b.vec(_yn)
            ynm = snap(b.ve)
            def _so(g, hp=hp, c=c, t0=t0, ynm=ynm, oit=c * NHP + hp):
                gate(g, ynm)
                oring.issue(g.dma_start(
                    out=outs["yng_fm"][hp * 128:(hp + 1) * 128, t0:t0 + CH],
                    in_=ev_o[oit % 2][:, :]), oit % 2)
            if WCUT >= 7:
                b.gpd(_so)
            def _ds(te, hp=hp, khm=khm):
                gate(te, khm)
                gate(te, {b.ve: b.ve.n})
                te.matmul(p_dS[0:64, 0:64], khat[:, hp * 128:hp * 128 + 64],
                          vt_c[:, hp * 128:hp * 128 + 64],
                          start=True, stop=True, tile_position=(0, 0))
                return te.matmul(p_dS[64:128, 64:128], khat[:, hp * 128 + 64:hp * 128 + 128],
                                 vt_c[:, hp * 128 + 64:hp * 128 + 128],
                                 start=True, stop=True, tile_position=(0, 64))
            if WCUT >= 8:
                b.ten(_ds)
            dsm = snap(b.te)
            def _su(v, I, hp=hp, dsm=dsm):
                gate(v, dsm)
                I(v.tensor_scalar_mul(t_s2[0:64, 0:64], p_dS[0:64, 0:64],
                                      plq[0:64, 1:2]))
                v.wait_ge(b.ve.h, b.ve.n)
                I(v.tensor_scalar_mul(t_s2[64:128, 64:128], p_dS[64:128, 64:128],
                                      plq[64:128, 1:2]))
                v.wait_ge(b.ve.h, b.ve.n)
                I(v.tensor_scalar_mul(S_bd[hp][:, :], S_bd[hp][:, :], plq[:, 0:1]))
                v.wait_ge(b.ve.h, b.ve.n)
                I(v.tensor_add(S_bd[hp][0:64, 0:64], S_bd[hp][0:64, 0:64], t_s2[0:64, 0:64]))
                v.wait_ge(b.ve.h, b.ve.n)
                I(v.tensor_add(S_bd[hp][64:128, 64:128], S_bd[hp][64:128, 64:128],
                               t_s2[64:128, 64:128]))
            if WCUT >= 8:
                b.vec(_su)
        prevc = {b.ve: b.ve.n, b.sc: b.sc.n, b.te: b.te.n, **oring.marks()}
    return oring.marks()

def host_consts(T):
    import ml_dtypes
    ar = np.arange(CH)
    LTincl = (ar[:, None] <= ar[None, :]).astype(np.float32)
    LTstrict = (ar[:, None] < ar[None, :]).astype(np.float32)
    B63 = np.zeros((CH, CH), np.float32)
    B63[:CH // 2, :] = 1.0
    R1 = B63 - LTstrict
    R2 = LTincl - B63
    colL = -np.ones((CH, 1), np.float32)
    colQ = B63[:, :1] - 1.0
    colM = -B63[:, :1]
    Rfeat = np.concatenate([R1, R2, colL, colQ, colM], axis=1)
    c = {
        "c_ones128": np.ones((128, 128), np.float32),
        "c_rfeat": Rfeat.astype(ml_dtypes.bfloat16),
        "c_rtok": R2.astype(ml_dtypes.bfloat16),
        "c_ident": np.eye(128, dtype=ml_dtypes.bfloat16),
        "c_swkv": (ar[:, None] < ar[None, :]).astype(ml_dtypes.bfloat16),
        "c_bd2": np.concatenate(
            [np.concatenate([np.ones((64, 1)), np.zeros((64, 1))]),
             np.concatenate([np.zeros((64, 1)), np.ones((64, 1))])],
            axis=1).astype(ml_dtypes.bfloat16),
        "c_selh": np.concatenate(
            [np.concatenate([np.ones((1, 64)), np.zeros((1, 64))], 1),
             np.concatenate([np.zeros((1, 64)), np.ones((1, 64))], 1)],
            0).astype(ml_dtypes.bfloat16),
        "c_minf": np.where(ar[:, None] > ar[None, :], -1e30, 0.0).astype(np.float32),
    }
    slopes = (2.0 ** (-8.0 / H)) ** np.arange(1, H + 1)
    kcoord = np.arange(T, dtype=np.float32)
    iota2 = np.zeros((2, 128), np.float32)
    iota2[0] = np.arange(128)
    iota2[1] = 1.0
    slopes2 = np.zeros((2, 512), np.float32)
    for h in range(H):
        for kc in range(min(32, T // 128)):
            slopes2[0, h * 32 + kc] = slopes[h]
            slopes2[1, h * 32 + kc] = slopes[h] * 128.0 * kc
    qbn = np.zeros((H, 1, T), np.float32)
    for h in range(H):
        qbn[h, 0] = -slopes[h] * kcoord
    c["c_iota2"] = iota2
    c["c_slopes2"] = slopes2
    c["c_qbn"] = qbn
    return c


def stage_l0fattn(b, tag, q_dram, k_dram, v_dram, out_dram, T, inmarks):
    NKC = T // CH
    G = 4
    NQB = T // 512
    EXPW = min(1024, T)
    NHALF = T // EXPW
    lr = DmaRing(b, f"{tag}ld", 2)
    oring = DmaRing(b, f"{tag}os", 2)

    qt = b.sb(f"{tag}q", [128, T], BF16)
    kt = b.sb(f"{tag}k", [128, T], BF16)
    vv = b.sb(f"{tag}v", [128, 128], BF16)
    vn = [b.sb(f"{tag}vn{i}", [128, 128], BF16) for i in range(G)]
    E1 = [b.sb(f"{tag}E1_{i}", [128, T], BF16) for i in range(G)]
    E2 = [b.sb(f"{tag}E2_{i}", [128, T], BF16) for i in range(G)]
    zp = b.sb(f"{tag}zp", [128, 2 * NHALF], F32)
    zz = b.sb(f"{tag}zz", [128, 2], F32)
    rz = b.sb(f"{tag}rz", [128, 2], F32)
    acc = [b.sb(f"{tag}ac{i}", [128, T], F32) for i in range(1)]
    evo = [b.sb(f"{tag}ev{i}", [128, 512], BF16) for i in range(2)]
    p_s1 = b.ps(f"{tag}ps1", [128, 1024], F32)
    p_s2 = b.ps(f"{tag}ps2", [128, 1024], F32)
    p_av = [b.ps(f"{tag}pav{i}", [128, 512], F32) for i in range(2)]
    p_aw = [b.ps(f"{tag}paw{i}", [128, 512], F32) for i in range(2)]

    prev_hp = dict(inmarks)
    for hp in range(NDC):
        pmh = dict(prev_hp)
        def _ldq(sync, hp=hp, pmh=pmh):
            gate(sync, pmh)
            lr.issue(sync.dma_start(out=qt[:, :], in_=q_dram[hp * 128:(hp + 1) * 128, :]), 0)
            lr.issue(sync.dma_start(out=kt[:, :], in_=k_dram[hp * 128:(hp + 1) * 128, :]), 0)
        b.syn(_ldq)
        qm = lr.marks()
        prev_g = {}
        for kg in range(NKC // G):
            pmg = dict(prev_g)
            for ki in range(G):
                kc = kg * G + ki
                k0 = kc * CH
                for half in range(NHALF):
                    def _s1(te, hp=hp, kc=kc, k0=k0, half=half, qm=qm, pmg=pmg, ki=ki):
                        if half == 0 and ki == 0:
                            gate(te, qm)
                            gate(te, pmg)
                        gate(te, {b.sc: b.sc.n})
                        last = None
                        for sub in range(EXPW // 512):
                            q0 = half * EXPW + sub * 512
                            te.matmul(p_s1[:, sub * 512:(sub + 1) * 512],
                                      kt[0:64, k0:k0 + CH], qt[0:64, q0:q0 + 512],
                                      start=True, stop=True, tile_position=(0, 0))
                            last = te.matmul(p_s2[:, sub * 512:(sub + 1) * 512],
                                             kt[64:128, k0:k0 + CH], qt[64:128, q0:q0 + 512],
                                             start=True, stop=True, tile_position=(64, 0))
                        return last
                    b.ten(_s1)
                    tm = snap(b.te)
                    def _ex(s, I, ki=ki, half=half, tm=tm):
                        gate(s, tm)
                        gate(s, {b.ve: b.ve.n})
                        I(s.activation(E1[ki][:, half * EXPW:(half + 1) * EXPW],
                                       p_s1[:, 0:EXPW], AF.Exp,
                                       accum_out=zp[:, 2 * half:2 * half + 1]))
                        s.wait_ge(b.sc.h, b.sc.n)
                        I(s.activation(E2[ki][:, half * EXPW:(half + 1) * EXPW],
                                       p_s2[:, 0:EXPW], AF.Exp,
                                       accum_out=zp[:, 2 * half + 1:2 * half + 2]))
                    b.scl(_ex)
                exm = snap(b.sc)
                def _ldv(sync, hp=hp, k0=k0, exm=exm):
                    gate(sync, {b.ve: b.ve.n})
                    lr.issue(sync.dma_start(out=vv[:, :],
                                            in_=v_dram[k0:k0 + CH, hp * 128:(hp + 1) * 128]), 1)
                b.syn(_ldv)
                vm_ = {lr.sems[1]: lr.sems[1].n}
                def _z(v, I, ki=ki, exm=exm, vm_=vm_):
                    gate(v, exm)
                    gate(v, vm_)
                    gate(v, {b.te: b.te.n})
                    z3 = zp[:, :].rearrange("p (a b) -> p b a", b=2)
                    I(v.tensor_reduce(zz[:, :], z3, AX.X, ALU.add))
                    v.wait_ge(b.ve.h, b.ve.n)
                    I(v.reciprocal(rz[:, :], zz[:, :]))
                    v.wait_ge(b.ve.h, b.ve.n)
                    v3 = vv[:, :].rearrange("p (h j) -> p h j", h=2)
                    r3 = rz[:, :].unsqueeze(2).broadcast_to([128, 2, 64])
                    I(v.tensor_tensor(vn[ki][:, :].rearrange("p (h j) -> p h j", h=2),
                                      v3, r3, ALU.mult))
                b.vec(_z)
            egm = {b.ve: b.ve.n, b.sc: b.sc.n}
            for qb in range(NQB):
                pb = qb % 2
                def _av(te, hp=hp, kg=kg, qb=qb, pb=pb, egm=egm):
                    gate(te, egm)
                    gate(te, {b.ve: b.ve.n})
                    for ki in range(G):
                        te.matmul(p_av[pb][0:64, :], vn[ki][:, 0:64],
                                  E1[ki][:, qb * 512:(qb + 1) * 512],
                                  start=(ki == 0), stop=(ki == G - 1), tile_position=(0, 0))
                    last = None
                    for ki in range(G):
                        last = te.matmul(p_aw[pb][64:128, :], vn[ki][:, 64:128],
                                         E2[ki][:, qb * 512:(qb + 1) * 512],
                                         start=(ki == 0), stop=(ki == G - 1),
                                         tile_position=(0, 64))
                    return last
                b.ten(_av)
                avm = snap(b.te)
                def _acc(v, I, kg=kg, qb=qb, pb=pb, avm=avm):
                    gate(v, avm)
                    if kg == 0:
                        I(v.tensor_copy(acc[0][0:64, qb * 512:(qb + 1) * 512], p_av[pb][0:64, :]))
                        v.wait_ge(b.ve.h, b.ve.n)
                        I(v.tensor_copy(acc[0][64:128, qb * 512:(qb + 1) * 512], p_aw[pb][64:128, :]))
                    else:
                        I(v.tensor_add(acc[0][0:64, qb * 512:(qb + 1) * 512],
                                       acc[0][0:64, qb * 512:(qb + 1) * 512], p_av[pb][0:64, :]))
                        v.wait_ge(b.ve.h, b.ve.n)
                        I(v.tensor_add(acc[0][64:128, qb * 512:(qb + 1) * 512],
                                       acc[0][64:128, qb * 512:(qb + 1) * 512], p_aw[pb][64:128, :]))
                b.vec(_acc)
            prev_g = {b.ve: b.ve.n}
        accm = {b.ve: b.ve.n}
        for qb in range(NQB):
            def _cv(v, I, qb=qb, accm=accm, oit=hp * NQB + qb):
                gate(v, accm)
                if oit >= 2:
                    oring.wait_slot(v, oit % 2)
                I(v.tensor_copy(evo[oit % 2][:, :], acc[0][:, qb * 512:(qb + 1) * 512]))
            b.vec(_cv)
            cvm_ = snap(b.ve)
            def _st(g, hp=hp, qb=qb, cvm_=cvm_, oit=hp * NQB + qb):
                gate(g, cvm_)
                oring.issue(g.dma_start(
                    out=out_dram[hp * 128:(hp + 1) * 128, qb * 512:(qb + 1) * 512],
                    in_=evo[oit % 2][:, :]), oit % 2)
            b.gpd(_st)
        prev_hp = {b.ve: b.ve.n, b.sc: b.sc.n, b.te: b.te.n, **oring.marks()}
    return oring.marks()


def stage_ffntail(b, tag, z1_dram, wff_dram, out_dram, T, ones_f32, omarks, inmarks):
    tb = 512
    ntb = T // tb
    wt, wm = load_cast_fm(b, f"{tag}w", wff_dram, NDC, C)
    zt, zm = load_fm_raw(b, f"{tag}z", z1_dram, NDC, T, BF16, wait=inmarks)
    oring = DmaRing(b, f"{tag}os", 2)
    p_z = [b.ps(f"{tag}pz{i}", [128, tb], F32) for i in range(2)]
    p_ss = [b.ps(f"{tag}ss{i}", [128, tb], F32) for i in range(2)]
    z2sb = [b.sb(f"{tag}z2{oc}", [128, tb], BF16) for oc in range(NDC)]
    zsq = b.sb(f"{tag}zq", [128, tb], F32)
    rn = b.sb(f"{tag}rn", [128, tb], F32)
    t1 = b.sb(f"{tag}t1", [128, tb], F32)
    te_ = b.sb(f"{tag}te", [128, tb], F32)
    t3 = b.sb(f"{tag}t3", [128, tb], F32)
    s4 = b.sb(f"{tag}s4", [128, tb], F32)
    evo = [b.sb(f"{tag}ev{i}", [128, tb], BF16) for i in range(2)]
    prev = {}
    for t in range(ntb):
        pm = dict(prev)
        for oc in range(NDC):
            pb = oc % 2
            def _mm(te, t=t, oc=oc, pb=pb, pm=pm):
                if t == 0 and oc == 0:
                    gate(te, wm); gate(te, zm); gate(te, omarks)
                if oc < 2:
                    gate(te, pm)
                gate(te, {b.sc: b.sc.n})
                last = None
                for dc in range(NDC):
                    last = te.matmul(p_z[pb][:, :], wt[dc][:, oc * 128:(oc + 1) * 128],
                                     zt[dc][:, t * tb:(t + 1) * tb],
                                     start=(dc == 0), stop=(dc == NDC - 1))
                return last
            b.ten(_mm)
            tm = snap(b.te)
            def _ev(s, I, oc=oc, pb=pb, tm=tm, pm=pm):
                gate(s, tm)
                gate(s, pm)
                I(s.activation(z2sb[oc][:, :], p_z[pb][:, :], AF.Copy))
                s.wait_ge(b.sc.h, b.sc.n)
                I(s.activation(zsq[:, :], p_z[pb][:, :], AF.Square))
            b.scl(_ev)
            evm = snap(b.sc)
            def _ssm(te, t=t, oc=oc, evm=evm):
                gate(te, evm)
                return te.matmul(p_ss[t % 2][:, :], ones_f32[:, :], zsq[:, :],
                                 start=(oc == 0), stop=(oc == NDC - 1))
            b.ten(_ssm)
        ssm = snap(b.te)
        def _rn(s, I, t=t, ssm=ssm):
            gate(s, ssm)
            gate(s, {b.ve: b.ve.n})
            I(s.activation(rn[:, :], p_ss[t % 2][:, :], AF.Ln))
            s.wait_ge(b.sc.h, b.sc.n)
            I(s.activation(rn[:, :], rn[:, :], AF.Exp, scale=-0.5))
        b.scl(_rn)
        rnm = snap(b.sc)
        for oc in range(NDC):
            oit = t * NDC + oc
            def _g(v, I, oc=oc, rnm=rnm, oit=oit):
                gate(v, rnm)
                gate(v, {b.sc: b.sc.n})
                I(v.tensor_mul(t1[:, :], z2sb[oc][:, :], rn[:, :]))
            b.vec(_g)
            g1m = snap(b.ve)
            def _e(s, I, g1m=g1m, oit=oit):
                gate(s, g1m)
                if oit >= 2:
                    oring.wait_slot(s, oit % 2)
                I(s.activation(t3[:, :], t1[:, :], AF.Gelu))
                s.wait_ge(b.sc.h, b.sc.n)
                I(s.activation(evo[oit % 2][:, :], t3[:, :], AF.Silu))
            b.scl(_e)
            g3m = snap(b.sc)
            def _st(g, t=t, oc=oc, g3m=g3m, oit=oit):
                gate(g, g3m)
                oring.issue(g.dma_start(
                    out=out_dram[oc * 128:(oc + 1) * 128, t * tb:(t + 1) * tb],
                    in_=evo[oit % 2][:, :]), oit % 2)
            b.gpd(_st)
        prev = {b.ve: b.ve.n, b.sc: b.sc.n, b.te: b.te.n, **oring.marks()}
    return oring.marks()



def stage_combine(b, tag, x_dram, s_dram, g_dram, out_dram, T, inmarks):
    tb = 512
    ntb = T // tb
    lrx = DmaRing(b, f"{tag}lx", 2)
    lrs = DmaRing(b, f"{tag}ls", 2)
    oring = DmaRing(b, f"{tag}os", 2)
    xb = [b.sb(f"{tag}x{i}", [128, tb], F32) for i in range(2)]
    sb_ = [b.sb(f"{tag}s{i}", [128, tb], BF16) for i in range(2)]
    gb = [b.sb(f"{tag}g{i}", [128, tb], BF16) for i in range(2)]
    tt = b.sb(f"{tag}t", [128, tb], F32)
    ob = [b.sb(f"{tag}o{i}", [128, tb], F32) for i in range(2)]
    vm_hist = [{}, {}]
    it = 0
    for t in range(ntb):
        for dc in range(NDC):
            sl = it % 2
            def _ld(sync, t=t, dc=dc, sl=sl, pm=vm_hist[sl]):
                if it == 0:
                    gate(sync, inmarks)
                gate(sync, pm)
                lrx.issue(sync.dma_start(out=xb[sl][:, :],
                                         in_=x_dram[dc * 128:(dc + 1) * 128, t * tb:(t + 1) * tb]), sl)
                lrs.issue(sync.dma_start(out=sb_[sl][:, :],
                                         in_=s_dram[dc * 128:(dc + 1) * 128, t * tb:(t + 1) * tb]), sl)
                lrs.issue(sync.dma_start(out=gb[sl][:, :],
                                         in_=g_dram[dc * 128:(dc + 1) * 128, t * tb:(t + 1) * tb]), sl)
            b.syn(_ld)
            lm = {lrx.sems[sl]: lrx.sems[sl].n, lrs.sems[sl]: lrs.sems[sl].n}
            def _v(v, I, sl=sl, lm=lm, it=it):
                gate(v, lm)
                if it >= 2:
                    oring.wait_slot(v, sl)
                I(v.tensor_mul(tt[:, :], sb_[sl][:, :], gb[sl][:, :]))
                v.wait_ge(b.ve.h, b.ve.n)
                I(v.tensor_add(ob[sl][:, :], xb[sl][:, :], tt[:, :]))
            b.vec(_v)
            vm = snap(b.ve)
            vm_hist[sl] = vm
            def _st(g, t=t, dc=dc, sl=sl, vm=vm):
                gate(g, vm)
                oring.issue(g.dma_start(
                    out=out_dram[dc * 128:(dc + 1) * 128, t * tb:(t + 1) * tb],
                    in_=ob[sl][:, :]), sl)
            b.gpd(_st)
            it += 1
    return oring.marks()


def stage_l1attn(b, tag, q_dram, k_dram, v_dram, iota2_dram, slopes2_dram,
                 qbn_dram, minf_dram, out_dram, T, inmarks):
    NQT = T // 512
    NKC = T // CH
    lr = DmaRing(b, f"{tag}ld", 2)
    oring = DmaRing(b, f"{tag}os", 2)
    qt_ = b.sb(f"{tag}q", [128, T], BF16)
    kt_ = b.sb(f"{tag}k", [128, T], BF16)
    v_hp = b.sb(f"{tag}v", [128, T], BF16)
    iota2 = b.sb(f"{tag}i2", [2, 128], F32)
    slopes2 = b.sb(f"{tag}s2", [2, 512], F32)
    bcolmat = b.sb(f"{tag}bc", [128, 512], F32)
    qbs = [b.sb(f"{tag}qb{i}", [1, 512], F32) for i in range(2)]
    minf = b.sb(f"{tag}mi", [128, 128], F32)
    E = [b.sb(f"{tag}E{i}", [128, 512], BF16) for i in range(4)]
    onesb = b.sb(f"{tag}on", [128, 32], BF16)
    ones1 = b.sb(f"{tag}o1", [1, 128], BF16)
    ones1f = b.sb(f"{tag}o1f", [1, 128], F32)
    zrow = b.sb(f"{tag}zr", [1, 512], F32)
    rzrow = b.sb(f"{tag}rz", [1, 512], F32)
    rzb = b.sb(f"{tag}rb", [1, 512], BF16)
    avsb = b.sb(f"{tag}av", [128, 512], F32)
    evo = [b.sb(f"{tag}ev{i}", [128, 512], BF16) for i in range(2)]
    p_s = [b.ps(f"{tag}ps{i}", [128, 512], F32) for i in range(2)]
    p_av = [b.ps(f"{tag}pa{i}", [128, 512], F32) for i in range(2)]
    p_z = [b.ps(f"{tag}pz{i}", [32, 512], F32) for i in range(2)]
    p_bc = b.ps(f"{tag}pb", [128, 512], F32)

    def _c0(g, I):
        I(g.memset(onesb[:, :], 1.0))
        I(g.memset(ones1[:, :], 1.0))
        I(g.memset(ones1f[:, :], 1.0))
    b.gpc(_c0)
    c0m = snap(b.gc)
    def _ldm(sync):
        gate(sync, inmarks)
        lr.issue(sync.dma_start(out=minf[:, :], in_=minf_dram[:, :]), 0)
        lr.issue(sync.dma_start(out=iota2[:, :], in_=iota2_dram[:, :]), 0)
        lr.issue(sync.dma_start(out=slopes2[:, :], in_=slopes2_dram[:, :]), 0)
    b.syn(_ldm)
    ldm0 = lr.marks()
    def _bcol(te, ldm0=ldm0):
        gate(te, ldm0)
        return te.matmul(p_bc[:, :], iota2[:, :], slopes2[:, :], start=True, stop=True)
    b.ten(_bcol)
    bm0 = snap(b.te)
    def _bev(v, I, bm0=bm0):
        gate(v, bm0)
        I(v.tensor_copy(bcolmat[:, :], p_bc[:, :]))
    b.vec(_bev)
    bcolm = snap(b.ve)

    prev_hp = dict(inmarks)
    for hp in range(NDC):
        pmh = dict(prev_hp)
        def _ldq(sync, hp=hp, pmh=pmh):
            gate(sync, pmh)
            if hp == 0:
                gate(sync, ldm0)
            lr.issue(sync.dma_start(out=qt_[:, :], in_=q_dram[hp * 128:(hp + 1) * 128, :]), 0)
            lr.issue(sync.dma_start(out=kt_[:, :], in_=k_dram[hp * 128:(hp + 1) * 128, :]), 0)
            for cc in range(T // 128):
                lr.issue(sync.dma_start(out=v_hp[:, cc * 128:(cc + 1) * 128],
                                        in_=v_dram[cc * 128:(cc + 1) * 128,
                                                   hp * 128:(hp + 1) * 128]), 0)
        b.syn(_ldq)
        qm = lr.marks()
        for qt in range(NQT):
            q0 = qt * 512
            nkc = qt * 4 + 4
            def _ldqb(sync, hp=hp, qt=qt, q0=q0):
                gate(sync, {b.te: b.te.n})
                for hh in range(2):
                    lr.issue(sync.dma_start(out=qbs[hh][:, :],
                                            in_=qbn_dram[hp * 2 + hh, 0:1, q0:q0 + 512]), 1)
            b.syn(_ldqb)
            qbm = {lr.sems[1]: lr.sems[1].n}
            for kc in range(nkc):
                k0 = kc * CH
                diag_j = kc - qt * 4
                c0 = max(0, diag_j) * 128
                for hh in range(2):
                    hsl = slice(hh * 64, (hh + 1) * 64)
                    eb = E[(kc % 2) * 2 + hh]
                    def _s(te, hp=hp, qt=qt, kc=kc, hh=hh, hsl=hsl, q0=q0, k0=k0,
                           c0=c0, qm=qm, c0m=c0m, qbm=qbm):
                        if qt == 0 and kc == 0 and hh == 0:
                            gate(te, qm); gate(te, c0m); gate(te, bcolm)
                        if kc == 0 and hh == 0:
                            gate(te, qbm)
                        gate(te, {b.sc: b.sc.n})
                        te.matmul(p_s[hh][:, c0:512], kt_[hsl, k0:k0 + CH],
                                  qt_[hsl, q0 + c0:q0 + 512],
                                  start=True, stop=False,
                                  tile_position=(hh * 64, 0))
                        return te.matmul(p_s[hh][:, c0:512],
                                         ones1f[:, :],
                                         qbs[hh][:, c0:512],
                                         start=False, stop=True)
                    b.ten(_s)
                    sm_ = snap(b.te)
                    if diag_j >= 0:
                        def _mi(v, I, hh=hh, c0=c0, sm_=sm_):
                            gate(v, sm_)
                            I(v.tensor_add(p_s[hh][:, c0:c0 + 128], p_s[hh][:, c0:c0 + 128],
                                           minf[:, :]))
                        b.vec(_mi)
                        mim = {b.ve: b.ve.n, b.te: b.te.n}
                    else:
                        mim = dict(sm_)
                    hidx = hp * 2 + hh
                    def _ex(s, I, hh=hh, c0=c0, mim=mim, eb=eb, hidx=hidx, kc=kc):
                        gate(s, mim)
                        gate(s, {b.te: b.te.n})
                        I(s.activation(eb[:, c0:512], p_s[hh][:, c0:512], AF.Exp,
                                       bias=bcolmat[:, hidx * 32 + kc:hidx * 32 + kc + 1]))
                    b.scl(_ex)
                    exm = snap(b.sc)
                    def _av(te, hp=hp, kc=kc, hh=hh, k0=k0, c0=c0, exm=exm, eb=eb, nkc=nkc):
                        gate(te, exm)
                        te.matmul(p_av[hh][0:64, c0:512],
                                  v_hp[:, kc * 128 + hh * 64:kc * 128 + hh * 64 + 64],
                                  eb[:, c0:512],
                                  start=(kc == 0), stop=(kc == nkc - 1),
                                  tile_position=(0, 0))
                        return te.matmul(p_z[hh][:, c0:512], onesb[:, :], eb[:, c0:512],
                                         start=(kc == 0), stop=(kc == nkc - 1))
                    b.ten(_av)
            avm = snap(b.te)
            for hh in range(2):
                oit = (hp * NQT + qt) * 2 + hh
                def _n1(s, I, hh=hh, avm=avm):
                    gate(s, avm)
                    gate(s, {b.ve: b.ve.n})
                    I(s.activation(zrow[:, :], p_z[hh][0:1, :], AF.Copy))
                b.scl(_n1)
                n1m = snap(b.sc)
                def _n2(v, I, n1m=n1m):
                    gate(v, n1m)
                    I(v.reciprocal(rzrow[:, :], zrow[:, :]))
                    v.wait_ge(b.ve.h, b.ve.n)
                    I(v.tensor_copy(rzb[:, :], rzrow[:, :]))
                b.vec(_n2)
                n2m = snap(b.ve)
                def _n3(te, n2m=n2m):
                    gate(te, n2m)
                    return te.matmul(p_bc[:, :], ones1[:, :], rzb[:, :],
                                     start=True, stop=True)
                b.ten(_n3)
                n3m = snap(b.te)
                def _n4(s, I, hh=hh, n3m=n3m):
                    gate(s, n3m)
                    I(s.activation(avsb[0:64, :], p_av[hh][0:64, :], AF.Copy))
                b.scl(_n4)
                n4m = snap(b.sc)
                def _n5(v, I, oit=oit, n4m=n4m):
                    gate(v, n4m)
                    if oit >= 2:
                        oring.wait_slot(v, oit % 2)
                    I(v.tensor_mul(evo[oit % 2][0:64, :], avsb[0:64, :], p_bc[0:64, :]))
                b.vec(_n5)
                n5m = snap(b.ve)
                def _st(g, hp=hp, qt=qt, hh=hh, oit=oit, n5m=n5m):
                    gate(g, n5m)
                    oring.issue(g.dma_start(
                        out=out_dram[hp * 128 + hh * 64:hp * 128 + hh * 64 + 64,
                                     qt * 512:(qt + 1) * 512],
                        in_=evo[oit % 2][0:64, :]), oit % 2)
                b.gpd(_st)
        prev_hp = {b.ve: b.ve.n, b.sc: b.sc.n, b.te: b.te.n, **oring.marks()}
    return oring.marks()


def stage_moe(b, tag, xn_dram, selw_dram, selb_dram, comp_dram, compb_dram,
              out_dram, T, inmarks):
    NTC = T // 128
    NOB = 16
    xt, xm = load_fm_raw(b, f"{tag}x", xn_dram, NDC, T, BF16, wait=inmarks)
    oring = DmaRing(b, f"{tag}os", 2)
    wring = DmaRing(b, f"{tag}wl", 2)
    ws32 = [b.sb(f"{tag}w32{i}", [128, 512], F32) for i in range(2)]
    selt = [[b.sb(f"{tag}se{i}_{dc}", [128, 512], BF16) for dc in range(NDC)] for i in range(2)]
    compt = [[b.sb(f"{tag}co{i}_{dc}", [128, 512], BF16) for dc in range(NDC)] for i in range(2)]
    brow = [b.sb(f"{tag}br{i}", [1, 1024], BF16) for i in range(2)]
    brow32 = b.sb(f"{tag}br32", [1, 1024], F32)
    ones1 = b.sb(f"{tag}o1", [1, 128], BF16)
    lg = b.sb(f"{tag}lg", [128, 512], F32)
    mx = b.sb(f"{tag}mx", [128, 64], F32)
    dd = b.sb(f"{tag}dd", [128, 512], F32)
    sgn = b.sb(f"{tag}sg", [128, 512], BF16)
    csb = b.sb(f"{tag}cs", [128, 512], BF16)
    cm = b.sb(f"{tag}cm", [128, 512], BF16)
    r1 = b.sb(f"{tag}r1", [128, 64], F32)
    r2 = b.sb(f"{tag}r2", [128, 64], F32)
    gp = b.sb(f"{tag}gp", [128, 64], F32)
    s4 = b.sb(f"{tag}s4", [128, 64], F32)
    evo = [b.sb(f"{tag}ev{i}", [128, 64], BF16) for i in range(2)]
    p_lg = b.ps(f"{tag}plg", [128, 512], F32)
    p_cp = b.ps(f"{tag}pcp", [128, 512], F32)

    def _c0(g, I):
        I(g.memset(ones1[:, :], 1.0))
    b.gpc(_c0)
    c0m = snap(b.gc)

    prev_ocb = {}
    for ocb in range(NOB):
        o0 = ocb * 512
        pmo = dict(prev_ocb)
        wb = ocb % 2
        cvm = []
        for kind, dst in (("sel", selt[wb]), ("comp", compt[wb])):
            src = selw_dram if kind == "sel" else comp_dram
            for dc in range(NDC):
                sl = dc % 2
                def _lw(sync, src=src, dc=dc, o0=o0, sl=sl, pmo=pmo):
                    gate(sync, pmo)
                    gate(sync, {b.ve: b.ve.n})
                    wring.issue(sync.dma_start(out=ws32[sl][:, :],
                                               in_=src[dc * 128:(dc + 1) * 128, o0:o0 + 512]), sl)
                b.syn(_lw)
                def _cw(v, I, dst=dst, dc=dc, sl=sl):
                    wring.wait_slot(v, sl)
                    I(v.tensor_copy(dst[dc][:, :], ws32[sl][:, :]))
                b.vec(_cw)
        def _lb(sync, ocb=ocb, o0=o0, pmo=pmo):
            gate(sync, {b.ve: b.ve.n})
            wring.issue(sync.dma_start(out=brow32[0:1, 0:512], in_=selb_dram[0:1, o0:o0 + 512]), 0)
            wring.issue(sync.dma_start(out=brow32[0:1, 512:1024], in_=compb_dram[0:1, o0:o0 + 512]), 0)
        b.syn(_lb)
        def _cb(v, I, wb=wb):
            wring.wait_slot(v, 0)
            I(v.tensor_copy(brow[wb][:, :], brow32[:, :]))
        b.vec(_cb)
        wm = {b.ve: b.ve.n}

        for tc in range(NTC):
            def _mlg(te, tc=tc, wb=wb, wm=wm, c0m=c0m):
                if tc == 0:
                    gate(te, wm); gate(te, xm); gate(te, c0m)
                gate(te, {b.sc: b.sc.n})
                for dc in range(NDC):
                    te.matmul(p_lg[:, :], xt[dc][:, tc * 128:(tc + 1) * 128],
                              selt[wb][dc][:, :], start=(dc == 0), stop=False)
                te.matmul(p_lg[:, :], ones1[:, :], brow[wb][0:1, 0:512],
                          start=False, stop=True)
                for dc in range(NDC):
                    te.matmul(p_cp[:, :], xt[dc][:, tc * 128:(tc + 1) * 128],
                              compt[wb][dc][:, :], start=(dc == 0), stop=False)
                return te.matmul(p_cp[:, :], ones1[:, :], brow[wb][0:1, 512:1024],
                                 start=False, stop=True)
            b.ten(_mlg)
            tm = snap(b.te)
            def _ev(s, I, tm=tm):
                gate(s, tm)
                gate(s, {b.ve: b.ve.n})
                gate(s, {b.gc: b.gc.n})
                I(s.activation(lg[:, :], p_lg[:, :], AF.Copy))
                s.wait_ge(b.sc.h, b.sc.n)
                I(s.activation(csb[:, :], p_cp[:, :], AF.Copy))
            b.scl(_ev)
            evm = snap(b.sc)
            def _mx(v, I, evm=evm):
                gate(v, evm)
                I(v.tensor_reduce(mx[:, :], lg[:, :].rearrange("p (o n) -> p o n", n=8),
                                  AX.X, ALU.max))
            b.vec(_mx)
            mxm = snap(b.ve)
            def _sub(g, I, mxm=mxm):
                gate(g, mxm)
                mb = mx[:, :].unsqueeze(2).broadcast_to([128, 64, 8])
                I(g.tensor_tensor(dd[:, :].rearrange("p (o n) -> p o n", n=8),
                                  lg[:, :].rearrange("p (o n) -> p o n", n=8), mb,
                                  ALU.subtract))
            b.gpc(_sub)
            subm = snap(b.gc)
            def _sgn(s, I, subm=subm):
                gate(s, subm)
                I(s.sign(sgn[:, :], dd[:, :]))
            b.scl(_sgn)
            sgm = snap(b.sc)
            def _cm(g, I, sgm=sgm):
                gate(g, sgm)
                I(g.tensor_mul(cm[:, :], csb[:, :], sgn[:, :]))
            b.gpc(_cm)
            cmm = snap(b.gc)
            def _red(v, I, cmm=cmm):
                gate(v, cmm)
                I(v.tensor_reduce(r1[:, :], csb[:, :].rearrange("p (o n) -> p o n", n=8),
                                  AX.X, ALU.add))
                v.wait_ge(b.ve.h, b.ve.n)
                I(v.tensor_reduce(r2[:, :], cm[:, :].rearrange("p (o n) -> p o n", n=8),
                                  AX.X, ALU.add))
                v.wait_ge(b.ve.h, b.ve.n)
                I(v.tensor_add(gp[:, :], r1[:, :], r2[:, :]))
            b.vec(_red)
            redm = snap(b.ve)
            def _si(s, I, redm=redm):
                gate(s, redm)
                I(s.activation(s4[:, :], gp[:, :], AF.Sigmoid))
            b.scl(_si)
            sim_ = snap(b.sc)
            oit = ocb * NTC + tc
            def _g3(v, I, sim_=sim_, oit=oit):
                gate(v, sim_)
                if oit >= 2:
                    oring.wait_slot(v, oit % 2)
                I(v.tensor_mul(evo[oit % 2][:, :], gp[:, :], s4[:, :]))
            b.vec(_g3)
            g3m = snap(b.ve)
            def _st(g, tc=tc, ocb=ocb, g3m=g3m, oit=oit):
                gate(g, g3m)
                oring.issue(g.dma_start(
                    out=out_dram[tc * 128:(tc + 1) * 128, ocb * 64:(ocb + 1) * 64],
                    in_=evo[oit % 2][:, :]), oit % 2)
            b.gpd(_st)
        prev_ocb = {b.ve: b.ve.n, b.sc: b.sc.n, b.te: b.te.n, b.gc: b.gc.n,
                    **oring.marks()}
    return oring.marks()


def stage_transpose(b, tag, src, dst, T, inmarks):
    ring = DmaRing(b, f"{tag}ld", 2)
    oring = DmaRing(b, f"{tag}os", 2)
    tin = [b.sb(f"{tag}i{i}", [128, 128], BF16) for i in range(2)]
    tout = [b.sb(f"{tag}o{i}", [128, 128], BF16) for i in range(2)]
    idt = b.sb(f"{tag}id", [128, 128], BF16)
    p_t = b.ps(f"{tag}p", [128, 128], BF16)
    def _mkid(g, I):
        I(g.memset(idt[:, :], 1.0))
        g.wait_ge(b.gc.h, b.gc.n)
        I(g.affine_select(out=idt[:, :], in_=idt[:, :],
                          compare_op=ALU.is_equal, fill=0.0,
                          base=0, pattern=[[-1, 128]], channel_multiplier=1))
    b.gpc(_mkid)
    idm = snap(b.gc)
    it = 0
    vm_hist = [{}, {}]
    for tc in range(T // 128):
        for dc in range(NDC):
            sl = it % 2
            def _ld(sync, tc=tc, dc=dc, sl=sl, pm=vm_hist[sl], it=it):
                if it == 0:
                    gate(sync, inmarks)
                gate(sync, pm)
                ring.issue(sync.dma_start(out=tin[sl][:, :],
                                          in_=src[tc * 128:(tc + 1) * 128,
                                                  dc * 128:(dc + 1) * 128]), sl)
            b.syn(_ld)
            lm = {ring.sems[sl]: ring.sems[sl].n}
            def _tr(te, sl=sl, lm=lm, idm=idm):
                gate(te, lm)
                gate(te, idm)
                gate(te, {b.ve: b.ve.n})
                return te.transpose(p_t[:, :], tin[sl][:, :], idt[:, :])
            b.ten(_tr)
            tm = snap(b.te)
            def _cv(v, I, sl=sl, tm=tm, it=it):
                gate(v, tm)
                if it >= 2:
                    oring.wait_slot(v, sl)
                I(v.tensor_copy(tout[sl][:, :], p_t[:, :]))
            b.vec(_cv)
            vm = snap(b.ve)
            vm_hist[sl] = vm
            def _st(g, tc=tc, dc=dc, sl=sl, vm=vm):
                gate(g, vm)
                oring.issue(g.dma_start(out=dst[dc * 128:(dc + 1) * 128,
                                                tc * 128:(tc + 1) * 128],
                                        in_=tout[sl][:, :]), sl)
            b.gpd(_st)
            it += 1
    return oring.marks()



T_FULL = 4096
N_CORES = 8


def build_program(T):
    hc = host_consts(T)
    b = B()
    nc = b.nc
    D = {}

    def inp(name, shape, dt=F32):
        D[name] = nc.dram_tensor(name, list(shape), dt, kind="ExternalInput").ap()

    inp("x_fm", [C, T])
    for nm in ["x", "w", "k", "v", "r", "g"]:
        inp(f"maa_{nm}", [C, 1])
    inp("w1", [C, 160]); inp("w2f", [160, C]); inp("td", [1, C])
    inp("td_w1", [C, 64]); inp("td_w2", [64, C]); inp("u", [C, 1])
    for wn in ["W_r", "W_k", "W_v", "W_g", "W_o"]:
        inp(wn, [C, C])
    inp("gamma", [C, 1]); inp("beta", [C, 1])
    for wn in ["Wq", "Wk", "Wv", "Wo", "Wff"]:
        inp("f0" + wn, [C, C])
    for wn in ["Wq", "Wk", "Wv", "Wo"]:
        inp("a1" + wn, [C, C])
    inp("selw", [C, 8192]); inp("selb", [1, 8192])
    inp("compw", [C, 8192]); inp("compb", [1, 8192])
    for cn in ["c_ones128", "c_iota2", "c_slopes2", "c_qbn", "c_minf"]:
        inp(cn, hc[cn].shape)
    for cn in ["c_rfeat", "c_rtok", "c_ident", "c_swkv", "c_bd2", "c_selh"]:
        D[cn] = nc.dram_tensor(cn, list(hc[cn].shape), BF16, kind="ExternalInput").ap()

    def scr(name, shape, dt=BF16):
        D[name] = nc.dram_tensor(name, shape, dt, kind="Internal").ap()
    for n in ["xn0", "r_fm", "k_fm", "g_fm", "q0", "k0", "aout", "z1",
              "yng", "s0", "g0", "xn1", "q1", "k1", "att1", "s1", "g1_fm"]:
        scr(n, [C, T])
    for n in ["v_tok", "k_tok", "ew_tok", "v0_tok", "v1_tok", "g1_tok"]:
        scr(n, [T, C])
    scr("x1_fm", [C, T], F32)
    D["out_fm"] = nc.dram_tensor("out_fm", [C, T], F32, kind="ExternalOutput").ap()

    ones_t, ones_m = load_fm_raw(b, "ones", D["c_ones128"], 1, 128, F32)
    ctiles = {}
    cmarks = {}
    for cn, key in [("c_rtok", "rtok"), ("c_rfeat", "rfeat"), ("c_ident", "ident"),
                    ("c_swkv", "swkv"), ("c_bd2", "bd2"), ("c_selh", "selh")]:
        tl, m = load_fm_raw(b, "C" + key, D[cn], 1, hc[cn].shape[1], BF16)
        ctiles[key] = tl[0] if hc[cn].shape[0] >= 128 else tl[0][0:hc[cn].shape[0], :]
        cmarks.update(m)

    b.barrier()
    with b.scope():
        m_ln0 = stage_ln(b, "ln0", D["x_fm"], D["xn0"], T, ones_t[0], ones_m)
    b.barrier()
    mixP = {f"maa_{nm}": D[f"maa_{nm}"] for nm in ["x", "w", "k", "v", "r", "g"]}
    mixP.update({"w1": D["w1"], "w2f": D["w2f"], "td": D["td"],
                 "td_w1": D["td_w1"], "td_w2": D["td_w2"],
                 "W_r": D["W_r"], "W_k": D["W_k"], "W_v": D["W_v"], "W_g": D["W_g"]})
    with b.scope():
        m_mix = stage_l0mix(b, "mx", D["xn0"], mixP,
                            {"r_fm": D["r_fm"], "k_fm": D["k_fm"], "g_fm": D["g_fm"],
                             "v_tok": D["v_tok"], "k_tok": D["k_tok"],
                             "ew_tok": D["ew_tok"]}, T, m_ln0)
    b.barrier()
    with b.scope():
        m_wkv = stage_wkv(b, "wk",
                          {"r_fm": D["r_fm"], "k_fm": D["k_fm"], "g_fm": D["g_fm"],
                           "k_tok": D["k_tok"], "v_tok": D["v_tok"],
                           "ew_tok": D["ew_tok"], "u": D["u"],
                           "gamma": D["gamma"], "beta": D["beta"]},
                          ctiles, {"yng_fm": D["yng"]}, T, {**m_mix, **cmarks})
    b.barrier()
    with b.scope():
        m_s0 = simple_proj(b, "wo0", D["W_o"], D["yng"], D["s0"], T, xmarks=m_wkv)
    b.barrier()
    with b.scope():
        m_q0 = simple_proj(b, "fq0", D["f0Wq"], D["xn0"], D["q0"], T,
                           scale=1.0 / 32.0, xmarks=m_ln0)
    b.barrier()
    with b.scope():
        m_k0 = simple_proj(b, "fk0", D["f0Wk"], D["xn0"], D["k0"], T, xmarks=m_ln0)
    b.barrier()
    with b.scope():
        xt0, xm0 = load_fm_raw(b, "fx0", D["xn0"], NDC, T, BF16, wait=m_ln0)
        wv0, wm0 = load_cast_fm(b, "fv0w", D["f0Wv"], NDC, C)
        wr_v0, ring_v0 = tok_store_writer(b, "fv0", D["v0_tok"])
        proj_tok(b, "fv0p", xt0, xm0, wv0, wm0, T, wr_v0)
        m_v0 = ring_v0.marks()
    b.barrier()
    with b.scope():
        m_fa = stage_l0fattn(b, "fa", D["q0"], D["k0"], D["v0_tok"], D["aout"], T,
                             {**m_q0, **m_k0, **m_v0})
    b.barrier()
    with b.scope():
        m_z1 = simple_proj(b, "fo0", D["f0Wo"], D["aout"], D["z1"], T, xmarks=m_fa)
    b.barrier()
    with b.scope():
        m_g0 = stage_ffntail(b, "ft", D["z1"], D["f0Wff"], D["g0"], T,
                             ones_t[0], ones_m, m_z1)
    b.barrier()
    with b.scope():
        m_x1 = stage_combine(b, "cb0", D["x_fm"], D["s0"], D["g0"], D["x1_fm"], T,
                             {**m_s0, **m_g0})
    b.barrier()
    with b.scope():
        m_ln1 = stage_ln(b, "ln1", D["x1_fm"], D["xn1"], T, ones_t[0],
                         {**ones_m, **m_x1})
    b.barrier()
    with b.scope():
        m_q1 = simple_proj(b, "aq1", D["a1Wq"], D["xn1"], D["q1"], T,
                           scale=1.0 / 8.0, xmarks=m_ln1)
    b.barrier()
    with b.scope():
        m_k1 = simple_proj(b, "ak1", D["a1Wk"], D["xn1"], D["k1"], T, xmarks=m_ln1)
    b.barrier()
    with b.scope():
        xt1, xm1 = load_fm_raw(b, "ax1", D["xn1"], NDC, T, BF16, wait=m_ln1)
        wv1, wm1 = load_cast_fm(b, "av1w", D["a1Wv"], NDC, C)
        wr_v1, ring_v1 = tok_store_writer(b, "av1", D["v1_tok"])
        proj_tok(b, "av1p", xt1, xm1, wv1, wm1, T, wr_v1)
        m_v1 = ring_v1.marks()
    b.barrier()
    with b.scope():
        m_at = stage_l1attn(b, "at", D["q1"], D["k1"], D["v1_tok"],
                            D["c_iota2"], D["c_slopes2"], D["c_qbn"],
                            D["c_minf"], D["att1"], T,
                            {**m_q1, **m_k1, **m_v1})
    b.barrier()
    with b.scope():
        m_s1 = simple_proj(b, "wo1", D["a1Wo"], D["att1"], D["s1"], T, xmarks=m_at)
    b.barrier()
    with b.scope():
        m_moe = stage_moe(b, "mo", D["xn1"], D["selw"], D["selb"], D["compw"],
                          D["compb"], D["g1_tok"], T, m_ln1)
    b.barrier()
    with b.scope():
        m_g1t = stage_transpose(b, "tg1", D["g1_tok"], D["g1_fm"], T, m_moe)
    b.barrier()
    with b.scope():
        m_out = stage_combine(b, "cb1", D["x1_fm"], D["s1"], D["g1_fm"], D["out_fm"],
                              T, {**m_s1, **m_g1t, **m_x1})

    def _fin(g):
        gate(g, m_out)
    b.gpd(_fin)
    b.close()
    return nc, hc


def kernel(**inputs):
    from concourse.bass_utils import run_bass_kernel_spmd
    T = T_FULL
    nc, hc = build_program(T)
    x = np.asarray(inputs["L0_attn_x"], dtype=np.float32)
    shared = {}
    shared["w1"] = np.ascontiguousarray(inputs["L0_attn_time_maa_w1"], dtype=np.float32)
    shared["w2f"] = np.ascontiguousarray(
        np.asarray(inputs["L0_attn_time_maa_w2"], dtype=np.float32).reshape(160, C))
    shared["td"] = np.ascontiguousarray(
        np.asarray(inputs["L0_attn_time_decay"], dtype=np.float32).reshape(1, C))
    shared["td_w1"] = np.ascontiguousarray(inputs["L0_attn_time_decay_w1"], dtype=np.float32)
    shared["td_w2"] = np.ascontiguousarray(inputs["L0_attn_time_decay_w2"], dtype=np.float32)
    shared["u"] = np.ascontiguousarray(
        np.asarray(inputs["L0_attn_time_faaaa"], dtype=np.float32).reshape(C, 1))
    for nm in ["x", "w", "k", "v", "r", "g"]:
        shared[f"maa_{nm}"] = np.ascontiguousarray(
            np.asarray(inputs[f"L0_attn_time_maa_{nm}"], dtype=np.float32).reshape(C, 1))
    for wn in ["W_r", "W_k", "W_v", "W_g", "W_o"]:
        shared[wn] = np.ascontiguousarray(inputs["L0_attn_" + wn], dtype=np.float32)
    shared["gamma"] = np.ascontiguousarray(
        np.asarray(inputs["L0_attn_ln_gamma"], dtype=np.float32).reshape(C, 1))
    shared["beta"] = np.ascontiguousarray(
        np.asarray(inputs["L0_attn_ln_beta"], dtype=np.float32).reshape(C, 1))
    for wn in ["Wq", "Wk", "Wv", "Wo", "Wff"]:
        shared["f0" + wn] = np.ascontiguousarray(inputs["L0_ffn_" + wn], dtype=np.float32)
    for wn in ["Wq", "Wk", "Wv", "Wo"]:
        shared["a1" + wn] = np.ascontiguousarray(
            np.asarray(inputs["L1_attn_" + wn], dtype=np.float32).T)
    shared["selw"] = np.ascontiguousarray(
        np.asarray(inputs["L1_ffn_sel_w"], dtype=np.float32).T)
    shared["selb"] = np.ascontiguousarray(
        np.asarray(inputs["L1_ffn_sel_b"], dtype=np.float32).reshape(1, 8192))
    shared["compw"] = np.ascontiguousarray(
        np.asarray(inputs["L1_ffn_comp_w"], dtype=np.float32).transpose(2, 1, 0).reshape(C, 8192))
    shared["compb"] = np.ascontiguousarray(
        np.asarray(inputs["L1_ffn_comp_b"], dtype=np.float32).T.reshape(1, 8192))
    for cn, arr in hc.items():
        shared[cn] = np.ascontiguousarray(arr)
    in_maps = []
    for core in range(N_CORES):
        m = dict(shared)
        m["x_fm"] = np.ascontiguousarray(x[core].T)
        in_maps.append(m)
    res = run_bass_kernel_spmd(nc, in_maps, core_ids=list(range(N_CORES)))
    out = np.stack([np.ascontiguousarray(np.asarray(r["out_fm"],
                                                    dtype=np.float32).reshape(C, T).T)
                    for r in res.results], axis=0)
    return out


# version 03
# speedup vs baseline: 1.0253x; 1.0253x over previous
import sys
sys.path.insert(0, "/opt/trn_rl_repo")
import numpy as np
from contextlib import ExitStack
import concourse.bass as bass
import concourse.mybir as mybir

F32 = mybir.dt.float32
BF16 = mybir.dt.bfloat16
AF = mybir.ActivationFunctionType
ALU = mybir.AluOpType
AX = mybir.AxisListType

C = 1024
NDC = 8
H = 16
HK = 64
CH = 128


class Sem:
    __slots__ = ("h", "n")
    def __init__(self, h):
        self.h = h
        self.n = 0
    def inc(self, instr, k=1):
        instr.then_inc(self.h, k)
        self.n += k
        return self.n


class B:
    def __init__(self, nc=None):
        self.nc = nc or bass.Bass("TRN2", target_bir_lowering=False)
        self.ctx = ExitStack()
        self.ctx.__enter__()
        self.block = self.ctx.enter_context(self.nc.Block())
        self._sems = {}
        self.ve = self.sem("Ove")
        self.sc = self.sem("Osc")
        self.gc = self.sem("Ogc")
        self.te = self.sem("Ote")
        self.wstage()

    def sem(self, name):
        if name not in self._sems:
            self._sems[name] = Sem(self.ctx.enter_context(self.nc.semaphore(name)))
        return self._sems[name]

    def sb(self, name, shape, dt):
        stk = self._scopes[-1] if getattr(self, "_scopes", None) else self.ctx
        return stk.enter_context(self.nc.sbuf_tensor(name, shape, dt))

    def scope(self):
        from contextlib import contextmanager
        if not hasattr(self, "_scopes"):
            self._scopes = []
        @contextmanager
        def _cm():
            stk = ExitStack()
            self._scopes.append(stk)
            try:
                yield
            finally:
                self._scopes.pop()
                stk.close()
        return _cm()

    def barrier(self):
        marks = {s: s.n for s in self._sems.values() if s.n > 0}
        def _w(eng):
            gate(eng, marks)
        for engname in ["sync", "vector", "scalar", "tensor", "gpsimd"]:
            getattr(self.block, engname)(_w)
        return marks

    def wstage(self):
        if not hasattr(self, "_wstage"):
            bufs = [self.ctx.enter_context(self.nc.sbuf_tensor(f"WST{i}", [128, 1024], F32))
                    for i in range(2)]
            ring = DmaRing(self, "WSTr", 2)
            self._wstage = (bufs, ring, [0, 0])
        return self._wstage

    def ps(self, name, shape, dt=F32):
        stk = self._scopes[-1] if getattr(self, "_scopes", None) else self.ctx
        return stk.enter_context(self.nc.psum_tensor(name, shape, dt))

    def _sect(self, engname, osem, f):
        mark = osem.n
        def wrapped(eng):
            if mark > 0:
                eng.wait_ge(osem.h, mark)
            f(eng)
        getattr(self.block, engname)(wrapped)

    def vec(self, f):
        self._sect("vector", self.ve, lambda e: f(e, lambda i, k=1: self.ve.inc(i, k)))
    def scl(self, f):
        self._sect("scalar", self.sc, lambda e: f(e, lambda i, k=1: self.sc.inc(i, k)))
    def gpc(self, f):
        self._sect("gpsimd", self.gc, lambda e: f(e, lambda i, k=1: self.gc.inc(i, k)))
    def ten(self, f):
        def wrapped(eng):
            last = f(eng)
            assert last is not None, "ten section must return last matmul"
            self.te.inc(last, 1)
        self.block.tensor(wrapped)
    def syn(self, f):
        self.block.sync(f)
    def gpd(self, f):
        self.block.gpsimd(f)

    def close(self):
        self.ctx.__exit__(None, None, None)


def gate(eng, marks):
    for s, v in marks.items():
        if v > 0:
            eng.wait_ge(s.h, v)


def snap(*sems):
    return {s: s.n for s in sems}


class DmaRing:
    def __init__(self, b, tag, depth=2):
        self.sems = [b.sem(f"{tag}r{i}") for i in range(depth)]
        self.depth = depth
    def issue(self, instr, slot):
        return self.sems[slot % self.depth].inc(instr, 16)
    def wait_slot(self, eng, slot):
        s = self.sems[slot % self.depth]
        if s.n > 0:
            eng.wait_ge(s.h, s.n)
    def marks(self):
        return {s: s.n for s in self.sems}



def load_cast_fm(b, tag, dram, nrc, ncols):
    stage, ring, cvm = b.wstage()
    assert ncols <= 1024
    tiles = [b.sb(f"{tag}b{i}", [128, ncols], BF16) for i in range(nrc)]
    for i in range(nrc):
        slot = ring.sems[0].n + ring.sems[1].n
        sl = (slot // 16) % 2
        def _ld(sync, i=i, sl=sl, pm=cvm[sl]):
            if pm > 0:
                sync.wait_ge(b.ve.h, pm)
            ring.issue(sync.dma_start(out=stage[sl][:, 0:ncols],
                                      in_=dram[i * 128:(i + 1) * 128, 0:ncols]), sl)
        b.syn(_ld)
        def _cv(v, I, i=i, sl=sl):
            ring.wait_slot(v, sl)
            I(v.tensor_copy(tiles[i][:, :], stage[sl][:, 0:ncols]))
        b.vec(_cv)
        cvm[sl] = b.ve.n
    return tiles, snap(b.ve)


def load_fm_raw(b, tag, dram, nrc, ncols, dt, wait=None):
    ld = b.sem(f"{tag}ld")
    nrows_tot = dram.shape[0]
    tiles = [b.sb(f"{tag}t{i}", [128, ncols], dt) for i in range(nrc)]
    def _ld(sync):
        if wait:
            gate(sync, wait)
        for i in range(nrc):
            r = min(128, nrows_tot - i * 128)
            ld.inc(sync.dma_start(out=tiles[i][0:r, :],
                                  in_=dram[i * 128:i * 128 + r, 0:ncols]), 16)
    b.syn(_ld)
    return tiles, snap(ld)


def load_col(b, tag, dram_vec, nrc):
    ld = b.sem(f"{tag}ld")
    tile = b.sb(f"{tag}c", [128, nrc], F32)
    def _ld(sync):
        for i in range(nrc):
            ld.inc(sync.dma_start(out=tile[:, i:i + 1],
                                  in_=dram_vec[i * 128:(i + 1) * 128, 0:1]), 16)
    b.syn(_ld)
    return tile, snap(ld)



def proj_fm(b, tag, wtiles, wmarks, xtiles, xmarks, T, writer,
            cout=C, tb=512, nin=NDC, xoff=0):
    noc = cout // 128
    ntb = T // tb
    pp = [b.ps(f"{tag}p{i}", [128, tb], F32) for i in range(2)]
    wr_marks = [{}, {}]
    it = 0
    for oc in range(noc):
        for t in range(ntb):
            pbuf = it % 2
            pm = dict(wr_marks[pbuf])
            def _mm(te, oc=oc, t=t, pbuf=pbuf, it=it, pm=pm):
                if it == 0:
                    gate(te, wmarks)
                    gate(te, xmarks)
                gate(te, pm)
                last = None
                for dc in range(nin):
                    last = te.matmul(pp[pbuf][:, :],
                                     wtiles[dc][:, oc * 128:(oc + 1) * 128],
                                     xtiles[dc][:, xoff + t * tb:xoff + (t + 1) * tb],
                                     start=(dc == 0), stop=(dc == nin - 1))
                return last
            b.ten(_mm)
            wr_marks[pbuf] = writer(it, oc, t * tb, tb, pp[pbuf], snap(b.te))
            it += 1


def fm_store_writer(b, tag, out_dram, act=None, scale=None, out_dt=BF16,
                    gate_tiles=None, gmarks=None):
    sring = DmaRing(b, f"{tag}st", 2)
    ev = [b.sb(f"{tag}e{i}", [128, 512], out_dt) for i in range(2)]
    evg = [b.sb(f"{tag}g{i}", [128, 512], F32) for i in range(2)] if gate_tiles is not None else None

    def writer(it, oc, t0, tw, psum, mmarks):
        ebuf = it % 2
        def _ev(s, I, it=it, ebuf=ebuf, psum=psum, mmarks=mmarks):
            gate(s, mmarks)
            if it >= 2:
                sring.wait_slot(s, ebuf)
            dst = evg[ebuf] if gate_tiles is not None else ev[ebuf]
            I(s.activation(dst[:, 0:tw], psum[:, 0:tw],
                           act if act is not None else AF.Copy,
                           scale=(scale if scale is not None else 1.0)))
        b.scl(_ev)
        em = snap(b.sc)
        if gate_tiles is not None:
            def _ev2(v, I, it=it, oc=oc, t0=t0, tw=tw, ebuf=ebuf, em=em):
                gate(v, em)
                if it == 0 and gmarks:
                    gate(v, gmarks)
                if it >= 2:
                    sring.wait_slot(v, ebuf)
                I(v.tensor_mul(ev[ebuf][:, 0:tw], evg[ebuf][:, 0:tw],
                               gate_tiles[oc][:, t0:t0 + tw]))
            b.vec(_ev2)
            evm = snap(b.ve)
        else:
            evm = em
        def _st(g, oc=oc, t0=t0, tw=tw, ebuf=ebuf, evm=evm):
            gate(g, evm)
            sring.issue(g.dma_start(out=out_dram[oc * 128:(oc + 1) * 128, t0:t0 + tw],
                                    in_=ev[ebuf][:, 0:tw]), ebuf)
        b.gpd(_st)
        return dict(em)
    return writer, sring


def simple_proj(b, tag, w_dram, x_dram_or_tiles, out_dram, T, act=None,
                scale=None, out_dt=BF16, xmarks=None, gate_tiles=None,
                gmarks=None, cin=C, cout=C):
    wtiles, wmarks = load_cast_fm(b, f"{tag}w", w_dram, cin // 128, cout)
    if isinstance(x_dram_or_tiles, tuple):
        xtiles, xm = x_dram_or_tiles
        if xmarks:
            xm = dict(xm); xm.update(xmarks)
    else:
        xtiles, xm = load_fm_raw(b, f"{tag}x", x_dram_or_tiles, cin // 128, T, BF16,
                                 wait=xmarks)
    writer, sring = fm_store_writer(b, tag, out_dram, act=act, scale=scale,
                                     out_dt=out_dt, gate_tiles=gate_tiles, gmarks=gmarks)
    proj_fm(b, tag, wtiles, wmarks, xtiles, xm, T, writer, cout=cout, nin=cin // 128)
    return sring.marks()



def stage_ln(b, tag, x_dram, out_dram, T, ones_f32, omarks, eps=1e-5):
    tb = 512
    ntb = T // tb
    ld = b.sem(f"{tag}ld")
    sring = DmaRing(b, f"{tag}st", 2)
    epst = b.sb(f"{tag}ep", [128, 1], F32)
    def _eps(g, I):
        I(g.memset(epst[:, :], eps))
    b.gpc(_eps)
    epsm = snap(b.gc)
    xb = [[b.sb(f"{tag}x{i}_{dc}", [128, tb], F32) for dc in range(NDC)] for i in range(2)]
    xq = [b.sb(f"{tag}q{i}", [128, tb], F32) for i in range(2)]
    p1 = [b.ps(f"{tag}p1{i}", [128, tb], F32) for i in range(2)]
    p2 = [b.ps(f"{tag}p2{i}", [128, tb], F32) for i in range(2)]
    t_mu = [b.sb(f"{tag}mu{i}", [128, tb], F32) for i in range(2)]
    t_m2 = b.sb(f"{tag}m2", [128, tb], F32)
    t_va = b.sb(f"{tag}va", [128, tb], F32)
    t_rs = [b.sb(f"{tag}rs{i}", [128, tb], F32) for i in range(2)]
    t_bm = [b.sb(f"{tag}bm{i}", [128, tb], F32) for i in range(2)]
    t_t1 = b.sb(f"{tag}t1", [128, tb], F32)
    xn = [b.sb(f"{tag}o{i}", [128, tb], BF16) for i in range(2)]
    prev = [{}, {}]

    for t in range(ntb):
        buf = t % 2
        pm = dict(prev[buf])
        def _ld(sync, t=t, buf=buf, pm=pm):
            if t == 0:
                gate(sync, omarks)
            gate(sync, pm)
            for dc in range(NDC):
                ld.inc(sync.dma_start(out=xb[buf][dc][:, :],
                                      in_=x_dram[dc * 128:(dc + 1) * 128, t * tb:(t + 1) * tb]), 16)
        b.syn(_ld)
        ldm = ld.n

        temarks = []
        for dc in range(NDC):
            qb = dc % 2
            def _sq(s, I, t=t, buf=buf, dc=dc, qb=qb, ldm=ldm, pm=pm):
                s.wait_ge(ld.h, ldm)
                gate(s, pm)
                if dc >= 2:
                    s.wait_ge(b.te.h, temarks[dc - 2])
                I(s.activation(xq[qb][:, :], xb[buf][dc][:, :], AF.Square))
            b.scl(_sq)
            sqm = snap(b.sc)
            def _mm(te, t=t, buf=buf, dc=dc, qb=qb, sqm=sqm, pm=pm):
                gate(te, sqm)
                if dc == 0:
                    gate(te, pm)
                te.matmul(p1[buf][:, :], ones_f32[:, :], xb[buf][dc][:, :],
                          start=(dc == 0), stop=(dc == NDC - 1))
                return te.matmul(p2[buf][:, :], ones_f32[:, :], xq[qb][:, :],
                                 start=(dc == 0), stop=(dc == NDC - 1))
            b.ten(_mm)
            temarks.append(b.te.n)
        mmark = snap(b.te)

        def _v1(v, I, buf=buf, mmark=mmark, pm=pm):
            gate(v, mmark)
            gate(v, pm)
            I(v.tensor_scalar_mul(t_mu[buf][:, :], p1[buf][:, :], 1.0 / C))
        b.vec(_v1)
        vm1 = snap(b.ve)
        def _s1(s, I, buf=buf, vm1=vm1):
            gate(s, vm1)
            I(s.activation(t_m2[:, :], t_mu[buf][:, :], AF.Square))
        b.scl(_s1)
        sm1 = snap(b.sc)
        def _v2(v, I, buf=buf, sm1=sm1):
            gate(v, sm1)
            I(v.scalar_tensor_tensor(t_va[:, :], p2[buf][:, :], 1.0 / C, t_m2[:, :],
                                     ALU.mult, ALU.subtract))
        b.vec(_v2)
        vm2 = snap(b.ve)
        def _s2(s, I, buf=buf, vm2=vm2):
            gate(s, vm2)
            gate(s, epsm)
            I(s.activation(t_va[:, :], t_va[:, :], AF.Ln, bias=epst[:, 0:1]))
            s.wait_ge(b.sc.h, b.sc.n)
            I(s.activation(t_rs[buf][:, :], t_va[:, :], AF.Exp, scale=-0.5))
        b.scl(_s2)
        sm2 = snap(b.sc)
        def _v3(v, I, buf=buf, sm2=sm2):
            gate(v, sm2)
            I(v.tensor_mul(t_bm[buf][:, :], t_mu[buf][:, :], t_rs[buf][:, :]))
        b.vec(_v3)
        vm3 = snap(b.ve)

        for dc in range(NDC):
            ob = dc % 2
            def _vx(v, I, t=t, buf=buf, dc=dc, ob=ob, vm3=vm3):
                gate(v, vm3)
                if t > 0 or dc >= 2:
                    sring.wait_slot(v, ob)
                I(v.tensor_mul(t_t1[:, :], xb[buf][dc][:, :], t_rs[buf][:, :]))
                v.wait_ge(b.ve.h, b.ve.n)
                I(v.tensor_sub(xn[ob][:, :], t_t1[:, :], t_bm[buf][:, :]))
            b.vec(_vx)
            vxm = snap(b.ve)
            def _st(g, t=t, dc=dc, ob=ob, vxm=vxm):
                gate(g, vxm)
                sring.issue(g.dma_start(out=out_dram[dc * 128:(dc + 1) * 128, t * tb:(t + 1) * tb],
                                        in_=xn[ob][:, :]), ob)
            b.gpd(_st)
        pm2 = {b.ve: b.ve.n, b.sc: b.sc.n, b.te: b.te.n}
        pm2.update(sring.marks())
        prev[buf] = pm2
    return sring.marks()



def proj_tok(b, tag, stat_tiles, statmarks, w_tiles, wmarks, T, writer,
             cout=C, nin=NDC, statoff=0, kdim=128):
    ntc = T // 128
    nob = cout // 512
    pp = [b.ps(f"{tag}p{i}", [128, 512], F32) for i in range(2)]
    wr_marks = [{}, {}]
    it = 0
    for tc in range(ntc):
        for ob in range(nob):
            pbuf = it % 2
            pm = dict(wr_marks[pbuf])
            def _mm(te, tc=tc, ob=ob, pbuf=pbuf, it=it, pm=pm):
                if it == 0:
                    gate(te, statmarks)
                    gate(te, wmarks)
                gate(te, pm)
                last = None
                for dc in range(nin):
                    last = te.matmul(pp[pbuf][:, :],
                                     stat_tiles[dc][:, statoff + tc * 128:statoff + (tc + 1) * 128],
                                     w_tiles[dc][:, ob * 512:(ob + 1) * 512],
                                     start=(dc == 0), stop=(dc == nin - 1))
                return last
            b.ten(_mm)
            wr_marks[pbuf] = writer(it, tc, ob * 512, 512, pp[pbuf], snap(b.te))
            it += 1


def tok_store_writer(b, tag, out_dram, act=None, scale=None, out_dt=BF16):
    sring = DmaRing(b, f"{tag}st", 2)
    ev = [b.sb(f"{tag}e{i}", [128, 512], out_dt) for i in range(2)]

    def writer(it, tc, o0, ow, psum, mmarks):
        ebuf = it % 2
        def _ev(s, I, it=it, ebuf=ebuf, psum=psum, mmarks=mmarks):
            gate(s, mmarks)
            if it >= 2:
                sring.wait_slot(s, ebuf)
            I(s.activation(ev[ebuf][:, 0:ow], psum[:, 0:ow],
                           act if act is not None else AF.Copy,
                           scale=(scale if scale is not None else 1.0)))
        b.scl(_ev)
        em = snap(b.sc)
        def _st(g, tc=tc, o0=o0, ow=ow, ebuf=ebuf, em=em):
            gate(g, em)
            sring.issue(g.dma_start(out=out_dram[tc * 128:(tc + 1) * 128, o0:o0 + ow],
                                    in_=ev[ebuf][:, 0:ow]), ebuf)
        b.gpd(_st)
        return dict(em)
    return writer, sring



def load_cast_rows(b, tag, dram, row_ranges, ncols):
    stage, ring, cvm = b.wstage()
    assert ncols <= 1024
    tiles = []
    for i, (r0, nr) in enumerate(row_ranges):
        tiles.append(b.sb(f"{tag}b{i}", [nr, ncols], BF16))
        slot = ring.sems[0].n + ring.sems[1].n
        sl = (slot // 16) % 2
        def _ld(sync, i=i, r0=r0, nr=nr, sl=sl, pm=cvm[sl]):
            if pm > 0:
                sync.wait_ge(b.ve.h, pm)
            ring.issue(sync.dma_start(out=stage[sl][0:nr, 0:ncols],
                                      in_=dram[r0:r0 + nr, 0:ncols]), sl)
        b.syn(_ld)
        def _cv(v, I, i=i, nr=nr, sl=sl):
            ring.wait_slot(v, sl)
            I(v.tensor_copy(tiles[i][:, :], stage[sl][0:nr, 0:ncols]))
        b.vec(_cv)
        cvm[sl] = b.ve.n
    return tiles, snap(b.ve)



def stage_l0mix(b, tag, xn_dram, P, outs, T, xnmarks):
    tb = 512
    ntb = T // tb
    ntc = tb // 128
    names = ["w", "k", "v", "r", "g"]

    maa_x, mx_m = load_col(b, f"{tag}mx", P["maa_x"], NDC)
    maa = {}
    maam = {}
    for nm in names:
        maa[nm], maam[nm] = load_col(b, f"{tag}m{nm}", P[f"maa_{nm}"], NDC)
    w1t, w1m = load_cast_fm(b, f"{tag}w1", P["w1"], NDC, 160)
    w2t, w2m = load_cast_rows(b, f"{tag}w2", P["w2f"],
                              [(f * 32, 32) for f in range(5)], C)
    td1t, td1m = load_cast_fm(b, f"{tag}td1", P["td_w1"], NDC, 64)
    td2t, td2m = load_cast_rows(b, f"{tag}td2", P["td_w2"], [(0, 64)], C)
    Wt, Wm = {}, {}
    for wn in ["r", "k", "v", "g"]:
        Wt[wn], Wm[wn] = load_cast_fm(b, f"{tag}W{wn}", P[f"W_{wn}"], NDC, C)

    xr = DmaRing(b, f"{tag}xl", 2)
    xnb_t = [[b.sb(f"{tag}xn{i}_{dc}", [128, tb + 1], BF16) for dc in range(NDC)]
             for i in range(2)]
    xnm = dict(xnmarks)

    onesr = b.sb(f"{tag}o1", [1, 128], BF16)
    tdrow, tdrm = load_fm_raw(b, f"{tag}tdr", P["td"], 1, C, F32)
    tdrow_bf = b.sb(f"{tag}tdbf", [1, C], BF16)
    td_bc = b.sb(f"{tag}tdb", [128, C], F32)
    p_bc = b.ps(f"{tag}pbc", [128, 512], F32)
    def _o2(g, I):
        I(g.memset(onesr[:, :], 1.0))
    b.gpc(_o2)
    o2m = snap(b.gc)
    def _cvtd(v, I):
        gate(v, tdrm)
        gate(v, o2m)
        I(v.tensor_copy(tdrow_bf[:, :], tdrow[0][0:1, :]))
    b.vec(_cvtd)
    for half in range(2):
        cvtdm = snap(b.ve)
        def _bc(te, half=half, cvtdm=cvtdm):
            gate(te, cvtdm)
            return te.matmul(p_bc[:, :], onesr[:, :],
                             tdrow_bf[:, half * 512:(half + 1) * 512],
                             start=True, stop=True)
        b.ten(_bc)
        tem = snap(b.te)
        def _ev(v, I, half=half, tem=tem):
            gate(v, tem)
            I(v.tensor_copy(td_bc[:, half * 512:(half + 1) * 512], p_bc[:, :]))
        b.vec(_ev)
    tdbcm = snap(b.ve)

    xxb = [b.sb(f"{tag}xx{dc}", [128, tb], BF16) for dc in range(NDC)]
    xxx = b.sb(f"{tag}xxx", [128, tb], BF16)
    mixed = {nm: [b.sb(f"{tag}mi{nm}{dc}", [128, tb], BF16) for dc in range(NDC)]
             for nm in names}
    p_t1 = b.ps(f"{tag}pt1", [96, tb], F32)
    p_t2 = b.ps(f"{tag}pt2", [64, tb], F32)
    Tf = [b.sb(f"{tag}Tf{f}", [32, tb], BF16) for f in range(5)]
    p_m = [b.ps(f"{tag}pm{i}", [128, tb], F32) for i in range(2)]
    t_mf = b.sb(f"{tag}tmf", [128, tb], F32)
    t_mf2 = b.sb(f"{tag}tm2", [128, tb], F32)
    h_td = b.sb(f"{tag}htd", [64, T], BF16)
    p_h = b.ps(f"{tag}ph", [64, tb], F32)
    p_o = [b.ps(f"{tag}po{i}", [128, tb], F32) for i in range(2)]
    evo = [b.sb(f"{tag}ev{i}", [128, 512], BF16) for i in range(2)]
    ewt = b.sb(f"{tag}ewt", [128, 512], F32)
    oring = DmaRing(b, f"{tag}os", 2)
    prev_tb = {}

    for t in range(ntb):
        pm = dict(prev_tb)
        buf = t % 2
        xn = xnb_t[buf]
        def _ldx(sync, t=t, buf=buf, pm=pm):
            gate(sync, xnm)
            gate(sync, pm)
            for dc in range(NDC):
                if t == 0:
                    xr.issue(sync.dma_start(out=xnb_t[buf][dc][:, 1:tb + 1],
                                            in_=xn_dram[dc * 128:(dc + 1) * 128, 0:tb]), buf)
                else:
                    xr.issue(sync.dma_start(out=xnb_t[buf][dc][:, 0:tb + 1],
                                            in_=xn_dram[dc * 128:(dc + 1) * 128,
                                                        t * tb - 1:(t + 1) * tb]), buf)
        b.syn(_ldx)
        xlm = {xr.sems[buf]: xr.sems[buf].n}
        if t == 0:
            def _z0(v, I, buf=buf):
                for dc in range(NDC):
                    I(v.memset(xnb_t[buf][dc][:, 0:1], 0.0))
            b.vec(_z0)
        for dc in range(NDC):
            def _vx(v, I, t=t, dc=dc, pm=pm, xlm=xlm, xn=xn):
                if t == 0 and dc == 0:
                    gate(v, mx_m); gate(v, tdbcm)
                    for nm in names:
                        gate(v, maam[nm])
                gate(v, pm)
                gate(v, xlm)
                gate(v, {b.te: b.te.n})
                I(v.tensor_sub(xxb[dc][:, :], xn[dc][:, 0:tb],
                               xn[dc][:, 1:tb + 1]))
                v.wait_ge(b.ve.h, b.ve.n)
                I(v.scalar_tensor_tensor(xxx[:, :], xxb[dc][:, :], maa_x[:, dc:dc + 1],
                                         xn[dc][:, 1:tb + 1],
                                         ALU.mult, ALU.add))
            b.vec(_vx)
            vxm = snap(b.ve)
            def _m1(te, t=t, dc=dc, vxm=vxm, pm=pm):
                gate(te, vxm)
                if dc == 0:
                    gate(te, w1m); gate(te, pm)
                te.matmul(p_t1[:, :], w1t[dc][:, 0:96], xxx[:, :],
                          start=(dc == 0), stop=(dc == NDC - 1))
                return te.matmul(p_t2[:, :], w1t[dc][:, 96:160], xxx[:, :],
                                 start=(dc == 0), stop=(dc == NDC - 1))
            b.ten(_m1)
        m1m = snap(b.te)

        def _th(s, I, m1m=m1m):
            gate(s, m1m)
            for f in range(3):
                I(s.activation(Tf[f][:, :], p_t1[f * 32:(f + 1) * 32, :], AF.Tanh))
            for f in range(2):
                I(s.activation(Tf[3 + f][:, :], p_t2[f * 32:(f + 1) * 32, :], AF.Tanh))
        b.scl(_th)
        thm = snap(b.sc)

        it = 0
        evm_hist = [{}, {}]
        for fi, nm in enumerate(names):
            rhs = Tf[fi][:, :]
            for dc in range(NDC):
                pb = it % 2
                pmm = dict(evm_hist[pb])
                def _mf(te, fi=fi, dc=dc, pb=pb, thm=thm, rhs=rhs, pmm=pmm):
                    gate(te, thm)
                    if it < 2:
                        gate(te, w2m)
                    gate(te, pmm)
                    return te.matmul(p_m[pb][:, :], w2t[fi][:, dc * 128:(dc + 1) * 128],
                                     rhs, start=True, stop=True)
                b.ten(_mf)
                mfm = snap(b.te)
                def _mix(v, I, t=t, nm=nm, dc=dc, pb=pb, mfm=mfm, xn=xn):
                    gate(v, mfm)
                    I(v.tensor_scalar_add(t_mf[:, :], p_m[pb][:, :], maa[nm][:, dc:dc + 1]))
                    v.wait_ge(b.ve.h, b.ve.n)
                    I(v.tensor_mul(t_mf2[:, :], xxb[dc][:, :], t_mf[:, :]))
                    v.wait_ge(b.ve.h, b.ve.n)
                    I(v.tensor_add(mixed[nm][dc][:, :], t_mf2[:, :],
                                   xn[dc][:, 1:tb + 1]))
                b.vec(_mix)
                evm_hist[pb] = snap(b.ve)
                it += 1
        mixm = snap(b.ve)

        fm_jobs = [("r", "r", outs["r_fm"], None), ("k", "k", outs["k_fm"], None),
                   ("g", "g", outs["g_fm"], AF.Silu)]
        oit = 0
        for src_nm, wname, odram, act in fm_jobs:
            for oc in range(NDC):
                pb = oit % 2
                def _pmm(te, src_nm=src_nm, wname=wname, oc=oc, pb=pb, mixm=mixm, oit=oit):
                    gate(te, mixm)
                    if oit < 2:
                        gate(te, Wm[wname])
                    if oit >= 2:
                        gate(te, {b.sc: b.sc.n})
                    last = None
                    for dc in range(NDC):
                        last = te.matmul(p_o[pb][:, :], Wt[wname][dc][:, oc * 128:(oc + 1) * 128],
                                         mixed[src_nm][dc][:, :],
                                         start=(dc == 0), stop=(dc == NDC - 1))
                    return last
                b.ten(_pmm)
                tem = snap(b.te)
                def _ev(s, I, oc=oc, pb=pb, tem=tem, act=act, oit=oit):
                    gate(s, tem)
                    if oit >= 2:
                        oring.wait_slot(s, oit % 2)
                    I(s.activation(evo[oit % 2][:, :], p_o[pb][:, :],
                                   act if act is not None else AF.Copy))
                b.scl(_ev)
                evm = snap(b.sc)
                def _st(g, t=t, oc=oc, odram=odram, evm=evm, oit=oit):
                    gate(g, evm)
                    oring.issue(g.dma_start(
                        out=odram[oc * 128:(oc + 1) * 128, t * tb:(t + 1) * tb],
                        in_=evo[oit % 2][:, :]), oit % 2)
                b.gpd(_st)
                oit += 1

        for src_nm, wname, odram in [("v", "v", outs["v_tok"]), ("k", "k", outs["k_tok"])]:
            for tc in range(ntc):
                for ob in range(2):
                    pb = oit % 2
                    def _pmm(te, src_nm=src_nm, wname=wname, tc=tc, ob=ob, pb=pb, oit=oit):
                        gate(te, {b.sc: b.sc.n})
                        last = None
                        for dc in range(NDC):
                            last = te.matmul(p_o[pb][:, :],
                                             mixed[src_nm][dc][:, tc * 128:(tc + 1) * 128],
                                             Wt[wname][dc][:, ob * 512:(ob + 1) * 512],
                                             start=(dc == 0), stop=(dc == NDC - 1))
                        return last
                    b.ten(_pmm)
                    tem = snap(b.te)
                    def _ev(s, I, pb=pb, tem=tem, oit=oit):
                        gate(s, tem)
                        oring.wait_slot(s, oit % 2)
                        I(s.activation(evo[oit % 2][:, :], p_o[pb][:, :], AF.Copy))
                    b.scl(_ev)
                    evm = snap(b.sc)
                    def _st(g, t=t, tc=tc, ob=ob, odram=odram, evm=evm, oit=oit):
                        gate(g, evm)
                        oring.issue(g.dma_start(
                            out=odram[t * tb + tc * 128:t * tb + (tc + 1) * 128,
                                      ob * 512:(ob + 1) * 512],
                            in_=evo[oit % 2][:, :]), oit % 2)
                    b.gpd(_st)
                    oit += 1

        def _ph(te, t=t, mixm=mixm):
            gate(te, mixm)
            gate(te, td1m)
            last = None
            for dc in range(NDC):
                last = te.matmul(p_h[:, :], td1t[dc][:, 0:64], mixed["w"][dc][:, :],
                                 start=(dc == 0), stop=(dc == NDC - 1))
            return last
        b.ten(_ph)
        phm = snap(b.te)
        def _tanh2(s, I, t=t, phm=phm):
            gate(s, phm)
            I(s.activation(h_td[:, t * tb:(t + 1) * tb], p_h[:, :], AF.Tanh))
        b.scl(_tanh2)
        htm = snap(b.sc)

        for tc in range(ntc):
            for ob in range(2):
                pb = oit % 2
                def _pw(te, t=t, tc=tc, ob=ob, pb=pb, htm=htm, oit=oit):
                    gate(te, htm)
                    if oit < 2:
                        gate(te, td2m)
                    gate(te, {b.sc: b.sc.n})
                    return te.matmul(p_o[pb][:, :],
                                     h_td[:, t * tb + tc * 128:t * tb + (tc + 1) * 128],
                                     td2t[0][:, ob * 512:(ob + 1) * 512],
                                     start=True, stop=True)
                b.ten(_pw)
                tem = snap(b.te)
                def _vew(v, I, pb=pb, ob=ob, tem=tem):
                    gate(v, tem)
                    I(v.tensor_add(ewt[:, :], p_o[pb][:, :],
                                   td_bc[:, ob * 512:(ob + 1) * 512]))
                b.vec(_vew)
                vem = snap(b.ve)
                def _sew(s, I, pb=pb, vem=vem, oit=oit):
                    gate(s, vem)
                    oring.wait_slot(s, oit % 2)
                    I(s.activation(evo[oit % 2][:, :], ewt[:, :], AF.Exp))
                b.scl(_sew)
                sem_ = snap(b.sc)
                def _st(g, t=t, tc=tc, ob=ob, sem_=sem_, oit=oit):
                    gate(g, sem_)
                    oring.issue(g.dma_start(
                        out=outs["ew_tok"][t * tb + tc * 128:t * tb + (tc + 1) * 128,
                                           ob * 512:(ob + 1) * 512],
                        in_=evo[oit % 2][:, :]), oit % 2)
                b.gpd(_st)
                oit += 1

        prev_tb = {b.ve: b.ve.n, b.sc: b.sc.n, b.te: b.te.n, **oring.marks()}
    return oring.marks()


def stage_wkv(b, tag, ins, consts, outs, T, inmarks):
    import os
    WCUT = int(os.environ.get("WCUT", "99"))
    NCHK = T // CH
    NHP = 8
    eps = 1e-5 * 64
    lr = DmaRing(b, f"{tag}ld", 2)
    oring = DmaRing(b, f"{tag}os", 2)

    ucol, ucm = load_col(b, f"{tag}u", ins["u"], NDC)
    gcol, gcm = load_col(b, f"{tag}ga", ins["gamma"], NDC)
    bcol, bcm = load_col(b, f"{tag}be", ins["beta"], NDC)
    epst = b.sb(f"{tag}ep", [128, 1], F32)
    u_bd = [b.sb(f"{tag}ub{hp}", [128, 2], BF16) for hp in range(NHP)]
    S_bd = [b.sb(f"{tag}S{hp}", [128, 128], F32) for hp in range(NHP)]
    S_bf = [b.sb(f"{tag}Sb{hp}", [128, 128], BF16) for hp in range(NHP)]

    def _init(v, I):
        gate(v, ucm); gate(v, gcm); gate(v, bcm)
        I(v.memset(epst[:, :], eps))
        for hp in range(NHP):
            I(v.memset(S_bd[hp][:, :], 0.0))
            I(v.memset(S_bf[hp][:, :], 0.0))
            I(v.memset(u_bd[hp][:, :], 0.0))
        for hp in range(NHP):
            v.wait_ge(b.ve.h, b.ve.n)
            I(v.tensor_copy(u_bd[hp][0:64, 0:1], ucol[0:64, hp:hp + 1]))
            v.wait_ge(b.ve.h, b.ve.n)
            I(v.tensor_copy(u_bd[hp][64:128, 1:2], ucol[64:128, hp:hp + 1]))
    b.vec(_init)
    initm = snap(b.ve)

    ew_c = b.sb(f"{tag}ew", [128, C], BF16)
    kt_c = b.sb(f"{tag}ktk", [128, C], BF16)
    vt_c = b.sb(f"{tag}vtk", [128, C], BF16)
    E2 = b.sb(f"{tag}E2", [128, C], BF16)
    khat = b.sb(f"{tag}kh", [128, C], BF16)
    r_c = [b.sb(f"{tag}rc{i}", [128, 128], BF16) for i in range(NHP)]
    k_c = [b.sb(f"{tag}kc{i}", [128, 128], BF16) for i in range(NHP)]
    g_c = [b.sb(f"{tag}gc{i}", [128, 128], BF16) for i in range(NHP)]
    Efeat = b.sb(f"{tag}Ef", [128, 259], BF16)
    rt = b.sb(f"{tag}rt", [128, 128], BF16)
    ktl = b.sb(f"{tag}kt", [128, 128], BF16)
    Am = [b.sb(f"{tag}Am{i}", [128, 128], BF16) for i in range(2)]
    prk = b.sb(f"{tag}prk", [128, 128], BF16)
    vmod = b.sb(f"{tag}vm", [128, 128], BF16)
    y_sb = b.sb(f"{tag}ysb", [128, 128], BF16)
    ysq = b.sb(f"{tag}ysq", [128, 128], BF16)
    s_mu = b.sb(f"{tag}smu", [2, 128], F32)
    s_m2 = b.sb(f"{tag}sm2", [2, 128], F32)
    s_va = b.sb(f"{tag}sva", [2, 128], F32)
    s_rs = b.sb(f"{tag}srs", [2, 128], F32)
    s_bm = b.sb(f"{tag}sbm", [2, 128], F32)
    s_rsb = b.sb(f"{tag}srb", [2, 128], BF16)
    s_bmb = b.sb(f"{tag}sbb", [2, 128], BF16)
    plq = b.sb(f"{tag}plq", [128, 3], F32)
    t_y1 = b.sb(f"{tag}ty1", [128, 128], F32)
    t_y2 = b.sb(f"{tag}ty2", [128, 128], F32)
    t_s2 = b.sb(f"{tag}ts2", [128, 128], F32)
    ev_o = [b.sb(f"{tag}evo{i}", [128, 128], BF16) for i in range(2)]
    p_karg = b.ps(f"{tag}pka", [128, C], F32)
    _pf = b.ps(f"{tag}pf", [128, 512], F32)
    _pa = b.ps(f"{tag}pa", [128, 512], F32)
    _pb = b.ps(f"{tag}pb", [128, 512], F32)
    _pc = b.ps(f"{tag}pc", [128, 512], F32)
    p_feat = _pf[:, 0:259]
    p_bc = [_pf[:, 260:388], _pb[:, 258:386]]
    p_A = [_pa[:, 0:128], _pc[:, 0:128]]
    p_Y = _pa[:, 256:384]
    p_dS = _pc[:, 256:384]
    p_d = _pb[:, 0:2]
    p_st = [_pb[0:2, 2:130], _pb[0:2, 130:258]]

    prevc = dict(inmarks)

    for c in range(NCHK):
        t0 = c * CH
        pm = dict(prevc)
        def _ld(sync, c=c, t0=t0, pm=pm):
            gate(sync, pm)
            if c == 0:
                gate(sync, initm)
            lr.issue(sync.dma_start(out=ew_c[:, :], in_=ins["ew_tok"][t0:t0 + CH, :]), 0)
            lr.issue(sync.dma_start(out=kt_c[:, :], in_=ins["k_tok"][t0:t0 + CH, :]), 0)
            lr.issue(sync.dma_start(out=vt_c[:, :], in_=ins["v_tok"][t0:t0 + CH, :]), 0)
            for hp in range(NHP):
                lr.issue(sync.dma_start(out=r_c[hp][:, :],
                                        in_=ins["r_fm"][hp * 128:(hp + 1) * 128, t0:t0 + CH]), 0)
                lr.issue(sync.dma_start(out=k_c[hp][:, :],
                                        in_=ins["k_fm"][hp * 128:(hp + 1) * 128, t0:t0 + CH]), 0)
                lr.issue(sync.dma_start(out=g_c[hp][:, :],
                                        in_=ins["g_fm"][hp * 128:(hp + 1) * 128, t0:t0 + CH]), 0)
        b.syn(_ld)
        ldm = lr.marks()

        def _ctok(te, ldm=ldm, pm=pm):
            gate(te, ldm)
            gate(te, pm)
            te.matmul(p_karg[:, 0:512], consts["rtok"][:, :], ew_c[:, 0:512],
                      start=True, stop=True)
            return te.matmul(p_karg[:, 512:1024], consts["rtok"][:, :], ew_c[:, 512:1024],
                             start=True, stop=True)
        b.ten(_ctok)
        ctm = snap(b.te)
        def _e2(s, I, ctm=ctm):
            gate(s, ctm)
            I(s.activation(E2[:, :], p_karg[:, :], AF.Exp))
        b.scl(_e2)
        e2m = snap(b.sc)
        def _kh(v, I, e2m=e2m):
            gate(v, e2m)
            I(v.tensor_mul(khat[:, :], kt_c[:, :], E2[:, :]))
        b.vec(_kh)
        khm = snap(b.ve)

        for hp in range(NHP):
            def _cf(te, hp=hp, ldm=ldm):
                gate(te, ldm)
                gate(te, {b.sc: b.sc.n})
                return te.matmul(p_feat, ew_c[:, hp * 128:(hp + 1) * 128],
                                 consts["rfeat"][:, :], start=True, stop=True)
            b.ten(_cf)
            cfm = snap(b.te)
            def _ef(s, I, cfm=cfm):
                gate(s, cfm)
                gate(s, {b.ve: b.ve.n})
                I(s.activation(Efeat[:, :], p_feat, AF.Exp))
                s.wait_ge(b.sc.h, b.sc.n)
                I(s.activation(plq[:, :], _pf[:, 256:259], AF.Exp))
            b.scl(_ef)
            efm = snap(b.sc)
            def _rk(v, I, hp=hp, efm=efm, khm=khm):
                gate(v, efm)
                gate(v, khm)
                gate(v, {b.te: b.te.n})
                I(v.tensor_scalar_mul(S_bf[hp][:, :], S_bd[hp][:, :], plq[:, 2:3]))
                v.wait_ge(b.ve.h, b.ve.n)
                I(v.tensor_mul(rt[:, :], r_c[hp][:, :], Efeat[:, 0:128]))
                v.wait_ge(b.ve.h, b.ve.n)
                I(v.tensor_mul(ktl[:, :], k_c[hp][:, :], Efeat[:, 128:256]))
                v.wait_ge(b.ve.h, b.ve.n)
                I(v.tensor_mul(prk[:, :], r_c[hp][:, :], k_c[hp][:, :]))
            b.vec(_rk)
            rkm = snap(b.ve)
            def _am(te, hp=hp, rkm=rkm):
                gate(te, rkm)
                te.matmul(p_A[0], ktl[0:64, :], rt[0:64, :],
                          start=True, stop=True, tile_position=(0, 0))
                te.matmul(p_A[1], ktl[64:128, :], rt[64:128, :],
                          start=True, stop=True, tile_position=(64, 0))
                return te.matmul(p_d, prk[:, :], u_bd[hp][:, :],
                                 start=True, stop=True)
            if WCUT >= 3:
                b.ten(_am)
            amm = snap(b.te)
            def _mk(v, I, hp=hp, amm=amm):
                gate(v, amm)
                I(v.tensor_mul(Am[0][:, :], p_A[0], consts["swkv"][:, :]))
                v.wait_ge(b.ve.h, b.ve.n)
                I(v.tensor_mul(Am[1][:, :], p_A[1], consts["swkv"][:, :]))
                v.wait_ge(b.ve.h, b.ve.n)
                vv = vt_c[:, hp * 128:(hp + 1) * 128].rearrange("p (h j) -> p h j", h=2)
                dd = p_d.unsqueeze(2).broadcast_to([128, 2, 64])
                I(v.tensor_tensor(vmod[:, :].rearrange("p (h j) -> p h j", h=2),
                                  vv, dd, ALU.mult))
            if WCUT >= 4:
                b.vec(_mk)
            mkm = snap(b.ve)
            def _ym(te, hp=hp, mkm=mkm):
                gate(te, mkm)
                gate(te, {b.sc: b.sc.n})
                te.matmul(p_Y[0:64, :], vt_c[:, hp * 128:hp * 128 + 64], Am[0][:, :],
                          start=True, stop=False, tile_position=(0, 0))
                te.matmul(p_Y[0:64, :], vmod[:, 0:64], consts["ident"][:, :],
                          start=False, stop=False, tile_position=(0, 0))
                te.matmul(p_Y[0:64, :], S_bf[hp][:, 0:64], rt[:, :],
                          start=False, stop=True, tile_position=(0, 0))
                te.matmul(p_Y[64:128, :], vt_c[:, hp * 128 + 64:hp * 128 + 128], Am[1][:, :],
                          start=True, stop=False, tile_position=(0, 64))
                te.matmul(p_Y[64:128, :], vmod[:, 64:128], consts["ident"][:, :],
                          start=False, stop=False, tile_position=(0, 64))
                return te.matmul(p_Y[64:128, :], S_bf[hp][:, 64:128], rt[:, :],
                                 start=False, stop=True, tile_position=(0, 64))
            if WCUT >= 5:
                b.ten(_ym)
            ymm = snap(b.te)
            def _ye(s, I, ymm=ymm):
                gate(s, ymm)
                gate(s, {b.ve: b.ve.n})
                I(s.activation(y_sb[:, :], p_Y, AF.Copy))
                s.wait_ge(b.sc.h, b.sc.n)
                I(s.activation(ysq[:, :], p_Y, AF.Square))
            if WCUT >= 6:
                b.scl(_ye)
            yem = snap(b.sc)
            def _st(te, yem=yem):
                gate(te, yem)
                gate(te, {b.ve: b.ve.n})
                te.matmul(p_st[0], consts["bd2"][:, :], y_sb[:, :],
                          start=True, stop=True)
                return te.matmul(p_st[1], consts["bd2"][:, :], ysq[:, :],
                                 start=True, stop=True)
            if WCUT >= 6:
                b.ten(_st)
            stm = snap(b.te)
            def _sv(v, I, stm=stm):
                gate(v, stm)
                I(v.tensor_scalar_mul(s_mu[:, :], p_st[0], 1.0 / 64))
            if WCUT >= 6:
                b.vec(_sv)
            svm = snap(b.ve)
            def _ss(s, I, svm=svm):
                gate(s, svm)
                I(s.activation(s_m2[:, :], s_mu[:, :], AF.Square))
            if WCUT >= 6:
                b.scl(_ss)
            ssm = snap(b.sc)
            def _sv2(v, I, ssm=ssm):
                gate(v, ssm)
                I(v.scalar_tensor_tensor(s_va[:, :], p_st[1], 1.0 / 64, s_m2[:, :],
                                         ALU.mult, ALU.subtract))
            if WCUT >= 6:
                b.vec(_sv2)
            sv2m = snap(b.ve)
            def _ss2(s, I, sv2m=sv2m):
                gate(s, sv2m)
                I(s.activation(s_va[:, :], s_va[:, :], AF.Ln, bias=epst[0:2, 0:1]))
                s.wait_ge(b.sc.h, b.sc.n)
                I(s.activation(s_rs[:, :], s_va[:, :], AF.Exp, scale=-0.5))
            if WCUT >= 6:
                b.scl(_ss2)
            ss2m = snap(b.sc)
            def _sv3(v, I, ss2m=ss2m):
                gate(v, ss2m)
                I(v.tensor_mul(s_bm[:, :], s_mu[:, :], s_rs[:, :]))
                v.wait_ge(b.ve.h, b.ve.n)
                I(v.tensor_copy(s_rsb[:, :], s_rs[:, :]))
                v.wait_ge(b.ve.h, b.ve.n)
                I(v.tensor_copy(s_bmb[:, :], s_bm[:, :]))
            if WCUT >= 6:
                b.vec(_sv3)
            sv3m = snap(b.ve)
            def _bc(te, sv3m=sv3m):
                gate(te, sv3m)
                te.matmul(p_bc[0], consts["selh"][:, :], s_rsb[:, :],
                          start=True, stop=True)
                return te.matmul(p_bc[1], consts["selh"][:, :], s_bmb[:, :],
                                 start=True, stop=True)
            if WCUT >= 6:
                b.ten(_bc)
            bcm2 = snap(b.te)
            def _yn(v, I, hp=hp, bcm2=bcm2, oit=c * NHP + hp):
                gate(v, bcm2)
                I(v.tensor_mul(t_y1[:, :], y_sb[:, :], p_bc[0]))
                v.wait_ge(b.ve.h, b.ve.n)
                I(v.tensor_sub(t_y2[:, :], t_y1[:, :], p_bc[1]))
                v.wait_ge(b.ve.h, b.ve.n)
                I(v.tensor_scalar(t_y1[:, :], t_y2[:, :], gcol[:, hp:hp + 1],
                                  bcol[:, hp:hp + 1], ALU.mult, ALU.add))
                v.wait_ge(b.ve.h, b.ve.n)
                if oit >= 2:
                    oring.wait_slot(v, oit % 2)
                I(v.tensor_mul(ev_o[oit % 2][:, :], t_y1[:, :], g_c[hp][:, :]))
            if WCUT >= 6:
                b.vec(_yn)
            ynm = snap(b.ve)
            def _so(g, hp=hp, c=c, t0=t0, ynm=ynm, oit=c * NHP + hp):
                gate(g, ynm)
                oring.issue(g.dma_start(
                    out=outs["yng_fm"][hp * 128:(hp + 1) * 128, t0:t0 + CH],
                    in_=ev_o[oit % 2][:, :]), oit % 2)
            if WCUT >= 7:
                b.gpd(_so)
            def _ds(te, hp=hp, khm=khm):
                gate(te, khm)
                gate(te, {b.ve: b.ve.n})
                te.matmul(p_dS[0:64, 0:64], khat[:, hp * 128:hp * 128 + 64],
                          vt_c[:, hp * 128:hp * 128 + 64],
                          start=True, stop=True, tile_position=(0, 0))
                return te.matmul(p_dS[64:128, 64:128], khat[:, hp * 128 + 64:hp * 128 + 128],
                                 vt_c[:, hp * 128 + 64:hp * 128 + 128],
                                 start=True, stop=True, tile_position=(0, 64))
            if WCUT >= 8:
                b.ten(_ds)
            dsm = snap(b.te)
            def _su(v, I, hp=hp, dsm=dsm):
                gate(v, dsm)
                I(v.tensor_scalar_mul(t_s2[0:64, 0:64], p_dS[0:64, 0:64],
                                      plq[0:64, 1:2]))
                v.wait_ge(b.ve.h, b.ve.n)
                I(v.tensor_scalar_mul(t_s2[64:128, 64:128], p_dS[64:128, 64:128],
                                      plq[64:128, 1:2]))
                v.wait_ge(b.ve.h, b.ve.n)
                I(v.tensor_scalar_mul(S_bd[hp][:, :], S_bd[hp][:, :], plq[:, 0:1]))
                v.wait_ge(b.ve.h, b.ve.n)
                I(v.tensor_add(S_bd[hp][0:64, 0:64], S_bd[hp][0:64, 0:64], t_s2[0:64, 0:64]))
                v.wait_ge(b.ve.h, b.ve.n)
                I(v.tensor_add(S_bd[hp][64:128, 64:128], S_bd[hp][64:128, 64:128],
                               t_s2[64:128, 64:128]))
            if WCUT >= 8:
                b.vec(_su)
        prevc = {b.ve: b.ve.n, b.sc: b.sc.n, b.te: b.te.n, **oring.marks()}
    return oring.marks()

def host_consts(T):
    import ml_dtypes
    ar = np.arange(CH)
    LTincl = (ar[:, None] <= ar[None, :]).astype(np.float32)
    LTstrict = (ar[:, None] < ar[None, :]).astype(np.float32)
    B63 = np.zeros((CH, CH), np.float32)
    B63[:CH // 2, :] = 1.0
    R1 = B63 - LTstrict
    R2 = LTincl - B63
    colL = -np.ones((CH, 1), np.float32)
    colQ = B63[:, :1] - 1.0
    colM = -B63[:, :1]
    Rfeat = np.concatenate([R1, R2, colL, colQ, colM], axis=1)
    c = {
        "c_ones128": np.ones((128, 128), np.float32),
        "c_rfeat": Rfeat.astype(ml_dtypes.bfloat16),
        "c_rtok": R2.astype(ml_dtypes.bfloat16),
        "c_ident": np.eye(128, dtype=ml_dtypes.bfloat16),
        "c_swkv": (ar[:, None] < ar[None, :]).astype(ml_dtypes.bfloat16),
        "c_bd2": np.concatenate(
            [np.concatenate([np.ones((64, 1)), np.zeros((64, 1))]),
             np.concatenate([np.zeros((64, 1)), np.ones((64, 1))])],
            axis=1).astype(ml_dtypes.bfloat16),
        "c_selh": np.concatenate(
            [np.concatenate([np.ones((1, 64)), np.zeros((1, 64))], 1),
             np.concatenate([np.zeros((1, 64)), np.ones((1, 64))], 1)],
            0).astype(ml_dtypes.bfloat16),
        "c_minf": np.where(ar[:, None] > ar[None, :], -1e30, 0.0).astype(np.float32),
    }
    slopes = (2.0 ** (-8.0 / H)) ** np.arange(1, H + 1)
    kcoord = np.arange(T, dtype=np.float32)
    iota2 = np.zeros((2, 128), np.float32)
    iota2[0] = np.arange(128)
    iota2[1] = 1.0
    slopes2 = np.zeros((2, 512), np.float32)
    for h in range(H):
        for kc in range(min(32, T // 128)):
            slopes2[0, h * 32 + kc] = slopes[h]
            slopes2[1, h * 32 + kc] = slopes[h] * 128.0 * kc
    qbn = np.zeros((H, 1, T), np.float32)
    for h in range(H):
        qbn[h, 0] = -slopes[h] * kcoord
    c["c_iota2"] = iota2
    c["c_slopes2"] = slopes2
    c["c_qbn"] = qbn
    return c


def stage_l0fattn(b, tag, q_dram, k_dram, v_dram, out_dram, T, inmarks):
    NKC = T // CH
    G = 4
    NQB = T // 512
    EXPW = min(1024, T)
    NHALF = T // EXPW
    lr = DmaRing(b, f"{tag}ld", 2)
    lrv = DmaRing(b, f"{tag}lv", 2)
    oring = DmaRing(b, f"{tag}os", 2)

    qt = b.sb(f"{tag}q", [128, T], BF16)
    kt = b.sb(f"{tag}k", [128, T], BF16)
    vv = [b.sb(f"{tag}vv{i}", [128, 128], BF16) for i in range(2)]
    vn = [b.sb(f"{tag}vn{i}", [128, 128], BF16) for i in range(G)]
    E1 = [b.sb(f"{tag}E1_{i}", [128, T], BF16) for i in range(G)]
    E2 = [b.sb(f"{tag}E2_{i}", [128, T], BF16) for i in range(G)]
    zp = b.sb(f"{tag}zp", [128, 2 * NHALF], F32)
    zz = b.sb(f"{tag}zz", [128, 2], F32)
    rz = b.sb(f"{tag}rz", [128, 2], F32)
    acc = b.sb(f"{tag}ac", [128, T], F32)
    evo = [b.sb(f"{tag}ev{i}", [128, 512], BF16) for i in range(2)]
    p_s1 = b.ps(f"{tag}ps1", [128, 1024], F32)
    p_s2 = b.ps(f"{tag}ps2", [128, 1024], F32)
    p_av = [b.ps(f"{tag}pav{i}", [128, 512], F32) for i in range(2)]
    p_aw = [b.ps(f"{tag}paw{i}", [128, 512], F32) for i in range(2)]

    prev_hp = dict(inmarks)
    for hp in range(NDC):
        pmh = dict(prev_hp)
        def _ldq(sync, hp=hp, pmh=pmh):
            gate(sync, pmh)
            lr.issue(sync.dma_start(out=qt[:, :], in_=q_dram[hp * 128:(hp + 1) * 128, :]), 0)
            lr.issue(sync.dma_start(out=kt[:, :], in_=k_dram[hp * 128:(hp + 1) * 128, :]), 0)
        b.syn(_ldq)
        qm = lr.marks()
        prev_g_av = {}
        last_exp = {}
        last_z = {}
        acc_marks = {}
        for kg in range(NKC // G):
            for ki in range(G):
                kc = kg * G + ki
                k0 = kc * CH
                def _ldv(sync, hp=hp, k0=k0, kc=kc):
                    zm = last_z.get(kc % 2)
                    if zm:
                        gate(sync, zm)
                    lrv.issue(sync.dma_start(out=vv[kc % 2][:, :],
                                             in_=v_dram[k0:k0 + CH, hp * 128:(hp + 1) * 128]),
                              kc)
                b.syn(_ldv)
                vm_ = {lrv.sems[kc % 2]: lrv.sems[kc % 2].n}
                for half in range(NHALF):
                    def _s1(te, hp=hp, kc=kc, k0=k0, half=half, qm=qm, ki=ki):
                        if half == 0 and kc == 0:
                            gate(te, qm)
                        em = last_exp.get(half % 1 or 0)
                        if em:
                            gate(te, em)
                        last = None
                        for sub in range(EXPW // 512):
                            q0 = half * EXPW + sub * 512
                            te.matmul(p_s1[:, sub * 512:(sub + 1) * 512],
                                      kt[0:64, k0:k0 + CH], qt[0:64, q0:q0 + 512],
                                      start=True, stop=True, tile_position=(0, 0))
                            last = te.matmul(p_s2[:, sub * 512:(sub + 1) * 512],
                                             kt[64:128, k0:k0 + CH], qt[64:128, q0:q0 + 512],
                                             start=True, stop=True, tile_position=(64, 0))
                        return last
                    b.ten(_s1)
                    tm = snap(b.te)
                    def _ex(s, I, ki=ki, half=half, tm=tm, kc=kc):
                        gate(s, tm)
                        if half == 0:
                            zm = last_z.get((kc - 1) % 2)
                            if zm:
                                gate(s, zm)
                            am = prev_g_av.get(ki)
                            if am:
                                gate(s, am)
                        I(s.activation(E1[ki][:, half * EXPW:(half + 1) * EXPW],
                                       p_s1[:, 0:EXPW], AF.Exp,
                                       accum_out=zp[:, 2 * half:2 * half + 1]))
                        s.wait_ge(b.sc.h, b.sc.n)
                        I(s.activation(E2[ki][:, half * EXPW:(half + 1) * EXPW],
                                       p_s2[:, 0:EXPW], AF.Exp,
                                       accum_out=zp[:, 2 * half + 1:2 * half + 2]))
                    b.scl(_ex)
                    last_exp[0] = snap(b.sc)
                exm = snap(b.sc)
                def _z(v, I, ki=ki, exm=exm, vm_=vm_, kc=kc):
                    gate(v, exm)
                    gate(v, vm_)
                    am = prev_g_av.get(ki)
                    if am:
                        gate(v, am)
                    z3 = zp[:, :].rearrange("p (a c) -> p c a", c=2)
                    I(v.tensor_reduce(zz[:, :], z3, AX.X, ALU.add))
                    v.wait_ge(b.ve.h, b.ve.n)
                    I(v.reciprocal(rz[:, :], zz[:, :]))
                    v.wait_ge(b.ve.h, b.ve.n)
                    v3 = vv[kc % 2][:, :].rearrange("p (h j) -> p h j", h=2)
                    r3 = rz[:, :].unsqueeze(2).broadcast_to([128, 2, 64])
                    I(v.tensor_tensor(vn[ki][:, :].rearrange("p (h j) -> p h j", h=2),
                                      v3, r3, ALU.mult))
                b.vec(_z)
                last_z[kc % 2] = snap(b.ve)
            egm = snap(b.ve)
            new_av = {}
            for qb in range(NQB):
                pb = qb % 2
                def _av(te, hp=hp, kg=kg, qb=qb, pb=pb, egm=egm):
                    if qb == 0:
                        gate(te, egm)
                    am = acc_marks.get((qb - 2) % NQB if qb >= 2 else None)
                    if qb >= 2:
                        gate(te, acc_marks.get(qb - 2, {}))
                    elif kg > 0:
                        gate(te, acc_marks.get(NQB - 2 + qb, {}))
                    for ki in range(G):
                        te.matmul(p_av[pb][0:64, :], vn[ki][:, 0:64],
                                  E1[ki][:, qb * 512:(qb + 1) * 512],
                                  start=(ki == 0), stop=(ki == G - 1), tile_position=(0, 0))
                    last = None
                    for ki in range(G):
                        last = te.matmul(p_aw[pb][64:128, :], vn[ki][:, 64:128],
                                         E2[ki][:, qb * 512:(qb + 1) * 512],
                                         start=(ki == 0), stop=(ki == G - 1),
                                         tile_position=(0, 64))
                    return last
                b.ten(_av)
                avm = snap(b.te)
                if qb == NQB - 1:
                    new_av = {ki: avm for ki in range(G)}
                def _acc(v, I, kg=kg, qb=qb, pb=pb, avm=avm):
                    gate(v, avm)
                    if kg == 0:
                        I(v.tensor_copy(acc[0:64, qb * 512:(qb + 1) * 512], p_av[pb][0:64, :]))
                        v.wait_ge(b.ve.h, b.ve.n)
                        I(v.tensor_copy(acc[64:128, qb * 512:(qb + 1) * 512], p_aw[pb][64:128, :]))
                    else:
                        I(v.tensor_add(acc[0:64, qb * 512:(qb + 1) * 512],
                                       acc[0:64, qb * 512:(qb + 1) * 512], p_av[pb][0:64, :]))
                        v.wait_ge(b.ve.h, b.ve.n)
                        I(v.tensor_add(acc[64:128, qb * 512:(qb + 1) * 512],
                                       acc[64:128, qb * 512:(qb + 1) * 512], p_aw[pb][64:128, :]))
                b.vec(_acc)
                acc_marks[qb] = snap(b.ve)
            prev_g_av = new_av
        accm = snap(b.ve)
        for qb in range(NQB):
            def _cv(v, I, qb=qb, accm=accm, oit=hp * NQB + qb):
                if qb == 0:
                    gate(v, accm)
                if oit >= 2:
                    oring.wait_slot(v, oit % 2)
                I(v.tensor_copy(evo[oit % 2][:, :], acc[:, qb * 512:(qb + 1) * 512]))
            b.vec(_cv)
            cvm_ = snap(b.ve)
            def _st(g, hp=hp, qb=qb, cvm_=cvm_, oit=hp * NQB + qb):
                gate(g, cvm_)
                oring.issue(g.dma_start(
                    out=out_dram[hp * 128:(hp + 1) * 128, qb * 512:(qb + 1) * 512],
                    in_=evo[oit % 2][:, :]), oit % 2)
            b.gpd(_st)
        prev_hp = {b.ve: b.ve.n, b.sc: b.sc.n, b.te: b.te.n, **oring.marks()}
    return oring.marks()



def stage_ffntail(b, tag, z1_dram, wff_dram, out_dram, T, ones_f32, omarks, inmarks):
    tb = 512
    ntb = T // tb
    wt, wm = load_cast_fm(b, f"{tag}w", wff_dram, NDC, C)
    zt, zm = load_fm_raw(b, f"{tag}z", z1_dram, NDC, T, BF16, wait=inmarks)
    oring = DmaRing(b, f"{tag}os", 2)
    p_z = [b.ps(f"{tag}pz{i}", [128, tb], F32) for i in range(2)]
    p_ss = [b.ps(f"{tag}ss{i}", [128, tb], F32) for i in range(2)]
    z2sb = [b.sb(f"{tag}z2{oc}", [128, tb], BF16) for oc in range(NDC)]
    zsq = b.sb(f"{tag}zq", [128, tb], F32)
    rn = b.sb(f"{tag}rn", [128, tb], F32)
    t1 = b.sb(f"{tag}t1", [128, tb], F32)
    te_ = b.sb(f"{tag}te", [128, tb], F32)
    t3 = b.sb(f"{tag}t3", [128, tb], F32)
    s4 = b.sb(f"{tag}s4", [128, tb], F32)
    evo = [b.sb(f"{tag}ev{i}", [128, tb], BF16) for i in range(2)]
    prev = {}
    for t in range(ntb):
        pm = dict(prev)
        for oc in range(NDC):
            pb = oc % 2
            def _mm(te, t=t, oc=oc, pb=pb, pm=pm):
                if t == 0 and oc == 0:
                    gate(te, wm); gate(te, zm); gate(te, omarks)
                if oc < 2:
                    gate(te, pm)
                gate(te, {b.sc: b.sc.n})
                last = None
                for dc in range(NDC):
                    last = te.matmul(p_z[pb][:, :], wt[dc][:, oc * 128:(oc + 1) * 128],
                                     zt[dc][:, t * tb:(t + 1) * tb],
                                     start=(dc == 0), stop=(dc == NDC - 1))
                return last
            b.ten(_mm)
            tm = snap(b.te)
            def _ev(s, I, oc=oc, pb=pb, tm=tm, pm=pm):
                gate(s, tm)
                gate(s, pm)
                I(s.activation(z2sb[oc][:, :], p_z[pb][:, :], AF.Copy))
                s.wait_ge(b.sc.h, b.sc.n)
                I(s.activation(zsq[:, :], p_z[pb][:, :], AF.Square))
            b.scl(_ev)
            evm = snap(b.sc)
            def _ssm(te, t=t, oc=oc, evm=evm):
                gate(te, evm)
                return te.matmul(p_ss[t % 2][:, :], ones_f32[:, :], zsq[:, :],
                                 start=(oc == 0), stop=(oc == NDC - 1))
            b.ten(_ssm)
        ssm = snap(b.te)
        def _rn(s, I, t=t, ssm=ssm):
            gate(s, ssm)
            gate(s, {b.ve: b.ve.n})
            I(s.activation(rn[:, :], p_ss[t % 2][:, :], AF.Ln))
            s.wait_ge(b.sc.h, b.sc.n)
            I(s.activation(rn[:, :], rn[:, :], AF.Exp, scale=-0.5))
        b.scl(_rn)
        rnm = snap(b.sc)
        for oc in range(NDC):
            oit = t * NDC + oc
            def _g(v, I, oc=oc, rnm=rnm, oit=oit):
                gate(v, rnm)
                gate(v, {b.sc: b.sc.n})
                I(v.tensor_mul(t1[:, :], z2sb[oc][:, :], rn[:, :]))
            b.vec(_g)
            g1m = snap(b.ve)
            def _e(s, I, g1m=g1m, oit=oit):
                gate(s, g1m)
                if oit >= 2:
                    oring.wait_slot(s, oit % 2)
                I(s.activation(t3[:, :], t1[:, :], AF.Gelu))
                s.wait_ge(b.sc.h, b.sc.n)
                I(s.activation(evo[oit % 2][:, :], t3[:, :], AF.Silu))
            b.scl(_e)
            g3m = snap(b.sc)
            def _st(g, t=t, oc=oc, g3m=g3m, oit=oit):
                gate(g, g3m)
                oring.issue(g.dma_start(
                    out=out_dram[oc * 128:(oc + 1) * 128, t * tb:(t + 1) * tb],
                    in_=evo[oit % 2][:, :]), oit % 2)
            b.gpd(_st)
        prev = {b.ve: b.ve.n, b.sc: b.sc.n, b.te: b.te.n, **oring.marks()}
    return oring.marks()



def stage_combine(b, tag, x_dram, s_dram, g_dram, out_dram, T, inmarks):
    tb = 512
    ntb = T // tb
    lrx = DmaRing(b, f"{tag}lx", 2)
    lrs = DmaRing(b, f"{tag}ls", 2)
    oring = DmaRing(b, f"{tag}os", 2)
    xb = [b.sb(f"{tag}x{i}", [128, tb], F32) for i in range(2)]
    sb_ = [b.sb(f"{tag}s{i}", [128, tb], BF16) for i in range(2)]
    gb = [b.sb(f"{tag}g{i}", [128, tb], BF16) for i in range(2)]
    tt = b.sb(f"{tag}t", [128, tb], F32)
    ob = [b.sb(f"{tag}o{i}", [128, tb], F32) for i in range(2)]
    vm_hist = [{}, {}]
    it = 0
    for t in range(ntb):
        for dc in range(NDC):
            sl = it % 2
            def _ld(sync, t=t, dc=dc, sl=sl, pm=vm_hist[sl]):
                if it == 0:
                    gate(sync, inmarks)
                gate(sync, pm)
                lrx.issue(sync.dma_start(out=xb[sl][:, :],
                                         in_=x_dram[dc * 128:(dc + 1) * 128, t * tb:(t + 1) * tb]), sl)
                lrs.issue(sync.dma_start(out=sb_[sl][:, :],
                                         in_=s_dram[dc * 128:(dc + 1) * 128, t * tb:(t + 1) * tb]), sl)
                lrs.issue(sync.dma_start(out=gb[sl][:, :],
                                         in_=g_dram[dc * 128:(dc + 1) * 128, t * tb:(t + 1) * tb]), sl)
            b.syn(_ld)
            lm = {lrx.sems[sl]: lrx.sems[sl].n, lrs.sems[sl]: lrs.sems[sl].n}
            def _v(v, I, sl=sl, lm=lm, it=it):
                gate(v, lm)
                if it >= 2:
                    oring.wait_slot(v, sl)
                I(v.tensor_mul(tt[:, :], sb_[sl][:, :], gb[sl][:, :]))
                v.wait_ge(b.ve.h, b.ve.n)
                I(v.tensor_add(ob[sl][:, :], xb[sl][:, :], tt[:, :]))
            b.vec(_v)
            vm = snap(b.ve)
            vm_hist[sl] = vm
            def _st(g, t=t, dc=dc, sl=sl, vm=vm):
                gate(g, vm)
                oring.issue(g.dma_start(
                    out=out_dram[dc * 128:(dc + 1) * 128, t * tb:(t + 1) * tb],
                    in_=ob[sl][:, :]), sl)
            b.gpd(_st)
            it += 1
    return oring.marks()


def stage_l1attn(b, tag, q_dram, k_dram, v_dram, iota2_dram, slopes2_dram,
                 qbn_dram, minf_dram, out_dram, T, inmarks):
    NQT = T // 512
    NKC = T // CH
    lr = DmaRing(b, f"{tag}ld", 2)
    lrq = DmaRing(b, f"{tag}lq", 2)
    oring = DmaRing(b, f"{tag}os", 2)
    qt_ = b.sb(f"{tag}q", [128, T], BF16)
    kt_ = b.sb(f"{tag}k", [128, T], BF16)
    v_hp = b.sb(f"{tag}v", [128, T], BF16)
    iota2 = b.sb(f"{tag}i2", [2, 128], F32)
    slopes2 = b.sb(f"{tag}s2", [2, 512], F32)
    bcolmat = b.sb(f"{tag}bc", [128, 512], F32)
    qbs = [b.sb(f"{tag}qb{i}", [1, 512], F32) for i in range(2)]
    minf = b.sb(f"{tag}mi", [128, 128], F32)
    E = [b.sb(f"{tag}E{i}", [128, 512], BF16) for i in range(8)]
    onesb = b.sb(f"{tag}on", [128, 32], BF16)
    ones1f = b.sb(f"{tag}o1f", [1, 128], F32)
    ones1 = b.sb(f"{tag}o1", [1, 128], BF16)
    zrow = b.sb(f"{tag}zr", [1, 512], F32)
    rzrow = b.sb(f"{tag}rz", [1, 512], F32)
    rzb = b.sb(f"{tag}rb", [1, 512], BF16)
    avsb = b.sb(f"{tag}av", [128, 512], F32)
    evo = [b.sb(f"{tag}ev{i}", [128, 512], BF16) for i in range(2)]
    p_s = [[b.ps(f"{tag}ps{h}{i}", [128, 512], F32) for i in range(2)] for h in range(2)]
    p_av = [b.ps(f"{tag}pa{i}", [128, 512], F32) for i in range(2)]
    p_z = [b.ps(f"{tag}pz{i}", [32, 512], F32) for i in range(2)]

    def _c0(g, I):
        I(g.memset(onesb[:, :], 1.0))
        I(g.memset(ones1[:, :], 1.0))
        I(g.memset(ones1f[:, :], 1.0))
    b.gpc(_c0)
    c0m = snap(b.gc)
    def _ldm(sync):
        gate(sync, inmarks)
        lr.issue(sync.dma_start(out=minf[:, :], in_=minf_dram[:, :]), 0)
        lr.issue(sync.dma_start(out=iota2[:, :], in_=iota2_dram[:, :]), 0)
        lr.issue(sync.dma_start(out=slopes2[:, :], in_=slopes2_dram[:, :]), 0)
    b.syn(_ldm)
    ldm0 = lr.marks()
    def _bcol(te, ldm0=ldm0):
        gate(te, ldm0)
        return te.matmul(p_s[0][0][:, :], iota2[:, :], slopes2[:, :], start=True, stop=True)
    b.ten(_bcol)
    bm0 = snap(b.te)
    def _bev(v, I, bm0=bm0):
        gate(v, bm0)
        I(v.tensor_copy(bcolmat[:, :], p_s[0][0][:, :]))
    b.vec(_bev)
    bcolm = snap(b.ve)

    prev_hp = dict(inmarks)
    for hp in range(NDC):
        pmh = dict(prev_hp)
        def _ldq(sync, hp=hp, pmh=pmh):
            gate(sync, pmh)
            if hp == 0:
                gate(sync, ldm0)
                gate(sync, bcolm)
            lr.issue(sync.dma_start(out=qt_[:, :], in_=q_dram[hp * 128:(hp + 1) * 128, :]), 0)
            lr.issue(sync.dma_start(out=kt_[:, :], in_=k_dram[hp * 128:(hp + 1) * 128, :]), 0)
            for cc in range(T // 128):
                lr.issue(sync.dma_start(out=v_hp[:, cc * 128:(cc + 1) * 128],
                                        in_=v_dram[cc * 128:(cc + 1) * 128,
                                                   hp * 128:(hp + 1) * 128]), 0)
        b.syn(_ldq)
        qm = lr.marks()
        prev_qt_sc = {}
        for qt in range(NQT):
            q0 = qt * 512
            nkc = qt * 4 + 4
            def _ldqb(sync, hp=hp, qt=qt, q0=q0):
                gate(sync, {b.te: b.te.n})
                for hh in range(2):
                    lrq.issue(sync.dma_start(out=qbs[hh][:, :],
                                             in_=qbn_dram[hp * 2 + hh, 0:1, q0:q0 + 512]), qt)
            b.syn(_ldqb)
            qbm = {lrq.sems[qt % 2]: lrq.sems[qt % 2].n}
            exp_marks = {}
            av_marks = {}
            for kc in range(nkc):
                k0 = kc * CH
                diag_j = kc - qt * 4
                c0 = max(0, diag_j) * 128
                for hh in range(2):
                    hsl = slice(hh * 64, (hh + 1) * 64)
                    pb = kc % 2
                    eb = E[(kc % 4) * 2 + hh]
                    def _s(te, hp=hp, qt=qt, kc=kc, hh=hh, hsl=hsl, q0=q0, k0=k0,
                           c0=c0, qm=qm, c0m=c0m, qbm=qbm, pb=pb, pmh=pmh):
                        if qt == 0 and kc == 0 and hh == 0:
                            gate(te, qm); gate(te, c0m); gate(te, pmh)
                        if kc == 0 and hh == 0:
                            gate(te, qbm)
                            if prev_qt_sc:
                                gate(te, prev_qt_sc)
                        pm = exp_marks.get((kc - 2, hh))
                        if pm:
                            gate(te, pm)
                        elif kc < 2 and qt > 0:
                            pm2 = prev_exp.get((hh, pb))
                            if pm2:
                                gate(te, pm2)
                        te.matmul(p_s[hh][pb][:, c0:512], kt_[hsl, k0:k0 + CH],
                                  qt_[hsl, q0 + c0:q0 + 512],
                                  start=True, stop=False,
                                  tile_position=(hh * 64, 0))
                        return te.matmul(p_s[hh][pb][:, c0:512],
                                         ones1f[:, :], qbs[hh][:, c0:512],
                                         start=False, stop=True)
                    b.ten(_s)
                    sm_ = snap(b.te)
                    if diag_j >= 0:
                        def _mi(v, I, hh=hh, c0=c0, sm_=sm_, pb=pb):
                            gate(v, sm_)
                            I(v.tensor_add(p_s[hh][pb][:, c0:c0 + 128],
                                           p_s[hh][pb][:, c0:c0 + 128], minf[:, :]))
                        b.vec(_mi)
                        mim = {b.ve: b.ve.n}
                    else:
                        mim = sm_
                    hidx = hp * 2 + hh
                    def _ex(s, I, hh=hh, c0=c0, mim=mim, eb=eb, hidx=hidx, kc=kc, pb=pb):
                        gate(s, mim)
                        am = av_marks.get((kc - 4, hh))
                        if am:
                            gate(s, am)
                        elif kc < 4 and qt > 0:
                            am2 = prev_av.get((hh, kc % 4))
                            if am2:
                                gate(s, am2)
                        I(s.activation(eb[:, c0:512], p_s[hh][pb][:, c0:512], AF.Exp,
                                       bias=bcolmat[:, hidx * 32 + kc:hidx * 32 + kc + 1]))
                    b.scl(_ex)
                    exp_marks[(kc, hh)] = snap(b.sc)
                    def _av(te, hp=hp, kc=kc, hh=hh, k0=k0, c0=c0, eb=eb, nkc=nkc, qt=qt):
                        gate(te, exp_marks[(kc, hh)])
                        if kc == 0 and prev_qt_sc:
                            gate(te, prev_qt_sc)
                        te.matmul(p_av[hh][0:64, c0:512],
                                  v_hp[:, kc * 128 + hh * 64:kc * 128 + hh * 64 + 64],
                                  eb[:, c0:512],
                                  start=(kc == 0), stop=(kc == nkc - 1),
                                  tile_position=(0, 0))
                        return te.matmul(p_z[hh][:, c0:512], onesb[:, :], eb[:, c0:512],
                                         start=(kc == 0), stop=(kc == nkc - 1))
                    b.ten(_av)
                    av_marks[(kc, hh)] = snap(b.te)
            prev_exp = {(hh, pbv): exp_marks.get((kcv, hh))
                        for hh in range(2) for kcv in range(nkc) for pbv in [kcv % 2]}
            prev_av = {(hh, kcv % 4): av_marks.get((kcv, hh))
                       for hh in range(2) for kcv in range(nkc)}
            for hh in range(2):
                oit = (hp * NQT + qt) * 2 + hh
                avm = av_marks[(nkc - 1, 1)]
                def _n1(s, I, hh=hh, avm=avm):
                    gate(s, avm)
                    gate(s, {b.ve: b.ve.n})
                    I(s.activation(zrow[:, :], p_z[hh][0:1, :], AF.Copy))
                    s.wait_ge(b.sc.h, b.sc.n)
                    I(s.activation(avsb[0:64, :], p_av[hh][0:64, :], AF.Copy))
                b.scl(_n1)
                n1m = snap(b.sc)
                def _n2(v, I, n1m=n1m, oit=oit, hh=hh):
                    gate(v, n1m)
                    I(v.reciprocal(rzrow[:, :], zrow[:, :]))
                    v.wait_ge(b.ve.h, b.ve.n)
                    I(v.tensor_copy(rzb[:, :], rzrow[:, :]))
                b.vec(_n2)
                n2m = snap(b.ve)
                def _n3(te, n2m=n2m, hh=hh):
                    gate(te, n2m)
                    return te.matmul(p_s[hh][0][:, :], ones1[:, :], rzb[:, :],
                                     start=True, stop=True)
                b.ten(_n3)
                n3m = snap(b.te)
                def _n5(v, I, oit=oit, n3m=n3m, hh=hh):
                    gate(v, n3m)
                    if oit >= 2:
                        oring.wait_slot(v, oit % 2)
                    I(v.tensor_mul(evo[oit % 2][0:64, :], avsb[0:64, :],
                                   p_s[hh][0][0:64, :]))
                b.vec(_n5)
                n5m = snap(b.ve)
                def _st(g, hp=hp, qt=qt, hh=hh, oit=oit, n5m=n5m):
                    gate(g, n5m)
                    oring.issue(g.dma_start(
                        out=out_dram[hp * 128 + hh * 64:hp * 128 + hh * 64 + 64,
                                     qt * 512:(qt + 1) * 512],
                        in_=evo[oit % 2][0:64, :]), oit % 2)
                b.gpd(_st)
            prev_qt_sc = {b.sc: b.sc.n, b.ve: b.ve.n}
        prev_hp = {b.ve: b.ve.n, b.sc: b.sc.n, b.te: b.te.n, **oring.marks()}
    return oring.marks()



def stage_moe(b, tag, xn_dram, selw_dram, selb_dram, comp_dram, compb_dram,
              out_dram, T, inmarks):
    NTC = T // 128
    NOB = 16
    xt, xm = load_fm_raw(b, f"{tag}x", xn_dram, NDC, T, BF16, wait=inmarks)
    oring = DmaRing(b, f"{tag}os", 2)
    wring = DmaRing(b, f"{tag}wl", 2)
    selt = [[b.sb(f"{tag}se{i}_{dc}", [128, 512], BF16) for dc in range(NDC)] for i in range(2)]
    compt = [[b.sb(f"{tag}co{i}_{dc}", [128, 512], BF16) for dc in range(NDC)] for i in range(2)]
    brow = [b.sb(f"{tag}br{i}", [1, 1024], BF16) for i in range(2)]
    brow32 = [b.sb(f"{tag}b32{i}", [1, 1024], F32) for i in range(2)]
    ones1 = b.sb(f"{tag}o1", [1, 128], BF16)
    lg = [b.sb(f"{tag}lg{i}", [128, 512], F32) for i in range(2)]
    mx = [b.sb(f"{tag}mx{i}", [128, 64], F32) for i in range(2)]
    dd = [b.sb(f"{tag}dd{i}", [128, 512], F32) for i in range(2)]
    sgn = [b.sb(f"{tag}sg{i}", [128, 512], BF16) for i in range(2)]
    csb = [b.sb(f"{tag}cs{i}", [128, 512], BF16) for i in range(2)]
    cm = [b.sb(f"{tag}cm{i}", [128, 512], BF16) for i in range(2)]
    r1 = [b.sb(f"{tag}r1{i}", [128, 64], F32) for i in range(2)]
    r2 = [b.sb(f"{tag}r2{i}", [128, 64], F32) for i in range(2)]
    gp = [b.sb(f"{tag}gp{i}", [128, 64], F32) for i in range(2)]
    s4 = [b.sb(f"{tag}s4{i}", [128, 64], F32) for i in range(2)]
    evo = [b.sb(f"{tag}ev{i}", [128, 64], BF16) for i in range(2)]
    p_lg = [b.ps(f"{tag}plg{i}", [128, 512], F32) for i in range(2)]
    p_cp = [b.ps(f"{tag}pcp{i}", [128, 512], F32) for i in range(2)]

    def _c0(g, I):
        I(g.memset(ones1[:, :], 1.0))
    b.gpc(_c0)
    c0m = snap(b.gc)

    slab_cast_marks = {}
    slab_used_marks = {}
    def load_slab(ocb):
        wb = ocb % 2
        o0 = ocb * 512
        um = slab_used_marks.get(ocb - 2, {})
        stage, ring, cvm = b.wstage()
        for kind, dst in (("sel", selt[wb]), ("comp", compt[wb])):
            srcd = selw_dram if kind == "sel" else comp_dram
            for dc in range(NDC):
                slot = (ring.sems[0].n + ring.sems[1].n) // 16 % 2
                def _lw(sync, srcd=srcd, dc=dc, o0=o0, slot=slot, pm=cvm[slot], um=um):
                    if pm > 0:
                        sync.wait_ge(b.ve.h, pm)
                    gate(sync, um)
                    ring.issue(sync.dma_start(out=stage[slot][:, 0:512],
                                              in_=srcd[dc * 128:(dc + 1) * 128, o0:o0 + 512]),
                               slot)
                b.syn(_lw)
                def _cw(v, I, dst=dst, dc=dc, slot=slot, um=um):
                    ring.wait_slot(v, slot)
                    gate(v, um)
                    I(v.tensor_copy(dst[dc][:, :], stage[slot][:, 0:512]))
                b.vec(_cw)
                cvm[slot] = b.ve.n
        def _lb(sync, o0=o0, wb=wb, um=um):
            gate(sync, um)
            prev = {wring.sems[wb]: wring.sems[wb].n}
            gate(sync, prev)
            wring.issue(sync.dma_start(out=brow32[wb][0:1, 0:512],
                                       in_=selb_dram[0:1, o0:o0 + 512]), wb)
            wring.issue(sync.dma_start(out=brow32[wb][0:1, 512:1024],
                                       in_=compb_dram[0:1, o0:o0 + 512]), wb)
        b.syn(_lb)
        def _cb(v, I, wb=wb):
            wring.wait_slot(v, wb)
            I(v.tensor_copy(brow[wb][:, :], brow32[wb][:, :]))
        b.vec(_cb)
        slab_cast_marks[ocb] = {b.ve: b.ve.n}

    load_slab(0)
    evac_marks = {}
    tail_marks = {}
    for ocb in range(NOB):
        wb = ocb % 2
        if ocb + 1 < NOB:
            pass
        for tc in range(NTC):
            tb2 = tc % 2
            it = ocb * NTC + tc
            def _mlg(te, tc=tc, wb=wb, ocb=ocb, tb2=tb2, it=it):
                if tc == 0:
                    gate(te, slab_cast_marks[ocb]); gate(te, xm); gate(te, c0m)
                em = evac_marks.get((it - 2) % 2 if it >= 2 else None)
                if it >= 2:
                    gate(te, evac_marks.get(it - 2, {}))
                for dc in range(NDC):
                    te.matmul(p_lg[tb2][:, :], xt[dc][:, tc * 128:(tc + 1) * 128],
                              selt[wb][dc][:, :], start=(dc == 0), stop=False)
                te.matmul(p_lg[tb2][:, :], ones1[:, :], brow[wb][0:1, 0:512],
                          start=False, stop=True)
                for dc in range(NDC):
                    te.matmul(p_cp[tb2][:, :], xt[dc][:, tc * 128:(tc + 1) * 128],
                              compt[wb][dc][:, :], start=(dc == 0), stop=False)
                return te.matmul(p_cp[tb2][:, :], ones1[:, :], brow[wb][0:1, 512:1024],
                                 start=False, stop=True)
            b.ten(_mlg)
            tm = snap(b.te)
            if tc == 0 and ocb + 1 < NOB:
                slab_used_marks[ocb - 1] = tm
                load_slab(ocb + 1)
            def _ev(s, I, tm=tm, tb2=tb2, it=it):
                gate(s, tm)
                gate(s, tail_marks.get(it - 2, {}))
                I(s.activation(lg[tb2][:, :], p_lg[tb2][:, :], AF.Copy))
                s.wait_ge(b.sc.h, b.sc.n)
                I(s.activation(csb[tb2][:, :], p_cp[tb2][:, :], AF.Copy))
            b.scl(_ev)
            evac_marks[it] = snap(b.sc)
            evm = evac_marks[it]
            def _mx(v, I, evm=evm, tb2=tb2):
                gate(v, evm)
                I(v.tensor_reduce(mx[tb2][:, :], lg[tb2][:, :].rearrange("p (o n) -> p o n", n=8),
                                  AX.X, ALU.max))
            b.vec(_mx)
            mxm = snap(b.ve)
            def _sub(g, I, mxm=mxm, tb2=tb2):
                gate(g, mxm)
                mb = mx[tb2][:, :].unsqueeze(2).broadcast_to([128, 64, 8])
                I(g.tensor_tensor(dd[tb2][:, :].rearrange("p (o n) -> p o n", n=8),
                                  lg[tb2][:, :].rearrange("p (o n) -> p o n", n=8), mb,
                                  ALU.subtract))
            b.gpc(_sub)
            subm = snap(b.gc)
            def _sgn(s, I, subm=subm, tb2=tb2):
                gate(s, subm)
                I(s.sign(sgn[tb2][:, :], dd[tb2][:, :]))
            b.scl(_sgn)
            sgm = snap(b.sc)
            def _cm(g, I, sgm=sgm, tb2=tb2):
                gate(g, sgm)
                I(g.tensor_mul(cm[tb2][:, :], csb[tb2][:, :], sgn[tb2][:, :]))
            b.gpc(_cm)
            cmm = snap(b.gc)
            def _red(v, I, cmm=cmm, tb2=tb2):
                gate(v, cmm)
                I(v.tensor_reduce(r1[tb2][:, :], csb[tb2][:, :].rearrange("p (o n) -> p o n", n=8),
                                  AX.X, ALU.add))
                v.wait_ge(b.ve.h, b.ve.n)
                I(v.tensor_reduce(r2[tb2][:, :], cm[tb2][:, :].rearrange("p (o n) -> p o n", n=8),
                                  AX.X, ALU.add))
                v.wait_ge(b.ve.h, b.ve.n)
                I(v.tensor_add(gp[tb2][:, :], r1[tb2][:, :], r2[tb2][:, :]))
            b.vec(_red)
            redm = snap(b.ve)
            def _si(s, I, redm=redm, tb2=tb2):
                gate(s, redm)
                I(s.activation(s4[tb2][:, :], gp[tb2][:, :], AF.Sigmoid))
            b.scl(_si)
            sim_ = snap(b.sc)
            def _g3(v, I, sim_=sim_, it=it, tb2=tb2):
                gate(v, sim_)
                if it >= 2:
                    oring.wait_slot(v, it % 2)
                I(v.tensor_mul(evo[it % 2][:, :], gp[tb2][:, :], s4[tb2][:, :]))
            b.vec(_g3)
            g3m = snap(b.ve)
            def _st(g, tc=tc, ocb=ocb, g3m=g3m, it=it):
                gate(g, g3m)
                oring.issue(g.dma_start(
                    out=out_dram[tc * 128:(tc + 1) * 128, ocb * 64:(ocb + 1) * 64],
                    in_=evo[it % 2][:, :]), it % 2)
            b.gpd(_st)
            tail_marks[it] = {b.ve: b.ve.n, b.gc: b.gc.n}
        slab_used_marks[ocb] = snap(b.te)
    return oring.marks()


def host_consts(T):
    import ml_dtypes
    ar = np.arange(CH)
    LTincl = (ar[:, None] <= ar[None, :]).astype(np.float32)
    LTstrict = (ar[:, None] < ar[None, :]).astype(np.float32)
    B63 = np.zeros((CH, CH), np.float32)
    B63[:CH // 2, :] = 1.0
    R1 = B63 - LTstrict
    R2 = LTincl - B63
    colL = -np.ones((CH, 1), np.float32)
    colQ = B63[:, :1] - 1.0
    colM = -B63[:, :1]
    Rfeat = np.concatenate([R1, R2, colL, colQ, colM], axis=1)
    c = {
        "c_ones128": np.ones((128, 128), np.float32),
        "c_rfeat": Rfeat.astype(ml_dtypes.bfloat16),
        "c_rtok": R2.astype(ml_dtypes.bfloat16),
        "c_ident": np.eye(128, dtype=ml_dtypes.bfloat16),
        "c_swkv": (ar[:, None] < ar[None, :]).astype(ml_dtypes.bfloat16),
        "c_bd2": np.concatenate(
            [np.concatenate([np.ones((64, 1)), np.zeros((64, 1))]),
             np.concatenate([np.zeros((64, 1)), np.ones((64, 1))])],
            axis=1).astype(ml_dtypes.bfloat16),
        "c_selh": np.concatenate(
            [np.concatenate([np.ones((1, 64)), np.zeros((1, 64))], 1),
             np.concatenate([np.zeros((1, 64)), np.ones((1, 64))], 1)],
            0).astype(ml_dtypes.bfloat16),
        "c_minf": np.where(ar[:, None] > ar[None, :], -1e30, 0.0).astype(np.float32),
    }
    slopes = (2.0 ** (-8.0 / H)) ** np.arange(1, H + 1)
    kcoord = np.arange(T, dtype=np.float32)
    iota2 = np.zeros((2, 128), np.float32)
    iota2[0] = np.arange(128)
    iota2[1] = 1.0
    slopes2 = np.zeros((2, 512), np.float32)
    for h in range(H):
        for kc in range(min(32, T // 128)):
            slopes2[0, h * 32 + kc] = slopes[h]
            slopes2[1, h * 32 + kc] = slopes[h] * 128.0 * kc
    qbn = np.zeros((H, 1, T), np.float32)
    for h in range(H):
        qbn[h, 0] = -slopes[h] * kcoord
    c["c_iota2"] = iota2
    c["c_slopes2"] = slopes2
    c["c_qbn"] = qbn
    return c


def stage_l0fattn(b, tag, q_dram, k_dram, v_dram, out_dram, T, inmarks):
    NKC = T // CH
    G = 4
    NQB = T // 512
    EXPW = min(1024, T)
    NHALF = T // EXPW
    lr = DmaRing(b, f"{tag}ld", 2)
    lrv = DmaRing(b, f"{tag}lv", 2)
    oring = DmaRing(b, f"{tag}os", 2)

    qt = b.sb(f"{tag}q", [128, T], BF16)
    kt = b.sb(f"{tag}k", [128, T], BF16)
    vv = [b.sb(f"{tag}vv{i}", [128, 128], BF16) for i in range(2)]
    vn = [b.sb(f"{tag}vn{i}", [128, 128], BF16) for i in range(G)]
    E1 = [b.sb(f"{tag}E1_{i}", [128, T], BF16) for i in range(G)]
    E2 = [b.sb(f"{tag}E2_{i}", [128, T], BF16) for i in range(G)]
    zp = b.sb(f"{tag}zp", [128, 2 * NHALF], F32)
    zz = b.sb(f"{tag}zz", [128, 2], F32)
    rz = b.sb(f"{tag}rz", [128, 2], F32)
    acc = b.sb(f"{tag}ac", [128, T], F32)
    evo = [b.sb(f"{tag}ev{i}", [128, 512], BF16) for i in range(2)]
    p_s1 = b.ps(f"{tag}ps1", [128, 1024], F32)
    p_s2 = b.ps(f"{tag}ps2", [128, 1024], F32)
    p_av = [b.ps(f"{tag}pav{i}", [128, 512], F32) for i in range(2)]
    p_aw = [b.ps(f"{tag}paw{i}", [128, 512], F32) for i in range(2)]

    prev_hp = dict(inmarks)
    for hp in range(NDC):
        pmh = dict(prev_hp)
        def _ldq(sync, hp=hp, pmh=pmh):
            gate(sync, pmh)
            lr.issue(sync.dma_start(out=qt[:, :], in_=q_dram[hp * 128:(hp + 1) * 128, :]), 0)
            lr.issue(sync.dma_start(out=kt[:, :], in_=k_dram[hp * 128:(hp + 1) * 128, :]), 0)
        b.syn(_ldq)
        qm = lr.marks()
        prev_g_av = {}
        last_exp = {}
        last_z = {}
        acc_marks = {}
        for kg in range(NKC // G):
            for ki in range(G):
                kc = kg * G + ki
                k0 = kc * CH
                def _ldv(sync, hp=hp, k0=k0, kc=kc):
                    zm = last_z.get(kc % 2)
                    if zm:
                        gate(sync, zm)
                    lrv.issue(sync.dma_start(out=vv[kc % 2][:, :],
                                             in_=v_dram[k0:k0 + CH, hp * 128:(hp + 1) * 128]),
                              kc)
                b.syn(_ldv)
                vm_ = {lrv.sems[kc % 2]: lrv.sems[kc % 2].n}
                for half in range(NHALF):
                    def _s1(te, hp=hp, kc=kc, k0=k0, half=half, qm=qm, ki=ki):
                        if half == 0 and kc == 0:
                            gate(te, qm)
                        em = last_exp.get(half % 1 or 0)
                        if em:
                            gate(te, em)
                        last = None
                        for sub in range(EXPW // 512):
                            q0 = half * EXPW + sub * 512
                            te.matmul(p_s1[:, sub * 512:(sub + 1) * 512],
                                      kt[0:64, k0:k0 + CH], qt[0:64, q0:q0 + 512],
                                      start=True, stop=True, tile_position=(0, 0))
                            last = te.matmul(p_s2[:, sub * 512:(sub + 1) * 512],
                                             kt[64:128, k0:k0 + CH], qt[64:128, q0:q0 + 512],
                                             start=True, stop=True, tile_position=(64, 0))
                        return last
                    b.ten(_s1)
                    tm = snap(b.te)
                    def _ex(s, I, ki=ki, half=half, tm=tm, kc=kc):
                        gate(s, tm)
                        if half == 0:
                            zm = last_z.get((kc - 1) % 2)
                            if zm:
                                gate(s, zm)
                            am = prev_g_av.get(ki)
                            if am:
                                gate(s, am)
                        I(s.activation(E1[ki][:, half * EXPW:(half + 1) * EXPW],
                                       p_s1[:, 0:EXPW], AF.Exp,
                                       accum_out=zp[:, 2 * half:2 * half + 1]))
                        s.wait_ge(b.sc.h, b.sc.n)
                        I(s.activation(E2[ki][:, half * EXPW:(half + 1) * EXPW],
                                       p_s2[:, 0:EXPW], AF.Exp,
                                       accum_out=zp[:, 2 * half + 1:2 * half + 2]))
                    b.scl(_ex)
                    last_exp[0] = snap(b.sc)
                exm = snap(b.sc)
                def _z(v, I, ki=ki, exm=exm, vm_=vm_, kc=kc):
                    gate(v, exm)
                    gate(v, vm_)
                    am = prev_g_av.get(ki)
                    if am:
                        gate(v, am)
                    z3 = zp[:, :].rearrange("p (a c) -> p c a", c=2)
                    I(v.tensor_reduce(zz[:, :], z3, AX.X, ALU.add))
                    v.wait_ge(b.ve.h, b.ve.n)
                    I(v.reciprocal(rz[:, :], zz[:, :]))
                    v.wait_ge(b.ve.h, b.ve.n)
                    v3 = vv[kc % 2][:, :].rearrange("p (h j) -> p h j", h=2)
                    r3 = rz[:, :].unsqueeze(2).broadcast_to([128, 2, 64])
                    I(v.tensor_tensor(vn[ki][:, :].rearrange("p (h j) -> p h j", h=2),
                                      v3, r3, ALU.mult))
                b.vec(_z)
                last_z[kc % 2] = snap(b.ve)
            egm = snap(b.ve)
            new_av = {}
            for qb in range(NQB):
                pb = qb % 2
                def _av(te, hp=hp, kg=kg, qb=qb, pb=pb, egm=egm):
                    if qb == 0:
                        gate(te, egm)
                    am = acc_marks.get((qb - 2) % NQB if qb >= 2 else None)
                    if qb >= 2:
                        gate(te, acc_marks.get(qb - 2, {}))
                    elif kg > 0:
                        gate(te, acc_marks.get(NQB - 2 + qb, {}))
                    for ki in range(G):
                        te.matmul(p_av[pb][0:64, :], vn[ki][:, 0:64],
                                  E1[ki][:, qb * 512:(qb + 1) * 512],
                                  start=(ki == 0), stop=(ki == G - 1), tile_position=(0, 0))
                    last = None
                    for ki in range(G):
                        last = te.matmul(p_aw[pb][64:128, :], vn[ki][:, 64:128],
                                         E2[ki][:, qb * 512:(qb + 1) * 512],
                                         start=(ki == 0), stop=(ki == G - 1),
                                         tile_position=(0, 64))
                    return last
                b.ten(_av)
                avm = snap(b.te)
                if qb == NQB - 1:
                    new_av = {ki: avm for ki in range(G)}
                def _acc(v, I, kg=kg, qb=qb, pb=pb, avm=avm):
                    gate(v, avm)
                    if kg == 0:
                        I(v.tensor_copy(acc[0:64, qb * 512:(qb + 1) * 512], p_av[pb][0:64, :]))
                        v.wait_ge(b.ve.h, b.ve.n)
                        I(v.tensor_copy(acc[64:128, qb * 512:(qb + 1) * 512], p_aw[pb][64:128, :]))
                    else:
                        I(v.tensor_add(acc[0:64, qb * 512:(qb + 1) * 512],
                                       acc[0:64, qb * 512:(qb + 1) * 512], p_av[pb][0:64, :]))
                        v.wait_ge(b.ve.h, b.ve.n)
                        I(v.tensor_add(acc[64:128, qb * 512:(qb + 1) * 512],
                                       acc[64:128, qb * 512:(qb + 1) * 512], p_aw[pb][64:128, :]))
                b.vec(_acc)
                acc_marks[qb] = snap(b.ve)
            prev_g_av = new_av
        accm = snap(b.ve)
        for qb in range(NQB):
            def _cv(v, I, qb=qb, accm=accm, oit=hp * NQB + qb):
                if qb == 0:
                    gate(v, accm)
                if oit >= 2:
                    oring.wait_slot(v, oit % 2)
                I(v.tensor_copy(evo[oit % 2][:, :], acc[:, qb * 512:(qb + 1) * 512]))
            b.vec(_cv)
            cvm_ = snap(b.ve)
            def _st(g, hp=hp, qb=qb, cvm_=cvm_, oit=hp * NQB + qb):
                gate(g, cvm_)
                oring.issue(g.dma_start(
                    out=out_dram[hp * 128:(hp + 1) * 128, qb * 512:(qb + 1) * 512],
                    in_=evo[oit % 2][:, :]), oit % 2)
            b.gpd(_st)
        prev_hp = {b.ve: b.ve.n, b.sc: b.sc.n, b.te: b.te.n, **oring.marks()}
    return oring.marks()



def stage_ffntail(b, tag, z1_dram, wff_dram, out_dram, T, ones_f32, omarks, inmarks):
    tb = 512
    ntb = T // tb
    wt, wm = load_cast_fm(b, f"{tag}w", wff_dram, NDC, C)
    zt, zm = load_fm_raw(b, f"{tag}z", z1_dram, NDC, T, BF16, wait=inmarks)
    oring = DmaRing(b, f"{tag}os", 2)
    p_z = [b.ps(f"{tag}pz{i}", [128, tb], F32) for i in range(2)]
    p_ss = [b.ps(f"{tag}ss{i}", [128, tb], F32) for i in range(2)]
    z2sb = [b.sb(f"{tag}z2{oc}", [128, tb], BF16) for oc in range(NDC)]
    zsq = b.sb(f"{tag}zq", [128, tb], F32)
    rn = b.sb(f"{tag}rn", [128, tb], F32)
    t1 = b.sb(f"{tag}t1", [128, tb], F32)
    te_ = b.sb(f"{tag}te", [128, tb], F32)
    t3 = b.sb(f"{tag}t3", [128, tb], F32)
    s4 = b.sb(f"{tag}s4", [128, tb], F32)
    evo = [b.sb(f"{tag}ev{i}", [128, tb], BF16) for i in range(2)]
    prev = {}
    for t in range(ntb):
        pm = dict(prev)
        for oc in range(NDC):
            pb = oc % 2
            def _mm(te, t=t, oc=oc, pb=pb, pm=pm):
                if t == 0 and oc == 0:
                    gate(te, wm); gate(te, zm); gate(te, omarks)
                if oc < 2:
                    gate(te, pm)
                gate(te, {b.sc: b.sc.n})
                last = None
                for dc in range(NDC):
                    last = te.matmul(p_z[pb][:, :], wt[dc][:, oc * 128:(oc + 1) * 128],
                                     zt[dc][:, t * tb:(t + 1) * tb],
                                     start=(dc == 0), stop=(dc == NDC - 1))
                return last
            b.ten(_mm)
            tm = snap(b.te)
            def _ev(s, I, oc=oc, pb=pb, tm=tm, pm=pm):
                gate(s, tm)
                gate(s, pm)
                I(s.activation(z2sb[oc][:, :], p_z[pb][:, :], AF.Copy))
                s.wait_ge(b.sc.h, b.sc.n)
                I(s.activation(zsq[:, :], p_z[pb][:, :], AF.Square))
            b.scl(_ev)
            evm = snap(b.sc)
            def _ssm(te, t=t, oc=oc, evm=evm):
                gate(te, evm)
                return te.matmul(p_ss[t % 2][:, :], ones_f32[:, :], zsq[:, :],
                                 start=(oc == 0), stop=(oc == NDC - 1))
            b.ten(_ssm)
        ssm = snap(b.te)
        def _rn(s, I, t=t, ssm=ssm):
            gate(s, ssm)
            gate(s, {b.ve: b.ve.n})
            I(s.activation(rn[:, :], p_ss[t % 2][:, :], AF.Ln))
            s.wait_ge(b.sc.h, b.sc.n)
            I(s.activation(rn[:, :], rn[:, :], AF.Exp, scale=-0.5))
        b.scl(_rn)
        rnm = snap(b.sc)
        for oc in range(NDC):
            oit = t * NDC + oc
            def _g(v, I, oc=oc, rnm=rnm, oit=oit):
                gate(v, rnm)
                gate(v, {b.sc: b.sc.n})
                I(v.tensor_mul(t1[:, :], z2sb[oc][:, :], rn[:, :]))
            b.vec(_g)
            g1m = snap(b.ve)
            def _e(s, I, g1m=g1m, oit=oit):
                gate(s, g1m)
                if oit >= 2:
                    oring.wait_slot(s, oit % 2)
                I(s.activation(t3[:, :], t1[:, :], AF.Gelu))
                s.wait_ge(b.sc.h, b.sc.n)
                I(s.activation(evo[oit % 2][:, :], t3[:, :], AF.Silu))
            b.scl(_e)
            g3m = snap(b.sc)
            def _st(g, t=t, oc=oc, g3m=g3m, oit=oit):
                gate(g, g3m)
                oring.issue(g.dma_start(
                    out=out_dram[oc * 128:(oc + 1) * 128, t * tb:(t + 1) * tb],
                    in_=evo[oit % 2][:, :]), oit % 2)
            b.gpd(_st)
        prev = {b.ve: b.ve.n, b.sc: b.sc.n, b.te: b.te.n, **oring.marks()}
    return oring.marks()



def stage_combine(b, tag, x_dram, s_dram, g_dram, out_dram, T, inmarks):
    tb = 512
    ntb = T // tb
    lrx = DmaRing(b, f"{tag}lx", 2)
    lrs = DmaRing(b, f"{tag}ls", 2)
    oring = DmaRing(b, f"{tag}os", 2)
    xb = [b.sb(f"{tag}x{i}", [128, tb], F32) for i in range(2)]
    sb_ = [b.sb(f"{tag}s{i}", [128, tb], BF16) for i in range(2)]
    gb = [b.sb(f"{tag}g{i}", [128, tb], BF16) for i in range(2)]
    tt = b.sb(f"{tag}t", [128, tb], F32)
    ob = [b.sb(f"{tag}o{i}", [128, tb], F32) for i in range(2)]
    vm_hist = [{}, {}]
    it = 0
    for t in range(ntb):
        for dc in range(NDC):
            sl = it % 2
            def _ld(sync, t=t, dc=dc, sl=sl, pm=vm_hist[sl]):
                if it == 0:
                    gate(sync, inmarks)
                gate(sync, pm)
                lrx.issue(sync.dma_start(out=xb[sl][:, :],
                                         in_=x_dram[dc * 128:(dc + 1) * 128, t * tb:(t + 1) * tb]), sl)
                lrs.issue(sync.dma_start(out=sb_[sl][:, :],
                                         in_=s_dram[dc * 128:(dc + 1) * 128, t * tb:(t + 1) * tb]), sl)
                lrs.issue(sync.dma_start(out=gb[sl][:, :],
                                         in_=g_dram[dc * 128:(dc + 1) * 128, t * tb:(t + 1) * tb]), sl)
            b.syn(_ld)
            lm = {lrx.sems[sl]: lrx.sems[sl].n, lrs.sems[sl]: lrs.sems[sl].n}
            def _v(v, I, sl=sl, lm=lm, it=it):
                gate(v, lm)
                if it >= 2:
                    oring.wait_slot(v, sl)
                I(v.tensor_mul(tt[:, :], sb_[sl][:, :], gb[sl][:, :]))
                v.wait_ge(b.ve.h, b.ve.n)
                I(v.tensor_add(ob[sl][:, :], xb[sl][:, :], tt[:, :]))
            b.vec(_v)
            vm = snap(b.ve)
            vm_hist[sl] = vm
            def _st(g, t=t, dc=dc, sl=sl, vm=vm):
                gate(g, vm)
                oring.issue(g.dma_start(
                    out=out_dram[dc * 128:(dc + 1) * 128, t * tb:(t + 1) * tb],
                    in_=ob[sl][:, :]), sl)
            b.gpd(_st)
            it += 1
    return oring.marks()


def stage_l1attn(b, tag, q_dram, k_dram, v_dram, iota2_dram, slopes2_dram,
                 qbn_dram, minf_dram, out_dram, T, inmarks):
    NQT = T // 512
    NKC = T // CH
    lr = DmaRing(b, f"{tag}ld", 2)
    lrq = DmaRing(b, f"{tag}lq", 2)
    oring = DmaRing(b, f"{tag}os", 2)
    qt_ = b.sb(f"{tag}q", [128, T], BF16)
    kt_ = b.sb(f"{tag}k", [128, T], BF16)
    v_hp = b.sb(f"{tag}v", [128, T], BF16)
    iota2 = b.sb(f"{tag}i2", [2, 128], F32)
    slopes2 = b.sb(f"{tag}s2", [2, 512], F32)
    bcolmat = b.sb(f"{tag}bc", [128, 512], F32)
    qbs = [b.sb(f"{tag}qb{i}", [1, 512], F32) for i in range(2)]
    minf = b.sb(f"{tag}mi", [128, 128], F32)
    E = [b.sb(f"{tag}E{i}", [128, 512], BF16) for i in range(8)]
    onesb = b.sb(f"{tag}on", [128, 32], BF16)
    ones1f = b.sb(f"{tag}o1f", [1, 128], F32)
    ones1 = b.sb(f"{tag}o1", [1, 128], BF16)
    zrow = b.sb(f"{tag}zr", [1, 512], F32)
    rzrow = b.sb(f"{tag}rz", [1, 512], F32)
    rzb = b.sb(f"{tag}rb", [1, 512], BF16)
    avsb = b.sb(f"{tag}av", [128, 512], F32)
    evo = [b.sb(f"{tag}ev{i}", [128, 512], BF16) for i in range(2)]
    p_s = [[b.ps(f"{tag}ps{h}{i}", [128, 512], F32) for i in range(2)] for h in range(2)]
    p_av = [b.ps(f"{tag}pa{i}", [128, 512], F32) for i in range(2)]
    p_z = [b.ps(f"{tag}pz{i}", [32, 512], F32) for i in range(2)]

    def _c0(g, I):
        I(g.memset(onesb[:, :], 1.0))
        I(g.memset(ones1[:, :], 1.0))
        I(g.memset(ones1f[:, :], 1.0))
    b.gpc(_c0)
    c0m = snap(b.gc)
    def _ldm(sync):
        gate(sync, inmarks)
        lr.issue(sync.dma_start(out=minf[:, :], in_=minf_dram[:, :]), 0)
        lr.issue(sync.dma_start(out=iota2[:, :], in_=iota2_dram[:, :]), 0)
        lr.issue(sync.dma_start(out=slopes2[:, :], in_=slopes2_dram[:, :]), 0)
    b.syn(_ldm)
    ldm0 = lr.marks()
    def _bcol(te, ldm0=ldm0):
        gate(te, ldm0)
        return te.matmul(p_s[0][0][:, :], iota2[:, :], slopes2[:, :], start=True, stop=True)
    b.ten(_bcol)
    bm0 = snap(b.te)
    def _bev(v, I, bm0=bm0):
        gate(v, bm0)
        I(v.tensor_copy(bcolmat[:, :], p_s[0][0][:, :]))
    b.vec(_bev)
    bcolm = snap(b.ve)

    prev_hp = dict(inmarks)
    for hp in range(NDC):
        pmh = dict(prev_hp)
        def _ldq(sync, hp=hp, pmh=pmh):
            gate(sync, pmh)
            if hp == 0:
                gate(sync, ldm0)
                gate(sync, bcolm)
            lr.issue(sync.dma_start(out=qt_[:, :], in_=q_dram[hp * 128:(hp + 1) * 128, :]), 0)
            lr.issue(sync.dma_start(out=kt_[:, :], in_=k_dram[hp * 128:(hp + 1) * 128, :]), 0)
            for cc in range(T // 128):
                lr.issue(sync.dma_start(out=v_hp[:, cc * 128:(cc + 1) * 128],
                                        in_=v_dram[cc * 128:(cc + 1) * 128,
                                                   hp * 128:(hp + 1) * 128]), 0)
        b.syn(_ldq)
        qm = lr.marks()
        prev_qt_sc = {}
        for qt in range(NQT):
            q0 = qt * 512
            nkc = qt * 4 + 4
            def _ldqb(sync, hp=hp, qt=qt, q0=q0):
                gate(sync, {b.te: b.te.n})
                for hh in range(2):
                    lrq.issue(sync.dma_start(out=qbs[hh][:, :],
                                             in_=qbn_dram[hp * 2 + hh, 0:1, q0:q0 + 512]), qt)
            b.syn(_ldqb)
            qbm = {lrq.sems[qt % 2]: lrq.sems[qt % 2].n}
            exp_marks = {}
            av_marks = {}
            for kc in range(nkc):
                k0 = kc * CH
                diag_j = kc - qt * 4
                c0 = max(0, diag_j) * 128
                for hh in range(2):
                    hsl = slice(hh * 64, (hh + 1) * 64)
                    pb = kc % 2
                    eb = E[(kc % 4) * 2 + hh]
                    def _s(te, hp=hp, qt=qt, kc=kc, hh=hh, hsl=hsl, q0=q0, k0=k0,
                           c0=c0, qm=qm, c0m=c0m, qbm=qbm, pb=pb, pmh=pmh):
                        if qt == 0 and kc == 0 and hh == 0:
                            gate(te, qm); gate(te, c0m); gate(te, pmh)
                        if kc == 0 and hh == 0:
                            gate(te, qbm)
                            if prev_qt_sc:
                                gate(te, prev_qt_sc)
                        pm = exp_marks.get((kc - 2, hh))
                        if pm:
                            gate(te, pm)
                        elif kc < 2 and qt > 0:
                            pm2 = prev_exp.get((hh, pb))
                            if pm2:
                                gate(te, pm2)
                        te.matmul(p_s[hh][pb][:, c0:512], kt_[hsl, k0:k0 + CH],
                                  qt_[hsl, q0 + c0:q0 + 512],
                                  start=True, stop=False,
                                  tile_position=(hh * 64, 0))
                        return te.matmul(p_s[hh][pb][:, c0:512],
                                         ones1f[:, :], qbs[hh][:, c0:512],
                                         start=False, stop=True)
                    b.ten(_s)
                    sm_ = snap(b.te)
                    if diag_j >= 0:
                        def _mi(v, I, hh=hh, c0=c0, sm_=sm_, pb=pb):
                            gate(v, sm_)
                            I(v.tensor_add(p_s[hh][pb][:, c0:c0 + 128],
                                           p_s[hh][pb][:, c0:c0 + 128], minf[:, :]))
                        b.vec(_mi)
                        mim = {b.ve: b.ve.n}
                    else:
                        mim = sm_
                    hidx = hp * 2 + hh
                    def _ex(s, I, hh=hh, c0=c0, mim=mim, eb=eb, hidx=hidx, kc=kc, pb=pb):
                        gate(s, mim)
                        am = av_marks.get((kc - 4, hh))
                        if am:
                            gate(s, am)
                        elif kc < 4 and qt > 0:
                            am2 = prev_av.get((hh, kc % 4))
                            if am2:
                                gate(s, am2)
                        I(s.activation(eb[:, c0:512], p_s[hh][pb][:, c0:512], AF.Exp,
                                       bias=bcolmat[:, hidx * 32 + kc:hidx * 32 + kc + 1]))
                    b.scl(_ex)
                    exp_marks[(kc, hh)] = snap(b.sc)
                    def _av(te, hp=hp, kc=kc, hh=hh, k0=k0, c0=c0, eb=eb, nkc=nkc, qt=qt):
                        gate(te, exp_marks[(kc, hh)])
                        if kc == 0 and prev_qt_sc:
                            gate(te, prev_qt_sc)
                        te.matmul(p_av[hh][0:64, c0:512],
                                  v_hp[:, kc * 128 + hh * 64:kc * 128 + hh * 64 + 64],
                                  eb[:, c0:512],
                                  start=(kc == 0), stop=(kc == nkc - 1),
                                  tile_position=(0, 0))
                        return te.matmul(p_z[hh][:, c0:512], onesb[:, :], eb[:, c0:512],
                                         start=(kc == 0), stop=(kc == nkc - 1))
                    b.ten(_av)
                    av_marks[(kc, hh)] = snap(b.te)
            prev_exp = {(hh, pbv): exp_marks.get((kcv, hh))
                        for hh in range(2) for kcv in range(nkc) for pbv in [kcv % 2]}
            prev_av = {(hh, kcv % 4): av_marks.get((kcv, hh))
                       for hh in range(2) for kcv in range(nkc)}
            for hh in range(2):
                oit = (hp * NQT + qt) * 2 + hh
                avm = av_marks[(nkc - 1, 1)]
                def _n1(s, I, hh=hh, avm=avm):
                    gate(s, avm)
                    gate(s, {b.ve: b.ve.n})
                    I(s.activation(zrow[:, :], p_z[hh][0:1, :], AF.Copy))
                    s.wait_ge(b.sc.h, b.sc.n)
                    I(s.activation(avsb[0:64, :], p_av[hh][0:64, :], AF.Copy))
                b.scl(_n1)
                n1m = snap(b.sc)
                def _n2(v, I, n1m=n1m, oit=oit, hh=hh):
                    gate(v, n1m)
                    I(v.reciprocal(rzrow[:, :], zrow[:, :]))
                    v.wait_ge(b.ve.h, b.ve.n)
                    I(v.tensor_copy(rzb[:, :], rzrow[:, :]))
                b.vec(_n2)
                n2m = snap(b.ve)
                def _n3(te, n2m=n2m, hh=hh):
                    gate(te, n2m)
                    return te.matmul(p_s[hh][0][:, :], ones1[:, :], rzb[:, :],
                                     start=True, stop=True)
                b.ten(_n3)
                n3m = snap(b.te)
                def _n5(v, I, oit=oit, n3m=n3m, hh=hh):
                    gate(v, n3m)
                    if oit >= 2:
                        oring.wait_slot(v, oit % 2)
                    I(v.tensor_mul(evo[oit % 2][0:64, :], avsb[0:64, :],
                                   p_s[hh][0][0:64, :]))
                b.vec(_n5)
                n5m = snap(b.ve)
                def _st(g, hp=hp, qt=qt, hh=hh, oit=oit, n5m=n5m):
                    gate(g, n5m)
                    oring.issue(g.dma_start(
                        out=out_dram[hp * 128 + hh * 64:hp * 128 + hh * 64 + 64,
                                     qt * 512:(qt + 1) * 512],
                        in_=evo[oit % 2][0:64, :]), oit % 2)
                b.gpd(_st)
            prev_qt_sc = {b.sc: b.sc.n, b.ve: b.ve.n}
        prev_hp = {b.ve: b.ve.n, b.sc: b.sc.n, b.te: b.te.n, **oring.marks()}
    return oring.marks()



def stage_moe(b, tag, xn_dram, selw_dram, selb_dram, comp_dram, compb_dram,
              out_dram, T, inmarks):
    NTC = T // 128
    NOB = 16
    xt, xm = load_fm_raw(b, f"{tag}x", xn_dram, NDC, T, BF16, wait=inmarks)
    oring = DmaRing(b, f"{tag}os", 2)
    wring = DmaRing(b, f"{tag}wl", 2)
    ws32 = [b.sb(f"{tag}w32{i}", [128, 512], F32) for i in range(2)]
    selt = [[b.sb(f"{tag}se{i}_{dc}", [128, 512], BF16) for dc in range(NDC)] for i in range(2)]
    compt = [[b.sb(f"{tag}co{i}_{dc}", [128, 512], BF16) for dc in range(NDC)] for i in range(2)]
    brow = [b.sb(f"{tag}br{i}", [1, 1024], BF16) for i in range(2)]
    brow32 = b.sb(f"{tag}br32", [1, 1024], F32)
    ones1 = b.sb(f"{tag}o1", [1, 128], BF16)
    lg = b.sb(f"{tag}lg", [128, 512], F32)
    mx = b.sb(f"{tag}mx", [128, 64], F32)
    dd = b.sb(f"{tag}dd", [128, 512], F32)
    sgn = b.sb(f"{tag}sg", [128, 512], BF16)
    csb = b.sb(f"{tag}cs", [128, 512], BF16)
    cm = b.sb(f"{tag}cm", [128, 512], BF16)
    r1 = b.sb(f"{tag}r1", [128, 64], F32)
    r2 = b.sb(f"{tag}r2", [128, 64], F32)
    gp = b.sb(f"{tag}gp", [128, 64], F32)
    s4 = b.sb(f"{tag}s4", [128, 64], F32)
    evo = [b.sb(f"{tag}ev{i}", [128, 64], BF16) for i in range(2)]
    p_lg = b.ps(f"{tag}plg", [128, 512], F32)
    p_cp = b.ps(f"{tag}pcp", [128, 512], F32)

    def _c0(g, I):
        I(g.memset(ones1[:, :], 1.0))
    b.gpc(_c0)
    c0m = snap(b.gc)

    prev_ocb = {}
    for ocb in range(NOB):
        o0 = ocb * 512
        pmo = dict(prev_ocb)
        wb = ocb % 2
        cvm = []
        for kind, dst in (("sel", selt[wb]), ("comp", compt[wb])):
            src = selw_dram if kind == "sel" else comp_dram
            for dc in range(NDC):
                sl = dc % 2
                def _lw(sync, src=src, dc=dc, o0=o0, sl=sl, pmo=pmo):
                    gate(sync, pmo)
                    gate(sync, {b.ve: b.ve.n})
                    wring.issue(sync.dma_start(out=ws32[sl][:, :],
                                               in_=src[dc * 128:(dc + 1) * 128, o0:o0 + 512]), sl)
                b.syn(_lw)
                def _cw(v, I, dst=dst, dc=dc, sl=sl):
                    wring.wait_slot(v, sl)
                    I(v.tensor_copy(dst[dc][:, :], ws32[sl][:, :]))
                b.vec(_cw)
        def _lb(sync, ocb=ocb, o0=o0, pmo=pmo):
            gate(sync, {b.ve: b.ve.n})
            wring.issue(sync.dma_start(out=brow32[0:1, 0:512], in_=selb_dram[0:1, o0:o0 + 512]), 0)
            wring.issue(sync.dma_start(out=brow32[0:1, 512:1024], in_=compb_dram[0:1, o0:o0 + 512]), 0)
        b.syn(_lb)
        def _cb(v, I, wb=wb):
            wring.wait_slot(v, 0)
            I(v.tensor_copy(brow[wb][:, :], brow32[:, :]))
        b.vec(_cb)
        wm = {b.ve: b.ve.n}

        for tc in range(NTC):
            def _mlg(te, tc=tc, wb=wb, wm=wm, c0m=c0m):
                if tc == 0:
                    gate(te, wm); gate(te, xm); gate(te, c0m)
                gate(te, {b.sc: b.sc.n})
                for dc in range(NDC):
                    te.matmul(p_lg[:, :], xt[dc][:, tc * 128:(tc + 1) * 128],
                              selt[wb][dc][:, :], start=(dc == 0), stop=False)
                te.matmul(p_lg[:, :], ones1[:, :], brow[wb][0:1, 0:512],
                          start=False, stop=True)
                for dc in range(NDC):
                    te.matmul(p_cp[:, :], xt[dc][:, tc * 128:(tc + 1) * 128],
                              compt[wb][dc][:, :], start=(dc == 0), stop=False)
                return te.matmul(p_cp[:, :], ones1[:, :], brow[wb][0:1, 512:1024],
                                 start=False, stop=True)
            b.ten(_mlg)
            tm = snap(b.te)
            def _ev(s, I, tm=tm):
                gate(s, tm)
                gate(s, {b.ve: b.ve.n})
                gate(s, {b.gc: b.gc.n})
                I(s.activation(lg[:, :], p_lg[:, :], AF.Copy))
                s.wait_ge(b.sc.h, b.sc.n)
                I(s.activation(csb[:, :], p_cp[:, :], AF.Copy))
            b.scl(_ev)
            evm = snap(b.sc)
            def _mx(v, I, evm=evm):
                gate(v, evm)
                I(v.tensor_reduce(mx[:, :], lg[:, :].rearrange("p (o n) -> p o n", n=8),
                                  AX.X, ALU.max))
            b.vec(_mx)
            mxm = snap(b.ve)
            def _sub(g, I, mxm=mxm):
                gate(g, mxm)
                mb = mx[:, :].unsqueeze(2).broadcast_to([128, 64, 8])
                I(g.tensor_tensor(dd[:, :].rearrange("p (o n) -> p o n", n=8),
                                  lg[:, :].rearrange("p (o n) -> p o n", n=8), mb,
                                  ALU.subtract))
            b.gpc(_sub)
            subm = snap(b.gc)
            def _sgn(s, I, subm=subm):
                gate(s, subm)
                I(s.sign(sgn[:, :], dd[:, :]))
            b.scl(_sgn)
            sgm = snap(b.sc)
            def _cm(g, I, sgm=sgm):
                gate(g, sgm)
                I(g.tensor_mul(cm[:, :], csb[:, :], sgn[:, :]))
            b.gpc(_cm)
            cmm = snap(b.gc)
            def _red(v, I, cmm=cmm):
                gate(v, cmm)
                I(v.tensor_reduce(r1[:, :], csb[:, :].rearrange("p (o n) -> p o n", n=8),
                                  AX.X, ALU.add))
                v.wait_ge(b.ve.h, b.ve.n)
                I(v.tensor_reduce(r2[:, :], cm[:, :].rearrange("p (o n) -> p o n", n=8),
                                  AX.X, ALU.add))
                v.wait_ge(b.ve.h, b.ve.n)
                I(v.tensor_add(gp[:, :], r1[:, :], r2[:, :]))
            b.vec(_red)
            redm = snap(b.ve)
            def _si(s, I, redm=redm):
                gate(s, redm)
                I(s.activation(s4[:, :], gp[:, :], AF.Sigmoid))
            b.scl(_si)
            sim_ = snap(b.sc)
            oit = ocb * NTC + tc
            def _g3(v, I, sim_=sim_, oit=oit):
                gate(v, sim_)
                if oit >= 2:
                    oring.wait_slot(v, oit % 2)
                I(v.tensor_mul(evo[oit % 2][:, :], gp[:, :], s4[:, :]))
            b.vec(_g3)
            g3m = snap(b.ve)
            def _st(g, tc=tc, ocb=ocb, g3m=g3m, oit=oit):
                gate(g, g3m)
                oring.issue(g.dma_start(
                    out=out_dram[tc * 128:(tc + 1) * 128, ocb * 64:(ocb + 1) * 64],
                    in_=evo[oit % 2][:, :]), oit % 2)
            b.gpd(_st)
        prev_ocb = {b.ve: b.ve.n, b.sc: b.sc.n, b.te: b.te.n, b.gc: b.gc.n,
                    **oring.marks()}
    return oring.marks()


def stage_transpose(b, tag, src, dst, T, inmarks):
    ring = DmaRing(b, f"{tag}ld", 2)
    oring = DmaRing(b, f"{tag}os", 2)
    tin = [b.sb(f"{tag}i{i}", [128, 128], BF16) for i in range(2)]
    tout = [b.sb(f"{tag}o{i}", [128, 128], BF16) for i in range(2)]
    idt = b.sb(f"{tag}id", [128, 128], BF16)
    p_t = b.ps(f"{tag}p", [128, 128], BF16)
    def _mkid(g, I):
        I(g.memset(idt[:, :], 1.0))
        g.wait_ge(b.gc.h, b.gc.n)
        I(g.affine_select(out=idt[:, :], in_=idt[:, :],
                          compare_op=ALU.is_equal, fill=0.0,
                          base=0, pattern=[[-1, 128]], channel_multiplier=1))
    b.gpc(_mkid)
    idm = snap(b.gc)
    it = 0
    vm_hist = [{}, {}]
    for tc in range(T // 128):
        for dc in range(NDC):
            sl = it % 2
            def _ld(sync, tc=tc, dc=dc, sl=sl, pm=vm_hist[sl], it=it):
                if it == 0:
                    gate(sync, inmarks)
                gate(sync, pm)
                ring.issue(sync.dma_start(out=tin[sl][:, :],
                                          in_=src[tc * 128:(tc + 1) * 128,
                                                  dc * 128:(dc + 1) * 128]), sl)
            b.syn(_ld)
            lm = {ring.sems[sl]: ring.sems[sl].n}
            def _tr(te, sl=sl, lm=lm, idm=idm):
                gate(te, lm)
                gate(te, idm)
                gate(te, {b.ve: b.ve.n})
                return te.transpose(p_t[:, :], tin[sl][:, :], idt[:, :])
            b.ten(_tr)
            tm = snap(b.te)
            def _cv(v, I, sl=sl, tm=tm, it=it):
                gate(v, tm)
                if it >= 2:
                    oring.wait_slot(v, sl)
                I(v.tensor_copy(tout[sl][:, :], p_t[:, :]))
            b.vec(_cv)
            vm = snap(b.ve)
            vm_hist[sl] = vm
            def _st(g, tc=tc, dc=dc, sl=sl, vm=vm):
                gate(g, vm)
                oring.issue(g.dma_start(out=dst[dc * 128:(dc + 1) * 128,
                                                tc * 128:(tc + 1) * 128],
                                        in_=tout[sl][:, :]), sl)
            b.gpd(_st)
            it += 1
    return oring.marks()



T_FULL = 4096
N_CORES = 8


def build_program(T):
    hc = host_consts(T)
    b = B()
    nc = b.nc
    D = {}

    def inp(name, shape, dt=F32):
        D[name] = nc.dram_tensor(name, list(shape), dt, kind="ExternalInput").ap()

    inp("x_fm", [C, T])
    for nm in ["x", "w", "k", "v", "r", "g"]:
        inp(f"maa_{nm}", [C, 1])
    inp("w1", [C, 160]); inp("w2f", [160, C]); inp("td", [1, C])
    inp("td_w1", [C, 64]); inp("td_w2", [64, C]); inp("u", [C, 1])
    for wn in ["W_r", "W_k", "W_v", "W_g", "W_o"]:
        inp(wn, [C, C])
    inp("gamma", [C, 1]); inp("beta", [C, 1])
    for wn in ["Wq", "Wk", "Wv", "Wo", "Wff"]:
        inp("f0" + wn, [C, C])
    for wn in ["Wq", "Wk", "Wv", "Wo"]:
        inp("a1" + wn, [C, C])
    inp("selw", [C, 8192]); inp("selb", [1, 8192])
    inp("compw", [C, 8192]); inp("compb", [1, 8192])
    for cn in ["c_ones128", "c_iota2", "c_slopes2", "c_qbn", "c_minf"]:
        inp(cn, hc[cn].shape)
    for cn in ["c_rfeat", "c_rtok", "c_ident", "c_swkv", "c_bd2", "c_selh"]:
        D[cn] = nc.dram_tensor(cn, list(hc[cn].shape), BF16, kind="ExternalInput").ap()

    def scr(name, shape, dt=BF16):
        D[name] = nc.dram_tensor(name, shape, dt, kind="Internal").ap()
    for n in ["xn0", "r_fm", "k_fm", "g_fm", "q0", "k0", "aout", "z1",
              "yng", "s0", "g0", "xn1", "q1", "k1", "att1", "s1", "g1_fm"]:
        scr(n, [C, T])
    for n in ["v_tok", "k_tok", "ew_tok", "v0_tok", "v1_tok", "g1_tok"]:
        scr(n, [T, C])
    scr("x1_fm", [C, T], F32)
    D["out_fm"] = nc.dram_tensor("out_fm", [C, T], F32, kind="ExternalOutput").ap()

    ones_t, ones_m = load_fm_raw(b, "ones", D["c_ones128"], 1, 128, F32)
    ctiles = {}
    cmarks = {}
    for cn, key in [("c_rtok", "rtok"), ("c_rfeat", "rfeat"), ("c_ident", "ident"),
                    ("c_swkv", "swkv"), ("c_bd2", "bd2"), ("c_selh", "selh")]:
        tl, m = load_fm_raw(b, "C" + key, D[cn], 1, hc[cn].shape[1], BF16)
        ctiles[key] = tl[0] if hc[cn].shape[0] >= 128 else tl[0][0:hc[cn].shape[0], :]
        cmarks.update(m)

    b.barrier()
    with b.scope():
        m_ln0 = stage_ln(b, "ln0", D["x_fm"], D["xn0"], T, ones_t[0], ones_m)
    b.barrier()
    mixP = {f"maa_{nm}": D[f"maa_{nm}"] for nm in ["x", "w", "k", "v", "r", "g"]}
    mixP.update({"w1": D["w1"], "w2f": D["w2f"], "td": D["td"],
                 "td_w1": D["td_w1"], "td_w2": D["td_w2"],
                 "W_r": D["W_r"], "W_k": D["W_k"], "W_v": D["W_v"], "W_g": D["W_g"]})
    with b.scope():
        m_mix = stage_l0mix(b, "mx", D["xn0"], mixP,
                            {"r_fm": D["r_fm"], "k_fm": D["k_fm"], "g_fm": D["g_fm"],
                             "v_tok": D["v_tok"], "k_tok": D["k_tok"],
                             "ew_tok": D["ew_tok"]}, T, m_ln0)
    b.barrier()
    with b.scope():
        m_wkv = stage_wkv(b, "wk",
                          {"r_fm": D["r_fm"], "k_fm": D["k_fm"], "g_fm": D["g_fm"],
                           "k_tok": D["k_tok"], "v_tok": D["v_tok"],
                           "ew_tok": D["ew_tok"], "u": D["u"],
                           "gamma": D["gamma"], "beta": D["beta"]},
                          ctiles, {"yng_fm": D["yng"]}, T, {**m_mix, **cmarks})
    b.barrier()
    with b.scope():
        m_s0 = simple_proj(b, "wo0", D["W_o"], D["yng"], D["s0"], T, xmarks=m_wkv)
    b.barrier()
    with b.scope():
        m_q0 = simple_proj(b, "fq0", D["f0Wq"], D["xn0"], D["q0"], T,
                           scale=1.0 / 32.0, xmarks=m_ln0)
    b.barrier()
    with b.scope():
        m_k0 = simple_proj(b, "fk0", D["f0Wk"], D["xn0"], D["k0"], T, xmarks=m_ln0)
    b.barrier()
    with b.scope():
        xt0, xm0 = load_fm_raw(b, "fx0", D["xn0"], NDC, T, BF16, wait=m_ln0)
        wv0, wm0 = load_cast_fm(b, "fv0w", D["f0Wv"], NDC, C)
        wr_v0, ring_v0 = tok_store_writer(b, "fv0", D["v0_tok"])
        proj_tok(b, "fv0p", xt0, xm0, wv0, wm0, T, wr_v0)
        m_v0 = ring_v0.marks()
    b.barrier()
    with b.scope():
        m_fa = stage_l0fattn(b, "fa", D["q0"], D["k0"], D["v0_tok"], D["aout"], T,
                             {**m_q0, **m_k0, **m_v0})
    b.barrier()
    with b.scope():
        m_z1 = simple_proj(b, "fo0", D["f0Wo"], D["aout"], D["z1"], T, xmarks=m_fa)
    b.barrier()
    with b.scope():
        m_g0 = stage_ffntail(b, "ft", D["z1"], D["f0Wff"], D["g0"], T,
                             ones_t[0], ones_m, m_z1)
    b.barrier()
    with b.scope():
        m_x1 = stage_combine(b, "cb0", D["x_fm"], D["s0"], D["g0"], D["x1_fm"], T,
                             {**m_s0, **m_g0})
    b.barrier()
    with b.scope():
        m_ln1 = stage_ln(b, "ln1", D["x1_fm"], D["xn1"], T, ones_t[0],
                         {**ones_m, **m_x1})
    b.barrier()
    with b.scope():
        m_q1 = simple_proj(b, "aq1", D["a1Wq"], D["xn1"], D["q1"], T,
                           scale=1.0 / 8.0, xmarks=m_ln1)
    b.barrier()
    with b.scope():
        m_k1 = simple_proj(b, "ak1", D["a1Wk"], D["xn1"], D["k1"], T, xmarks=m_ln1)
    b.barrier()
    with b.scope():
        xt1, xm1 = load_fm_raw(b, "ax1", D["xn1"], NDC, T, BF16, wait=m_ln1)
        wv1, wm1 = load_cast_fm(b, "av1w", D["a1Wv"], NDC, C)
        wr_v1, ring_v1 = tok_store_writer(b, "av1", D["v1_tok"])
        proj_tok(b, "av1p", xt1, xm1, wv1, wm1, T, wr_v1)
        m_v1 = ring_v1.marks()
    b.barrier()
    with b.scope():
        m_at = stage_l1attn(b, "at", D["q1"], D["k1"], D["v1_tok"],
                            D["c_iota2"], D["c_slopes2"], D["c_qbn"],
                            D["c_minf"], D["att1"], T,
                            {**m_q1, **m_k1, **m_v1})
    b.barrier()
    with b.scope():
        m_s1 = simple_proj(b, "wo1", D["a1Wo"], D["att1"], D["s1"], T, xmarks=m_at)
    b.barrier()
    with b.scope():
        m_moe = stage_moe(b, "mo", D["xn1"], D["selw"], D["selb"], D["compw"],
                          D["compb"], D["g1_tok"], T, m_ln1)
    b.barrier()
    with b.scope():
        m_g1t = stage_transpose(b, "tg1", D["g1_tok"], D["g1_fm"], T, m_moe)
    b.barrier()
    with b.scope():
        m_out = stage_combine(b, "cb1", D["x1_fm"], D["s1"], D["g1_fm"], D["out_fm"],
                              T, {**m_s1, **m_g1t, **m_x1})

    def _fin(g):
        gate(g, m_out)
    b.gpd(_fin)
    b.close()
    return nc, hc


def kernel(**inputs):
    from concourse.bass_utils import run_bass_kernel_spmd
    T = T_FULL
    nc, hc = build_program(T)
    x = np.asarray(inputs["L0_attn_x"], dtype=np.float32)
    shared = {}
    shared["w1"] = np.ascontiguousarray(inputs["L0_attn_time_maa_w1"], dtype=np.float32)
    shared["w2f"] = np.ascontiguousarray(
        np.asarray(inputs["L0_attn_time_maa_w2"], dtype=np.float32).reshape(160, C))
    shared["td"] = np.ascontiguousarray(
        np.asarray(inputs["L0_attn_time_decay"], dtype=np.float32).reshape(1, C))
    shared["td_w1"] = np.ascontiguousarray(inputs["L0_attn_time_decay_w1"], dtype=np.float32)
    shared["td_w2"] = np.ascontiguousarray(inputs["L0_attn_time_decay_w2"], dtype=np.float32)
    shared["u"] = np.ascontiguousarray(
        np.asarray(inputs["L0_attn_time_faaaa"], dtype=np.float32).reshape(C, 1))
    for nm in ["x", "w", "k", "v", "r", "g"]:
        shared[f"maa_{nm}"] = np.ascontiguousarray(
            np.asarray(inputs[f"L0_attn_time_maa_{nm}"], dtype=np.float32).reshape(C, 1))
    for wn in ["W_r", "W_k", "W_v", "W_g", "W_o"]:
        shared[wn] = np.ascontiguousarray(inputs["L0_attn_" + wn], dtype=np.float32)
    shared["gamma"] = np.ascontiguousarray(
        np.asarray(inputs["L0_attn_ln_gamma"], dtype=np.float32).reshape(C, 1))
    shared["beta"] = np.ascontiguousarray(
        np.asarray(inputs["L0_attn_ln_beta"], dtype=np.float32).reshape(C, 1))
    for wn in ["Wq", "Wk", "Wv", "Wo", "Wff"]:
        shared["f0" + wn] = np.ascontiguousarray(inputs["L0_ffn_" + wn], dtype=np.float32)
    for wn in ["Wq", "Wk", "Wv", "Wo"]:
        shared["a1" + wn] = np.ascontiguousarray(
            np.asarray(inputs["L1_attn_" + wn], dtype=np.float32).T)
    shared["selw"] = np.ascontiguousarray(
        np.asarray(inputs["L1_ffn_sel_w"], dtype=np.float32).T)
    shared["selb"] = np.ascontiguousarray(
        np.asarray(inputs["L1_ffn_sel_b"], dtype=np.float32).reshape(1, 8192))
    shared["compw"] = np.ascontiguousarray(
        np.asarray(inputs["L1_ffn_comp_w"], dtype=np.float32).transpose(2, 1, 0).reshape(C, 8192))
    shared["compb"] = np.ascontiguousarray(
        np.asarray(inputs["L1_ffn_comp_b"], dtype=np.float32).T.reshape(1, 8192))
    for cn, arr in hc.items():
        shared[cn] = np.ascontiguousarray(arr)
    in_maps = []
    for core in range(N_CORES):
        m = dict(shared)
        m["x_fm"] = np.ascontiguousarray(x[core].T)
        in_maps.append(m)
    res = run_bass_kernel_spmd(nc, in_maps, core_ids=list(range(N_CORES)))
    out = np.stack([np.ascontiguousarray(np.asarray(r["out_fm"],
                                                    dtype=np.float32).reshape(C, T).T)
                    for r in res.results], axis=0)
    return out
